# version 06
# baseline (speedup 1.0000x reference)
import numpy as np
import ml_dtypes
import concourse.bass as bass
import concourse.mybir as mybir
from concourse.bass_utils import run_bass_kernel_spmd

F32 = mybir.dt.float32
BF16 = mybir.dt.bfloat16
AF = mybir.ActivationFunctionType
ALU = mybir.AluOpType
AX = mybir.AxisListType

D = 2048
N = 4096
H = 16
HD = 128
NJ = 16
TOWN = NJ * 128
NB = N // 64
WIN = 64
SCALE = float(HD) ** -0.5
NKT = 33
VW = 132
NE = 8
EI = 1024
SI = 4096
NEG = -1.0e30

ENGS = ("tensor", "vector", "scalar", "gpsimd", "sync")
DEBUG_STAGE = False


class PB:
    def __init__(self):
        self.ops = {e: [] for e in ENGS}
        self.cnt = {}
        self.pending = {e: [] for e in ENGS}
        self.eng_sem = {"vector": "sv", "scalar": "ss", "gpsimd": "sg",
                        "tensor": "sp", "sync": "sy"}

    def now(self, sem):
        return self.cnt.get(sem, 0)

    def mark(self, eng):
        return (self.eng_sem[eng], self.now(self.eng_sem[eng]))

    def on(self, eng, fn, wait=(), inc=None, n=1, selfserial=None, swait=False):
        if selfserial is None:
            selfserial = eng in ("vector", "scalar")
        waits = [w for w in wait if w is not None and w[1] > 0]
        if selfserial:
            s = self.eng_sem[eng]
            c = self.now(s)
            if c > 0:
                waits.append((s, c))
            if inc is None:
                inc = s
        if self.pending[eng]:
            waits = self.pending[eng] + waits
            self.pending[eng] = []
        wd = {}
        for s, v in waits:
            wd[s] = max(wd.get(s, 0), v)
        self.ops[eng].append((fn, tuple(wd.items()), inc, n))
        if inc is not None:
            self.cnt[inc] = self.cnt.get(inc, 0) + n
            return (inc, self.cnt[inc])
        return None

    def dma(self, eng, fn, wait=()):
        q = "qs" if eng == "sync" else "qg"
        return self.on(eng, fn, wait=wait, inc=q, n=16, selfserial=False)

    def force_inc_last(self, eng):
        if not self.ops[eng]:
            return None
        fn, w, inc, n = self.ops[eng][-1]
        if inc is None:
            s = self.eng_sem[eng]
            self.cnt[s] = self.cnt.get(s, 0) + 1
            self.ops[eng][-1] = (fn, w, s, 1)
        if self.ops[eng][-1][2] == self.eng_sem[eng]:
            return self.mark(eng)
        return None

    def barrier(self):
        marks = []
        for e in ("tensor", "vector", "scalar"):
            m = self.force_inc_last(e)
            if m and m[1] > 0:
                marks.append(m)
        for q in ("qs", "qg"):
            if self.now(q) > 0:
                marks.append((q, self.now(q)))
        for e in ENGS:
            self.pending[e].extend(list(marks))


def ceil_div(a, b):
    return (a + b - 1) // b


def build_nc():
    nc = bass.Bass("TRN2", target_bir_lowering=False)

    inp = {}

    def di(name, shape, dtype=F32):
        inp[name] = nc.dram_tensor(name, list(shape), dtype, kind="ExternalInput").ap()
        return inp[name]

    di("x_own", (TOWN, D))
    di("x_seq", (N, D))
    di("wq", (D, D)); di("wk", (D, D)); di("wv", (D, D))
    di("wg48", (D, 48)); di("wo", (D, D))
    di("wrT", (D, NE)); di("gwT", (D, 1))
    di("wgT", (NE, D, EI)); di("wuT", (NE, D, EI)); di("wdT", (NE, EI, D))
    di("sgT", (D, SI)); di("suT", (D, SI)); di("sdT", (SI, D))
    di("cs_own", (128, TOWN)); di("cs_seq", (128, N))
    di("cmp_bias", (NJ, 128, NB)); di("cmp_add2", (NJ, 128, NB))
    di("anym", (NJ, 128, 1))
    di("tail01", (128, 256)); di("swab", (2, 128, 384))

    y = nc.dram_tensor("y", [TOWN, D], F32, kind="ExternalOutput").ap()

    stg = "ExternalOutput" if DEBUG_STAGE else "Internal"
    kt_d = nc.dram_tensor("kt_d", [H, 128, NKT * 128], BF16, kind=stg).ap()
    v_d = nc.dram_tensor("v_d", [H, NKT * 128, HD], BF16, kind=stg).ap()
    ot_d = nc.dram_tensor("ot_d", [NJ, H, HD, 128], BF16, kind=stg).ap()
    xmid_d = nc.dram_tensor("xmid_d", [TOWN, D], F32, kind=stg).ap()
    ln2_d = nc.dram_tensor("ln2_d", [16, 128, TOWN], BF16, kind=stg).ap()

    ident_h = nc.inline_tensor(np.eye(128, dtype=ml_dtypes.bfloat16), "identc")
    mm_np = np.zeros((128, 32 * 64), dtype=np.float32)
    for th in range(32):
        for t in range(128):
            mm_np[t, 64 * th + 2 * th + t // 64] = 1.0 / 64.0
    mm_h = nc.inline_tensor(mm_np.astype(ml_dtypes.bfloat16), "meanmatc")
    oh_np = np.zeros((16, 16 * 128), dtype=np.float32)
    for e in range(16):
        oh_np[e, 128 * e:128 * (e + 1)] = 1.0
    oh_h = nc.inline_tensor(oh_np.astype(ml_dtypes.bfloat16), "onehotc")

    BASE = ((nc.sbuf_base + 63) // 64) * 64
    TOP = nc.sbuf_top

    T = {}

    def at(name, shape, dtype, off):
        assert BASE + off + shape[1] * mybir.dt.size(dtype) <= TOP, name
        h = nc.alloc_sbuf_tensor_at(name, list(shape), dtype, offset=BASE + off)
        T[name] = h
        return h

    at("QT", [128, 32768], BF16, 0)
    at("XME", [128, 16384], F32, 0)
    at("LT", [128, 16384], BF16, 65536)
    at("S0", [128, 4096], F32, 98304)
    at("HT", [128, 8192], BF16, 98304)
    at("WB0", [128, 4096], BF16, 114688)
    at("WB1", [128, 4096], BF16, 122880)
    at("OC0", [128, 2048], F32, 114688)
    at("OC1", [128, 2048], F32, 122880)
    at("CMPB", [128, 1024], BF16, 131072)
    at("CMPA", [128, 1024], BF16, 133120)
    at("SWAB", [128, 768], BF16, 135168)
    at("TAIL", [128, 256], BF16, 136704)
    at("MMC", [128, 2048], BF16, 137216)
    at("V0", [128, NKT * VW], BF16, 141312)
    at("V1", [128, NKT * VW], BF16, 150080)
    at("XT0", [128, 2048], F32, 141312)
    at("XT1", [128, 2048], F32, 149568)
    at("KT0", [128, 4224], BF16, 158912)
    at("KT1", [128, 4224], BF16, 167424)
    at("XC", [128, 2048], F32, 175872)
    at("RP1", [64, 512], F32, 175872)
    at("RP2", [64, 512], F32, 177920)
    at("ID", [128, 128], BF16, 184064)
    at("ZERO", [128, 128], BF16, 184320)
    at("SM", [128, 64], F32, 184576)
    at("G", [128, 768], F32, 184832)
    at("WG", [128, 768], BF16, 187904)
    at("WR", [128, 128], BF16, 189440)
    at("GW", [128, 16], BF16, 189696)
    at("LNB", [128, 2048], BF16, 189760)
    at("CS", [128, 1024], F32, 193856)
    at("KST", [128, 512], BF16, 197952)
    at("VST", [128, 2048], BF16, 198976)
    at("PRAW", [128, 1024], BF16, 189760)
    at("PMSK", [128, 1024], BF16, 191808)
    at("PT", [128, 1024], BF16, 193856)
    at("SSW", [128, 768], F32, 195904)
    at("PSW", [128, 384], BF16, 198976)
    at("PSWT", [128, 384], BF16, 199744)
    at("KCT", [128, 128], BF16, 200512)
    at("VCM", [64, 264], BF16, 200768)
    at("SCP", [128, 64], F32, 201344)
    at("PC", [128, 64], F32, 201600)
    at("PCB", [128, 256], BF16, 201856)
    at("PCT", [64, 256], BF16, 202368)
    at("IMP", [128, 64], F32, 202880)
    at("T8", [128, 8], F32, 203136)
    at("SEL", [128, 128], BF16, 203200)
    at("MCOL", [128, 16], F32, 203456)
    at("OTS", [128, 128], BF16, 203520)
    at("OCB", [128, 128], BF16, 203776)
    at("XCH", [128, 512], F32, 193856)
    at("OH", [16, 2048], BF16, 189760)
    at("E8", [128, 32], F32, 193856)
    at("CMB", [128, 128], F32, 193984)
    at("CMBB", [128, 16], BF16, 194496)
    at("CMBT", [16, 1024], BF16, 194560)
    at("GTS", [128, 512], BF16, 196608)
    at("HTA", [128, 512], BF16, 197632)
    at("REP", [128, 1024], F32, 198656)
    at("GRP", [128, 1024], F32, 202752)

    import contextlib
    ctx = contextlib.ExitStack()

    def psum(name, shape, dtype):
        return ctx.enter_context(nc.psum_tensor(name, list(shape), dtype))

    P0 = psum("P0", [128, 512], F32)
    P1 = psum("P1", [128, 512], F32)
    P2 = psum("P2", [128, 512], F32)
    P3 = psum("P3", [128, 512], F32)
    P4 = psum("P4", [128, 512], F32)
    PB0 = psum("PB0", [128, 1024], BF16)
    PB1 = psum("PB1", [128, 1024], BF16)
    P7 = psum("P7", [128, 512], F32)
    pbank = [PB0, PB1]
    pbanks_f = {"P0": P0, "P1": P1}

    pb = PB()
    free = {}
    for k in list(T.keys()) + ["P0", "P1", "P2", "P3", "P4", "P7", "PB0", "PB1",
                               "PCB0", "PCB1", "PCT0", "PCT1"]:
        free.setdefault(k, None)

    pb.dma("sync", lambda e: e.dma_start(T["ID"].ap(), ident_h.ap()))
    pb.dma("sync", lambda e: e.dma_start(T["MMC"].ap(), mm_h.ap()))
    pb.dma("gpsimd", lambda e: e.dma_start(
        T["CMPB"].ap().rearrange("p (j b) -> p j b", b=64),
        inp["cmp_bias"].rearrange("j p b -> p j b")))
    pb.dma("gpsimd", lambda e: e.dma_start(
        T["CMPA"].ap().rearrange("p (j b) -> p j b", b=64),
        inp["cmp_add2"].rearrange("j p b -> p j b")))
    pb.dma("gpsimd", lambda e: e.dma_start(T["TAIL"].ap(), inp["tail01"]))
    pb.dma("gpsimd", lambda e: e.dma_start(
        T["SWAB"].ap().rearrange("p (s c) -> p s c", c=384),
        inp["swab"].rearrange("s p c -> p s c")))
    pb.dma("gpsimd", lambda e: e.dma_start(
        T["WG"].ap().rearrange("p (c g) -> p c g", g=48),
        inp["wg48"].rearrange("(c p) g -> p c g", p=128)))
    pb.dma("gpsimd", lambda e: e.dma_start(
        T["WR"].ap().rearrange("p (c g) -> p c g", g=8),
        inp["wrT"].rearrange("(c p) g -> p c g", p=128)))
    pb.dma("gpsimd", lambda e: e.dma_start(
        T["GW"].ap().rearrange("p (c g) -> p c g", g=1),
        inp["gwT"].rearrange("(c p) g -> p c g", p=128),
        allow_slow_non_contiguous=True))
    vz = pb.on("vector", lambda e: e.memset(T["ZERO"].ap(), 0.0))
    for h in range(H):
        pb.dma("sync", lambda e, h=h: e.dma_start(kt_d[h, :, 0:128], T["ZERO"].ap()),
               wait=[vz])
        pb.dma("sync", lambda e, h=h: e.dma_start(v_d[h, 0:128, :], T["ZERO"].ap()),
               wait=[vz])
    pb.dma("sync", lambda e: e.dma_start(
        T["SM"].ap()[:, 40:56].rearrange("p (j o) -> p j o", o=1),
        inp["anym"].rearrange("j p o -> p j o"),
        allow_slow_non_contiguous=True))

    def emit_ln(src_name, sq_name, lt_off, wait_src):
        src = T[src_name].ap()
        pb.on("vector", lambda e, src=src: e.reduce_sum(
            T["SM"].ap()[:, 0:1], src, axis=AX.X),
            wait=list(wait_src) + [free["XC"]])
        pb.on("vector", lambda e: e.tensor_scalar_mul(
            T["SM"].ap()[:, 1:2], T["SM"].ap()[:, 0:1], 1.0 / D))
        pb.on("vector", lambda e, src=src: e.tensor_scalar_sub(
            T["XC"].ap(), src, T["SM"].ap()[:, 1:2]),
            wait=[free.get("XCsq")], swait=True)
        vxc = pb.mark("vector")
        ssq = pb.on("scalar", lambda e, sq_name=sq_name: e.activation(
            T[sq_name].ap()[:, 0:D], T["XC"].ap(), AF.Square,
            accum_out=T["SM"].ap()[:, 3:4]),
            wait=[vxc, free[sq_name]])
        free["XCsq"] = ssq
        pb.on("vector", lambda e: e.tensor_scalar(
            T["SM"].ap()[:, 4:5], T["SM"].ap()[:, 3:4], 1.0 / D, 1e-5,
            ALU.mult, ALU.add), wait=[ssq])
        vv = pb.mark("vector")
        pb.on("scalar", lambda e: e.activation(
            T["SM"].ap()[:, 5:6], T["SM"].ap()[:, 4:5], AF.Ln), wait=[vv])
        sr = pb.on("scalar", lambda e: e.activation(
            T["SM"].ap()[:, 6:7], T["SM"].ap()[:, 5:6], AF.Exp, scale=-0.5),
            swait=True)
        pb.on("vector", lambda e: e.tensor_scalar_mul(
            T["LNB"].ap(), T["XC"].ap(), T["SM"].ap()[:, 6:7]),
            wait=[sr, free["LNB"]])
        vln = pb.mark("vector")
        free["XC"] = vln
        tp = None
        for gi in range(2):
            bank = pbank[gi]
            bname = "PB%d" % gi
            for k in range(8):
                c = gi * 8 + k
                pb.on("tensor", lambda e, bank=bank, k=k, c=c: e.transpose(
                    bank[:, k * 128:(k + 1) * 128],
                    T["LNB"].ap()[:, c * 128:(c + 1) * 128], T["ID"].ap()),
                    wait=([vln, free[bname]] if k == 0 else ()),
                    selfserial=False, inc=("sp" if k == 7 else None))
            tp = pb.mark("tensor")
            pb.on("vector", lambda e, bank=bank, gi=gi, lt_off=lt_off: e.tensor_copy(
                T["LT"].ap().rearrange("p (c w) -> p c w", w=1024)[
                    :, gi * 8:(gi + 1) * 8, lt_off:lt_off + 128],
                bank[:].rearrange("p (k w) -> p k w", w=128)),
                wait=[tp])
            free[bname] = pb.mark("vector")
        free["LNB"] = tp
        return pb.mark("vector")

    def emit_rope_evac(bank, cs_off, dst_top, dst_bot, wait_first=()):
        pb.on("vector", lambda e, bank=bank, cs_off=cs_off: e.tensor_mul(
            T["RP1"].ap(), bank[0:64, :], T["CS"].ap()[0:64, cs_off:cs_off + 512]),
            wait=list(wait_first))
        pb.on("vector", lambda e, bank=bank, cs_off=cs_off: e.tensor_mul(
            T["RP2"].ap(), bank[64:128, :], T["CS"].ap()[64:128, cs_off:cs_off + 512]))
        pb.on("vector", lambda e, dst_top=dst_top: e.tensor_sub(
            dst_top, T["RP1"].ap(), T["RP2"].ap()))
        pb.on("vector", lambda e, bank=bank, cs_off=cs_off: e.tensor_mul(
            T["RP1"].ap(), bank[0:64, :], T["CS"].ap()[64:128, cs_off:cs_off + 512]))
        pb.on("vector", lambda e, bank=bank, cs_off=cs_off: e.tensor_mul(
            T["RP2"].ap(), bank[64:128, :], T["CS"].ap()[0:64, cs_off:cs_off + 512]))
        pb.on("vector", lambda e, dst_bot=dst_bot: e.tensor_add(
            dst_bot, T["RP1"].ap(), T["RP2"].ap()))
        return pb.mark("vector")

    for half in range(2):
        pb.pending["vector"].append(("sp", pb.now("sp")))
        vln_half = None
        for tt8 in range(8):
            tt = half * 8 + tt8
            xb = "XT0" if tt8 % 2 == 0 else "XT1"
            sq = "XT1" if tt8 % 2 == 0 else "XT0"
            dq = pb.dma("sync", lambda e, tt=tt, xb=xb: e.dma_start(
                T[xb].ap(), inp["x_own"][tt * 128:(tt + 1) * 128, :]),
                wait=[free[xb], free[sq]])
            vln_half = emit_ln(xb, sq, tt8 * 128, [dq])
            free[xb] = vln_half
            free[sq] = pb.mark("scalar")
        dcs = pb.dma("sync", lambda e, half=half: e.dma_start(
            T["CS"].ap(), inp["cs_own"][:, half * 1024:(half + 1) * 1024]),
            wait=[free["CS"]])
        for sl in range(8):
            wb = "WB0" if sl % 2 == 0 else "WB1"
            dw = pb.dma("gpsimd", lambda e, sl=sl, wb=wb: e.dma_start(
                T[wb].ap().rearrange("p (c m) -> p c m", m=256),
                inp["wq"][:, sl * 256:(sl + 1) * 256].rearrange(
                    "(c p) m -> p c m", p=128)),
                wait=[free[wb]])
            for hh in range(2):
                h = sl * 2 + hh
                for tb in range(2):
                    bn = "P0" if (hh * 2 + tb) % 2 == 0 else "P1"
                    bank = pbanks_f[bn]
                    for c in range(16):
                        pb.on("tensor", lambda e, bank=bank, c=c, hh=hh, tb=tb, wb=wb:
                              e.matmul(
                                  bank[:],
                                  T[wb].ap()[:, c * 256 + hh * 128: c * 256 + (hh + 1) * 128],
                                  T["LT"].ap()[:, c * 1024 + tb * 512: c * 1024 + (tb + 1) * 512],
                                  start=(c == 0), stop=(c == 15)),
                              wait=([dw, vln_half, free[bn], dcs] if c == 0 else ()),
                              selfserial=False, inc=("sp" if c == 15 else None))
                    spq = pb.mark("tensor")
                    qc = h * TOWN + half * 1024 + tb * 512
                    vr = emit_rope_evac(
                        bank, tb * 512,
                        T["QT"].ap()[0:64, qc:qc + 512],
                        T["QT"].ap()[64:128, qc:qc + 512],
                        wait_first=[spq])
                    free[bn] = vr
            free[wb] = pb.mark("tensor")
        free["CS"] = pb.mark("vector")
        for tt8 in range(8):
            tt = half * 8 + tt8
            for c in range(16):
                pb.on("tensor", lambda e, c=c, tt8=tt8: e.matmul(
                    P4[:, 0:48],
                    T["LT"].ap()[:, c * 1024 + tt8 * 128: c * 1024 + tt8 * 128 + 128],
                    T["WG"].ap()[:, c * 48:(c + 1) * 48],
                    start=(c == 0), stop=(c == 15)),
                    wait=([free["P4"], vln_half] if c == 0 else ()),
                    selfserial=False, inc=("sp" if c == 15 else None))
            spg = pb.mark("tensor")
            free["P4"] = pb.on("scalar", lambda e, tt=tt: e.activation(
                T["G"].ap()[:, tt * 48:(tt + 1) * 48], P4[:, 0:48], AF.Sigmoid),
                wait=[spg])

    for qd in range(4):
        pb.pending["vector"].append(("sp", pb.now("sp")))
        vln_q = None
        for tt8 in range(8):
            tt = qd * 8 + tt8
            xb = "XT0" if tt8 % 2 == 0 else "XT1"
            sq = "XT1" if tt8 % 2 == 0 else "XT0"
            dq = pb.dma("sync", lambda e, tt=tt, xb=xb: e.dma_start(
                T[xb].ap(), inp["x_seq"][tt * 128:(tt + 1) * 128, :]),
                wait=[free[xb], free[sq]])
            vln_q = emit_ln(xb, sq, tt8 * 128, [dq])
            free[xb] = vln_q
            free[sq] = pb.mark("scalar")
        dcs = pb.dma("sync", lambda e, qd=qd: e.dma_start(
            T["CS"].ap(), inp["cs_seq"][:, qd * 1024:(qd + 1) * 1024]),
            wait=[free["CS"]])
        for sl in range(8):
            wb = "WB0" if sl % 2 == 0 else "WB1"
            dw = pb.dma("gpsimd", lambda e, sl=sl, wb=wb: e.dma_start(
                T[wb].ap().rearrange("p (c m) -> p c m", m=256),
                inp["wk"][:, sl * 256:(sl + 1) * 256].rearrange(
                    "(c p) m -> p c m", p=128)),
                wait=[free[wb]])
            for hh in range(2):
                h = sl * 2 + hh
                for tb in range(2):
                    bn = "P0" if (hh * 2 + tb) % 2 == 0 else "P1"
                    bank = pbanks_f[bn]
                    for c in range(16):
                        pb.on("tensor", lambda e, bank=bank, c=c, hh=hh, tb=tb, wb=wb:
                              e.matmul(
                                  bank[:],
                                  T[wb].ap()[:, c * 256 + hh * 128: c * 256 + (hh + 1) * 128],
                                  T["LT"].ap()[:, c * 1024 + tb * 512: c * 1024 + (tb + 1) * 512],
                                  start=(c == 0), stop=(c == 15)),
                              wait=([dw, vln_q, free[bn], dcs] if c == 0 else ()),
                              selfserial=False, inc=("sp" if c == 15 else None))
                    spq = pb.mark("tensor")
                    vr = emit_rope_evac(
                        bank, tb * 512,
                        T["KST"].ap()[0:64, :], T["KST"].ap()[64:128, :],
                        wait_first=[spq, free["KST"]])
                    free[bn] = vr
                    col = 128 + qd * 1024 + tb * 512
                    free["KST"] = pb.dma("sync", lambda e, h=h, col=col: e.dma_start(
                        kt_d[h, :, col:col + 512], T["KST"].ap()), wait=[vr])
            free[wb] = pb.mark("tensor")
        free["CS"] = pb.mark("vector")
        for ch in range(8):
            wb = "WB0" if ch % 2 == 0 else "WB1"
            dw = pb.dma("gpsimd", lambda e, ch=ch, wb=wb: e.dma_start(
                T[wb].ap().rearrange("p (c m) -> p c m", m=256),
                inp["wv"][:, ch * 256:(ch + 1) * 256].rearrange(
                    "(c p) m -> p c m", p=128)),
                wait=[free[wb]])
            for tt8 in range(8):
                bn = "P0" if tt8 % 2 == 0 else "P1"
                bank = pbanks_f[bn]
                for c in range(16):
                    pb.on("tensor", lambda e, bank=bank, c=c, tt8=tt8, wb=wb: e.matmul(
                        bank[:, 0:256],
                        T["LT"].ap()[:, c * 1024 + tt8 * 128: c * 1024 + tt8 * 128 + 128],
                        T[wb].ap()[:, c * 256:(c + 1) * 256],
                        start=(c == 0), stop=(c == 15)),
                        wait=([dw, free[bn]] if c == 0 else ()),
                        selfserial=False, inc=("sp" if c == 15 else None))
                spv = pb.mark("tensor")
                vp = tt8 % 2
                sv_ = pb.on("scalar", lambda e, bank=bank, vp=vp: e.activation(
                    T["VST"].ap()[:, vp * 256:(vp + 1) * 256], bank[:, 0:256], AF.Copy),
                    wait=[spv, free["VST%d" % vp] if ("VST%d" % vp) in free else None])
                free[bn] = sv_
                row = 128 + qd * 1024 + tt8 * 128
                free["VST%d" % vp] = pb.dma(
                    "sync", lambda e, row=row, ch=ch, vp=vp: e.dma_start(
                        v_d[2 * ch:2 * ch + 2, row:row + 128, :].rearrange(
                            "h r c -> r h c"),
                        T["VST"].ap()[:, vp * 256:(vp + 1) * 256].rearrange(
                            "r (h c) -> r h c", c=128)),
                    wait=[sv_])
            free[wb] = pb.mark("tensor")
    pb.barrier()

    for vb in ("V0", "V1"):
        pb.on("vector", lambda e, vb=vb: e.memset(
            T[vb].ap().rearrange("p (t w) -> p t w", w=VW)[:, :, 128:132], 0.0))
        pb.on("vector", lambda e, vb=vb: e.memset(
            T[vb].ap().rearrange("p (t w) -> p t w", w=VW)[:, :, 128:129], 1.0))
    vones = pb.mark("vector")

    kv_load_q = [None] * (H + 2)
    cmp_ready = [None] * (H + 1)
    head_free = [None] * (H + 2)
    head_ocb_free = {}
    vcmp_pvdone = [None, None]

    def kbuf(hp):
        return T["KT0"].ap() if hp == 0 else T["KT1"].ap()

    def vbuf(hp):
        return T["V0"].ap() if hp == 0 else T["V1"].ap()

    def qtile(h, j):
        return T["QT"].ap()[:, h * TOWN + j * 128: h * TOWN + (j + 1) * 128]

    def emit_kv_load(h):
        hp = h % 2
        kb, vb = ("KT0", "V0") if hp == 0 else ("KT1", "V1")
        w = [head_free[h], vones]
        pb.dma("sync", lambda e, h=h, kb=kb: e.dma_start(T[kb].ap(), kt_d[h]), wait=w)
        kv_load_q[h] = pb.dma("sync", lambda e, h=h, vb=vb: e.dma_start(
            T[vb].ap().rearrange("p (t w) -> p t w", w=VW)[:, :, 0:128],
            v_d[h].rearrange("(t p) c -> p t c", p=128)), wait=w)

    def emit_cmp_prep(h):
        hp = h % 2
        kb = kbuf(hp)
        vb = vbuf(hp)
        src = kb[:, 128:128 + 4096]
        width = 4096
        spall = pb.mark("tensor")
        for r in range(6):
            dst = T["XC"].ap()
            pb.on("vector", lambda e, src=src, dst=dst, width=width: e.tensor_add(
                dst[:, 0:width // 2].rearrange("p (x o) -> p x o", o=1),
                src.rearrange("p (x two) -> p x two", two=2)[:, :, 0:1],
                src.rearrange("p (x two) -> p x two", two=2)[:, :, 1:2]),
                wait=([kv_load_q[h], free["XC"], spall] if r == 0 else ()))
            src = T["XC"].ap()[:, 0:width // 2]
            width //= 2
        pb.on("vector", lambda e, src=src, hp=hp: e.tensor_scalar_mul(
            T["KCT"].ap()[:, hp * 64:(hp + 1) * 64], src, 1.0 / 64.0))
        free["XC"] = pb.mark("vector")
        for th in range(32):
            pb.on("tensor", lambda e, th=th, vb=vb: e.matmul(
                P7[0:64, 0:VW], T["MMC"].ap()[:, th * 64:(th + 1) * 64],
                vb[:, (1 + th) * VW:(1 + th) * VW + VW],
                start=(th == 0), stop=(th == 31)),
                wait=([kv_load_q[h], free["P7"], vcmp_pvdone[hp]] if th == 0 else ()),
                selfserial=False, inc=("sp" if th == 31 else None))
        spv = pb.mark("tensor")
        pb.on("vector", lambda e, hp=hp: e.tensor_copy(
            T["VCM"].ap()[0:64, hp * VW:(hp + 1) * VW], P7[0:64, 0:VW]), wait=[spv])
        free["P7"] = pb.mark("vector")
        cmp_ready[h] = pb.mark("vector")

    class JS:
        pass

    jobs = []
    for h in range(H):
        for j in range(NJ):
            js = JS()
            js.h, js.j, js.f = h, j, h * NJ + j
            js.KE = 2 * j + 2
            js.nch = ceil_div(js.KE, 4)
            jobs.append(js)
    NJOBS = len(jobs)

    def emit_A(js):
        h, j, jp = js.h, js.j, js.f % 2
        hp = h % 2
        pb.on("tensor", lambda e, h=h, j=j, hp=hp: e.matmul(
            P3[:, 0:64], qtile(h, j), T["KCT"].ap()[:, hp * 64:(hp + 1) * 64],
            start=True, stop=True),
            wait=[free["P3"], cmp_ready[h]], selfserial=False, inc="sp")
        spc = pb.mark("tensor")
        pb.on("vector", lambda e, j=j: e.tensor_add(
            T["SCP"].ap(), P3[:, 0:64], T["CMPB"].ap()[:, j * 64:(j + 1) * 64]),
            wait=[spc])
        free["P3"] = pb.mark("vector")
        pb.on("vector", lambda e: e.reduce_max(
            T["SM"].ap()[:, 8:9], T["SCP"].ap(), axis=AX.X, negate=True))
        pb.on("vector", lambda e: e.tensor_scalar_min(
            T["SM"].ap()[:, 8:9], T["SM"].ap()[:, 8:9], 1.0e5))
        pb.on("vector", lambda e: e.tensor_scalar_mul(
            T["SM"].ap()[:, 9:10], T["SM"].ap()[:, 8:9], SCALE))
        vmc = pb.mark("vector")
        se = pb.on("scalar", lambda e: e.activation(
            T["PC"].ap(), T["SCP"].ap(), AF.Exp, bias=T["SM"].ap()[:, 9:10],
            scale=SCALE, accum_out=T["SM"].ap()[:, 10:11]), wait=[vmc])
        pb.on("vector", lambda e: e.tensor_scalar_add(
            T["SM"].ap()[:, 10:11], T["SM"].ap()[:, 10:11], 1.0e-30), wait=[se])
        pb.on("vector", lambda e: e.reciprocal(
            T["SM"].ap()[:, 11:12], T["SM"].ap()[:, 10:11]))
        pb.on("vector", lambda e, j=j: e.tensor_scalar(
            T["PC"].ap(), T["PC"].ap(), T["SM"].ap()[:, 11:12],
            T["SM"].ap()[:, 40 + j:41 + j], ALU.mult, ALU.mult), swait=True)
        pb.on("vector", lambda e, j=j: e.tensor_add(
            T["IMP"].ap(), T["PC"].ap(), T["CMPA"].ap()[:, j * 64:(j + 1) * 64]))
        pb.on("vector", lambda e: e.max(out=T["T8"].ap(), in_=T["IMP"].ap()))
        pb.on("vector", lambda e, jp=jp: e.tensor_scalar(
            T["SEL"].ap()[:, jp * 64:(jp + 1) * 64], T["IMP"].ap(),
            T["T8"].ap()[:, 7:8], None, ALU.is_ge), swait=True)
        js.sel_done = pb.mark("vector")
        js.pcb_done = pb.on("scalar", lambda e, jp=jp: e.activation(
            T["PCB"].ap()[:, jp * 64:(jp + 1) * 64], T["PC"].ap(), AF.Copy),
            wait=[js.sel_done, free["PCB%d" % jp]])
        for c in range(js.nch):
            cw = 128 * min(4, js.KE - 4 * c)
            bn = "P0" if c % 2 == 0 else "P1"
            bank = pbanks_f[bn]
            pb.on("tensor", lambda e, h=h, j=j, c=c, cw=cw, bank=bank, hp=hp: e.matmul(
                bank[:, 0:cw], qtile(h, j),
                kbuf(hp)[:, 128 + c * 512: 128 + c * 512 + cw],
                start=True, stop=True),
                wait=[free[bn], kv_load_q[h]], selfserial=False, inc="sp")
            spm = pb.mark("tensor")
            ssx = pb.on("scalar", lambda e, c=c, cw=cw, bank=bank: e.activation(
                T["S0"].ap()[:, c * 512: c * 512 + cw], bank[:, 0:cw],
                AF.Copy, scale=SCALE),
                wait=[spm])
            free[bn] = ssx
            pb.on("vector", lambda e, c=c, cw=cw: e.reduce_max(
                T["MCOL"].ap()[:, c:c + 1], T["S0"].ap()[:, c * 512: c * 512 + cw],
                axis=AX.X), wait=[ssx])
        pb.on("vector", lambda e, jp=jp, nch=js.nch: e.reduce_max(
            T["SM"].ap()[:, 12 + jp:13 + jp], T["MCOL"].ap()[:, 0:nch],
            axis=AX.X, negate=True))
        js.mneg_done = pb.mark("vector")

    def emit_B1(js):
        h, j, jp = js.h, js.j, js.f % 2
        hp = h % 2
        vb = vbuf(hp)
        first = True
        for c in range(js.nch):
            nt = min(4, js.KE - 4 * c)
            cw = 128 * nt
            pp = c % 2
            sse = pb.on("scalar", lambda e, c=c, cw=cw, pp=pp, jp=jp: e.activation(
                T["PRAW"].ap()[:, pp * 512: pp * 512 + cw],
                T["S0"].ap()[:, c * 512:c * 512 + cw],
                AF.Exp, bias=T["SM"].ap()[:, 12 + jp:13 + jp]),
                wait=[js.mneg_done, free["PRAW%d" % pp]])
            nb_c = cw // 64
            pb.on("vector", lambda e, cw=cw, pp=pp, jp=jp, nb_c=nb_c, c=c:
                  e.tensor_tensor(
                      T["PMSK"].ap()[:, pp * 512:pp * 512 + cw].rearrange(
                          "p (a b) -> p a b", b=64),
                      T["PRAW"].ap()[:, pp * 512:pp * 512 + cw].rearrange(
                          "p (a b) -> p a b", b=64),
                      T["SEL"].ap()[:, jp * 64 + 8 * c: jp * 64 + 8 * c + nb_c].rearrange(
                          "p (a o) -> p a o", o=1).to_broadcast([128, nb_c, 64]),
                      ALU.mult),
                  wait=[sse, js.sel_done, free["PMSK%d" % pp]])
            free["PRAW%d" % pp] = pb.mark("vector")
            if c == js.nch - 1:
                pb.on("vector", lambda e, pp=pp, cw=cw: e.tensor_mul(
                    T["PMSK"].ap()[:, pp * 512 + cw - 256: pp * 512 + cw],
                    T["PMSK"].ap()[:, pp * 512 + cw - 256: pp * 512 + cw],
                    T["TAIL"].ap()))
            vmask = pb.mark("vector")
            for t in range(nt):
                pb.on("tensor", lambda e, t=t, pp=pp: e.transpose(
                    pbank[pp][:, t * 128:(t + 1) * 128],
                    T["PMSK"].ap()[:, pp * 512 + t * 128: pp * 512 + (t + 1) * 128],
                    T["ID"].ap()),
                    wait=([vmask, free["PB%d" % pp]] if t == 0 else ()),
                    selfserial=False, inc=("sp" if t == nt - 1 else None))
            spt = pb.mark("tensor")
            pb.on("scalar", lambda e, pp=pp, cw=cw: e.activation(
                T["PT"].ap()[:, pp * 512: pp * 512 + cw], pbank[pp][:, 0:cw],
                AF.Copy),
                wait=[spt, free["PT%d" % pp]])
            free["PB%d" % pp] = pb.mark("scalar")
            free["PMSK%d" % pp] = pb.mark("scalar")
            vpt = pb.mark("scalar")
            for t in range(nt):
                kidx = 1 + 4 * c + t
                st = first and t == 0
                sp_ = (c == js.nch - 1 and t == nt - 1)
                pb.on("tensor", lambda e, t=t, pp=pp, kidx=kidx, vb=vb, st=st, sp_=sp_:
                      e.matmul(
                          P2[:, 0:129],
                          T["PT"].ap()[:, pp * 512 + t * 128: pp * 512 + (t + 1) * 128],
                          vb[:, kidx * VW: kidx * VW + 129],
                          start=st, stop=sp_),
                      wait=(([vpt] + ([free["P2"]] if st else [])) if t == 0 else ()),
                      selfserial=False, inc=("sp" if sp_ else None))
            first = False
            free["PT%d" % pp] = pb.mark("tensor")
        js.slc_pv = pb.mark("tensor")

    def emit_swaS(js):
        h, j, jp = js.h, js.j, js.f % 2
        hp = h % 2
        pb.on("tensor", lambda e, h=h, j=j, hp=hp: e.matmul(
            P3[:, 0:384], qtile(h, j), kbuf(hp)[:, (2 * j) * 128: (2 * j) * 128 + 384],
            start=True, stop=True),
            wait=[free["P3"], kv_load_q[h]], selfserial=False, inc="sp")
        spw = pb.mark("tensor")
        jsel = 0 if j == 0 else 1
        pb.on("vector", lambda e, jp=jp, jsel=jsel: e.tensor_add(
            T["SSW"].ap()[:, jp * 384:(jp + 1) * 384], P3[:, 0:384],
            T["SWAB"].ap()[:, jsel * 384:(jsel + 1) * 384]),
            wait=[spw, free["SSW%d" % jp] if ("SSW%d" % jp) in free else None])
        free["P3"] = pb.mark("vector")
        pb.on("vector", lambda e, jp=jp: e.reduce_max(
            T["SM"].ap()[:, 14 + jp:15 + jp], T["SSW"].ap()[:, jp * 384:(jp + 1) * 384],
            axis=AX.X, negate=True))
        pb.on("vector", lambda e, jp=jp: e.tensor_scalar_mul(
            T["SM"].ap()[:, 16 + jp:17 + jp], T["SM"].ap()[:, 14 + jp:15 + jp], SCALE))
        js.swa_m = pb.mark("vector")

    def emit_B2(js):
        h, j, jp = js.h, js.j, js.f % 2
        hp = h % 2
        vb = vbuf(hp)
        ocb = "OC0" if hp == 0 else "OC1"
        pb.on("tensor", lambda e, jp=jp: e.transpose(
            PB1[0:64, 0:128], T["PCB"].ap()[:, jp * 64:(jp + 1) * 64], T["ID"].ap()),
            wait=[js.pcb_done, free["PB1"]], selfserial=False, inc="sp")
        spct = pb.mark("tensor")
        free["PCB%d" % jp] = spct
        pb.on("scalar", lambda e, jp=jp: e.activation(
            T["PCT"].ap()[0:64, jp * 128:(jp + 1) * 128], PB1[0:64, 0:128], AF.Copy),
            wait=[spct, free["PCT%d" % jp]])
        free["PB1"] = pb.mark("scalar")
        vpct = pb.mark("scalar")
        pb.on("tensor", lambda e, jp=jp, hp=hp: e.matmul(
            P4[:, 0:VW], T["PCT"].ap()[0:64, jp * 128:(jp + 1) * 128],
            T["VCM"].ap()[0:64, hp * VW:(hp + 1) * VW],
            start=True, stop=True),
            wait=[vpct, free["P4"]], selfserial=False, inc="sp")
        spcv = pb.mark("tensor")
        vcmp_pvdone[hp] = spcv
        free["PCT%d" % jp] = spcv
        pb.on("vector", lambda e, h=h, j=j, ocb=ocb: e.tensor_scalar_mul(
            T[ocb].ap()[:, j * 128:(j + 1) * 128], P4[:, 0:128],
            T["G"].ap()[:, j * 48 + h: j * 48 + h + 1]),
            wait=[spcv, head_ocb_free.get(h)])
        free["P4"] = pb.mark("vector")
        ssew = pb.on("scalar", lambda e, jp=jp: e.activation(
            T["PSW"].ap(), T["SSW"].ap()[:, jp * 384:(jp + 1) * 384], AF.Exp,
            bias=T["SM"].ap()[:, 16 + jp:17 + jp], scale=SCALE),
            wait=[js.swa_m, free["PSW"]])
        free["SSW%d" % jp] = ssew
        for t in range(3):
            pb.on("tensor", lambda e, t=t: e.transpose(
                PB1[:, t * 128:(t + 1) * 128], T["PSW"].ap()[:, t * 128:(t + 1) * 128],
                T["ID"].ap()),
                wait=([ssew, free["PB1"]] if t == 0 else ()),
                selfserial=False, inc=("sp" if t == 2 else None))
        spwt = pb.mark("tensor")
        free["PSW"] = spwt
        pb.on("scalar", lambda e: e.activation(T["PSWT"].ap(), PB1[:, 0:384],
                                               AF.Copy),
              wait=[spwt, free["PSWT"]])
        free["PB1"] = pb.mark("scalar")
        vpswt = pb.mark("scalar")
        for t in range(3):
            kidx = 2 * j + t
            pb.on("tensor", lambda e, t=t, kidx=kidx, vb=vb: e.matmul(
                P4[:, 0:129], T["PSWT"].ap()[:, t * 128:(t + 1) * 128],
                vb[:, kidx * VW: kidx * VW + 129],
                start=(t == 0), stop=(t == 2)),
                wait=([vpswt, free["P4"]] if t == 0 else ()),
                selfserial=False, inc=("sp" if t == 2 else None))
        spwv = pb.mark("tensor")
        free["PSWT"] = spwv
        pb.on("vector", lambda e: e.reciprocal(T["SM"].ap()[:, 20:21], P4[:, 128:129]),
              wait=[spwv])
        pb.on("vector", lambda e, h=h, j=j: e.tensor_mul(
            T["SM"].ap()[:, 21:22], T["SM"].ap()[:, 20:21],
            T["G"].ap()[:, j * 48 + 32 + h: j * 48 + 32 + h + 1]))
        pb.on("vector", lambda e, h=h, j=j, ocb=ocb: e.scalar_tensor_tensor(
            T[ocb].ap()[:, j * 128:(j + 1) * 128], P4[:, 0:128], T["SM"].ap()[:, 21:22],
            T[ocb].ap()[:, j * 128:(j + 1) * 128], ALU.mult, ALU.add), swait=True)
        free["P4"] = pb.mark("vector")
        pb.on("vector", lambda e: e.reciprocal(T["SM"].ap()[:, 22:23], P2[:, 128:129]),
              wait=[js.slc_pv])
        pb.on("vector", lambda e, h=h, j=j: e.tensor_mul(
            T["SM"].ap()[:, 23:24], T["SM"].ap()[:, 22:23],
            T["G"].ap()[:, j * 48 + 16 + h: j * 48 + 16 + h + 1]))
        pb.on("vector", lambda e, h=h, j=j, ocb=ocb: e.scalar_tensor_tensor(
            T[ocb].ap()[:, j * 128:(j + 1) * 128], P2[:, 0:128], T["SM"].ap()[:, 23:24],
            T[ocb].ap()[:, j * 128:(j + 1) * 128], ALU.mult, ALU.add), swait=True)
        free["P2"] = pb.mark("vector")
        js.done = pb.mark("vector")

    def emit_head_tail(h, ocdone):
        ocb = "OC0" if h % 2 == 0 else "OC1"
        prev_sp = None
        for j in range(NJ):
            scb = pb.on("scalar", lambda e, j=j, ocb=ocb: e.activation(
                T["OCB"].ap(), T[ocb].ap()[:, j * 128:(j + 1) * 128], AF.Copy),
                wait=[ocdone, prev_sp])
            pb.on("tensor", lambda e: e.transpose(
                PB1[:, 0:128], T["OCB"].ap(), T["ID"].ap()),
                wait=[scb, free["PB1"]], selfserial=False, inc="sp")
            prev_sp = pb.mark("tensor")
            pb.on("vector", lambda e: e.tensor_copy(T["OTS"].ap(), PB1[:, 0:128]),
                  wait=[prev_sp, free["OTS"]])
            free["PB1"] = pb.mark("vector")
            vot = pb.mark("vector")
            free["OTS"] = pb.dma("sync", lambda e, j=j, h=h: e.dma_start(
                ot_d[j, h], T["OTS"].ap()), wait=[vot])
        head_free[h + 2] = pb.mark("tensor")
        head_ocb_free[h + 2] = pb.mark("scalar")

    for k in ["PRAW0", "PRAW1", "PMSK0", "PMSK1", "PT0", "PT1", "SSW0", "SSW1"]:
        free.setdefault(k, None)

    emit_kv_load(0)
    emit_cmp_prep(0)
    emit_kv_load(1)

    for f in range(NJOBS + 2):
        if f >= 2:
            emit_B2(jobs[f - 2])
            if jobs[f - 2].j == NJ - 1:
                hdone = jobs[f - 2].h
                emit_head_tail(hdone, jobs[f - 2].done)
                if hdone + 2 < H:
                    emit_kv_load(hdone + 2)
        if 1 <= f <= NJOBS:
            emit_B1(jobs[f - 1])
            emit_swaS(jobs[f - 1])
        if f < NJOBS:
            js = jobs[f]
            emit_A(js)
            if js.j == 12 and js.h + 1 < H:
                emit_cmp_prep(js.h + 1)
    pb.barrier()

    dot = pb.dma("sync", lambda e: e.dma_start(
        T["QT"].ap().rearrange("p (tt h t) -> p tt h t", h=16, t=128),
        ot_d.rearrange("tt h d t -> d tt h t")))
    for ch in range(8):
        wb = "WB0" if ch % 2 == 0 else "WB1"
        dw = pb.dma("gpsimd", lambda e, ch=ch, wb=wb: e.dma_start(
            T[wb].ap().rearrange("p (c m) -> p c m", m=256),
            inp["wo"][:, ch * 256:(ch + 1) * 256].rearrange(
                "(c p) m -> p c m", p=128)),
            wait=[free[wb]])
        for tt in range(NJ):
            xp = tt % 2
            dx = pb.dma("sync", lambda e, tt=tt, ch=ch, xp=xp: e.dma_start(
                T["XCH"].ap()[:, xp * 256:(xp + 1) * 256],
                inp["x_own"][tt * 128:(tt + 1) * 128, ch * 256:(ch + 1) * 256]),
                wait=[free["XCH%d" % xp] if ("XCH%d" % xp) in free else None])
            bn = "P0" if tt % 2 == 0 else "P1"
            bank = pbanks_f[bn]
            for hh in range(16):
                pb.on("tensor", lambda e, bank=bank, hh=hh, tt=tt, wb=wb: e.matmul(
                    bank[:, 0:256],
                    T["QT"].ap()[:, tt * 2048 + hh * 128: tt * 2048 + (hh + 1) * 128],
                    T[wb].ap()[:, hh * 256:(hh + 1) * 256],
                    start=(hh == 0), stop=(hh == 15)),
                    wait=([dw, dot, free[bn]] if hh == 0 else ()),
                    selfserial=False, inc=("sp" if hh == 15 else None))
            spa = pb.mark("tensor")
            pb.on("vector", lambda e, bank=bank, xp=xp: e.tensor_add(
                T["XCH"].ap()[:, xp * 256:(xp + 1) * 256],
                T["XCH"].ap()[:, xp * 256:(xp + 1) * 256], bank[:, 0:256]),
                wait=[spa, dx])
            free[bn] = pb.mark("vector")
            vadd = pb.mark("vector")
            free["XCH%d" % xp] = pb.dma("sync", lambda e, tt=tt, ch=ch, xp=xp:
                                        e.dma_start(
                xmid_d[tt * 128:(tt + 1) * 128, ch * 256:(ch + 1) * 256],
                T["XCH"].ap()[:, xp * 256:(xp + 1) * 256]), wait=[vadd])
        free[wb] = pb.mark("tensor")
    free["XT0"] = None
    free["XT1"] = None
    free["LNB"] = None
    free["XC"] = None
    qs_all = ("qs", pb.now("qs"))
    for tt in range(NJ):
        xb = "XT0" if tt % 2 == 0 else "XT1"
        sq = "S0"
        dq = pb.dma("sync", lambda e, tt=tt, xb=xb: e.dma_start(
            T[xb].ap(), xmid_d[tt * 128:(tt + 1) * 128, :]),
            wait=[free[xb], qs_all if tt == 0 else None])
        vln = emit_ln(xb, sq, (tt % 8) * 128, [dq])
        free[xb] = vln
        free["S0"] = pb.mark("scalar")
        if tt % 8 == 7:
            half = tt // 8
            for c in range(16):
                pb.dma("sync", lambda e, c=c, half=half: e.dma_start(
                    ln2_d[c, :, half * 1024:(half + 1) * 1024],
                    T["LT"].ap()[:, c * 1024:(c + 1) * 1024]), wait=[vln])
            pb.pending["vector"].append(("qs", pb.now("qs")))
    pb.barrier()

    doh = pb.dma("sync", lambda e: e.dma_start(T["OH"].ap(), oh_h.ap()))
    ent_down_sp = None
    for hf in range(2):
        dln = pb.dma("sync", lambda e, hf=hf: e.dma_start(
            T["LT"].ap().rearrange("p (c w) -> p c w", w=1024),
            ln2_d[:, :, hf * 1024:(hf + 1) * 1024].rearrange("c p t -> p c t")),
            wait=[("sp", pb.now("sp")) if hf == 1 else None])
        dxm = pb.dma("sync", lambda e, hf=hf: e.dma_start(
            T["XME"].ap().rearrange("p (t w) -> p t w", w=2048),
            xmid_d[hf * 1024:(hf + 1) * 1024, :].rearrange("(t p) w -> p t w", p=128)),
            wait=[("sv", pb.now("sv")) if hf == 1 else None])

        def xm_slice(tt, ch):
            return T["XME"].ap()[:, tt * 2048 + ch * 512: tt * 2048 + (ch + 1) * 512]

        for tt in range(8):
            for c in range(16):
                pb.on("tensor", lambda e, tt=tt, c=c: e.matmul(
                    P4[:, 0:8],
                    T["LT"].ap()[:, c * 1024 + tt * 128: c * 1024 + tt * 128 + 128],
                    T["WR"].ap()[:, c * 8:(c + 1) * 8], start=(c == 0), stop=(c == 15)),
                    wait=([dln, free["P4"]] if c == 0 else ()),
                    selfserial=False, inc=("sp" if c == 15 else None))
            spr = pb.mark("tensor")
            pb.on("vector", lambda e: e.reduce_max(
                T["SM"].ap()[:, 30:31], P4[:, 0:8], axis=AX.X, negate=True),
                wait=[spr])
            vmr = pb.mark("vector")
            sse = pb.on("scalar", lambda e: e.activation(
                T["E8"].ap()[:, 0:8], P4[:, 0:8], AF.Exp, bias=T["SM"].ap()[:, 30:31]),
                wait=[vmr])
            free["P4"] = sse
            pb.on("vector", lambda e: e.max(out=T["T8"].ap(), in_=T["E8"].ap()[:, 0:8]),
                  wait=[sse])
            pb.on("vector", lambda e: e.tensor_scalar(
                T["E8"].ap()[:, 8:16], T["E8"].ap()[:, 0:8], T["T8"].ap()[:, 1:2],
                None, ALU.is_ge), swait=True)
            pb.on("vector", lambda e: e.tensor_mul(
                T["E8"].ap()[:, 16:24], T["E8"].ap()[:, 0:8], T["E8"].ap()[:, 8:16]))
            pb.on("vector", lambda e: e.reduce_sum(
                T["SM"].ap()[:, 32:33], T["E8"].ap()[:, 16:24], axis=AX.X))
            pb.on("vector", lambda e: e.reciprocal(
                T["SM"].ap()[:, 33:34], T["SM"].ap()[:, 32:33]))
            pb.on("vector", lambda e, tt=tt: e.tensor_scalar_mul(
                T["CMB"].ap()[:, tt * 16: tt * 16 + 8], T["E8"].ap()[:, 16:24],
                T["SM"].ap()[:, 33:34]), swait=True)
            pb.on("vector", lambda e, tt=tt: e.memset(
                T["CMB"].ap()[:, tt * 16 + 9:(tt + 1) * 16], 0.0))
        for tt in range(8):
            for c in range(16):
                pb.on("tensor", lambda e, tt=tt, c=c: e.matmul(
                    P4[:, 0:1],
                    T["LT"].ap()[:, c * 1024 + tt * 128: c * 1024 + tt * 128 + 128],
                    T["GW"].ap()[:, c:c + 1], start=(c == 0), stop=(c == 15)),
                    wait=([free["P4"]] if c == 0 else ()),
                    selfserial=False, inc=("sp" if c == 15 else None))
            spg = pb.mark("tensor")
            free["P4"] = pb.on("scalar", lambda e, tt=tt: e.activation(
                T["CMB"].ap()[:, tt * 16 + 8: tt * 16 + 9], P4[:, 0:1], AF.Sigmoid),
                wait=[spg])
        for tt in range(8):
            vcm = pb.mark("vector")
            scb = pb.on("scalar", lambda e, tt=tt: e.activation(
                T["CMBB"].ap(), T["CMB"].ap()[:, tt * 16:(tt + 1) * 16], AF.Copy),
                wait=[vcm])
            pb.on("tensor", lambda e: e.transpose(
                PB0[0:16, 0:128], T["CMBB"].ap(), T["ID"].ap()),
                wait=[scb, free["PB0"], doh], selfserial=False, inc="sp")
            spt = pb.mark("tensor")
            pb.on("vector", lambda e, tt=tt: e.tensor_copy(
                T["CMBT"].ap()[0:16, tt * 128:(tt + 1) * 128], PB0[0:16, 0:128]),
                wait=[spt])
            free["PB0"] = pb.mark("vector")
        vcmbt = pb.mark("vector")
        for q2 in range(2):
            pb.on("tensor", lambda e, q2=q2: e.matmul(
                P4[:], T["OH"].ap()[:, 8 * 128:9 * 128],
                T["CMBT"].ap()[0:16, q2 * 512:(q2 + 1) * 512],
                start=True, stop=True),
                wait=[vcmbt, free["P4"], doh], selfserial=False, inc="sp")
            spr = pb.mark("tensor")
            free["P4"] = pb.on("scalar", lambda e, q2=q2: e.activation(
                T["GRP"].ap()[:, q2 * 512:(q2 + 1) * 512], P4[:], AF.Copy),
                wait=[spr])
        grp_done = pb.mark("scalar")

        for ent in range(12):
            is_shared = ent < 4
            eidx = ent - 4
            if is_shared:
                rep_t, rep_done = "GRP", grp_done
            else:
                for q2 in range(2):
                    pb.on("tensor", lambda e, eidx=eidx, q2=q2: e.matmul(
                        P4[:], T["OH"].ap()[:, eidx * 128:(eidx + 1) * 128],
                        T["CMBT"].ap()[0:16, q2 * 512:(q2 + 1) * 512],
                        start=True, stop=True),
                        wait=[vcmbt, free["REP"], free["P4"]],
                        selfserial=False, inc="sp")
                    spr = pb.mark("tensor")
                    free["P4"] = pb.on("scalar", lambda e, q2=q2: e.activation(
                        T["REP"].ap()[:, q2 * 512:(q2 + 1) * 512], P4[:], AF.Copy),
                        wait=[spr])
                rep_t, rep_done = "REP", pb.mark("scalar")
            for sl in range(4):
                wbg = "WB0" if sl % 2 == 0 else "WB1"
                wbu = "KT0" if sl % 2 == 0 else "KT1"
                if is_shared:
                    gsrc = inp["sgT"][:, ent * 1024 + sl * 256: ent * 1024 + (sl + 1) * 256]
                    usrc = inp["suT"][:, ent * 1024 + sl * 256: ent * 1024 + (sl + 1) * 256]
                else:
                    gsrc = inp["wgT"][eidx, :, sl * 256:(sl + 1) * 256]
                    usrc = inp["wuT"][eidx, :, sl * 256:(sl + 1) * 256]
                dg = pb.dma("gpsimd", lambda e, gsrc=gsrc, wbg=wbg: e.dma_start(
                    T[wbg].ap().rearrange("p (c m) -> p c m", m=256),
                    gsrc.rearrange("(c p) m -> p c m", p=128)),
                    wait=[free[wbg]])
                du = pb.dma("gpsimd", lambda e, usrc=usrc, wbu=wbu: e.dma_start(
                    T[wbu].ap()[:, 0:4096].rearrange("p (c m) -> p c m", m=256),
                    usrc.rearrange("(c p) m -> p c m", p=128)),
                    wait=[free[wbu]])
                for it2 in range(2):
                    it = sl * 2 + it2
                    for tch in range(2):
                        for c in range(16):
                            pb.on("tensor", lambda e, c=c, it2=it2, tch=tch, wbg=wbg:
                                  e.matmul(
                                      P0[:],
                                      T[wbg].ap()[:, c * 256 + it2 * 128: c * 256 + (it2 + 1) * 128],
                                      T["LT"].ap()[:, c * 1024 + tch * 512: c * 1024 + (tch + 1) * 512],
                                      start=(c == 0), stop=(c == 15)),
                                  wait=([dg, free["P0"], dln] if c == 0 else ()),
                                  selfserial=False, inc=("sp" if c == 15 else None))
                        spg_ = pb.mark("tensor")
                        sgs = pb.on("scalar", lambda e: e.activation(
                            T["GTS"].ap(), P0[:], AF.Silu), wait=[spg_, free["GTS"]])
                        free["P0"] = sgs
                        for c in range(16):
                            pb.on("tensor", lambda e, c=c, it2=it2, tch=tch, wbu=wbu:
                                  e.matmul(
                                      P1[:],
                                      T[wbu].ap()[:, c * 256 + it2 * 128: c * 256 + (it2 + 1) * 128],
                                      T["LT"].ap()[:, c * 1024 + tch * 512: c * 1024 + (tch + 1) * 512],
                                      start=(c == 0), stop=(c == 15)),
                                  wait=([du, free["P1"]] if c == 0 else ()),
                                  selfserial=False, inc=("sp" if c == 15 else None))
                        spu = pb.mark("tensor")
                        pb.on("vector", lambda e: e.tensor_mul(
                            T["HTA"].ap(), P1[:], T["GTS"].ap()),
                            wait=[spu, sgs, free["HTA"]])
                        free["P1"] = pb.mark("vector")
                        free["GTS"] = pb.mark("vector")
                        pb.on("vector", lambda e, it=it, tch=tch, rep_t=rep_t:
                              e.tensor_mul(
                                  T["HT"].ap()[:, it * 1024 + tch * 512: it * 1024 + (tch + 1) * 512],
                                  T["HTA"].ap(), T[rep_t].ap()[:, tch * 512:(tch + 1) * 512]),
                              wait=[rep_done, ent_down_sp])
                        free["HTA"] = pb.mark("vector")
                free[wbg] = pb.mark("tensor")
                free[wbu] = pb.mark("tensor")
            ht_done = pb.mark("vector")
            free["REP"] = ht_done
            for ch in range(4):
                wbd = "WB0" if ch % 2 == 0 else "WB1"
                if is_shared:
                    dsrc = inp["sdT"][ent * 1024:(ent + 1) * 1024, ch * 512:(ch + 1) * 512]
                else:
                    dsrc = inp["wdT"][eidx, :, ch * 512:(ch + 1) * 512]
                dd = pb.dma("gpsimd", lambda e, dsrc=dsrc, wbd=wbd: e.dma_start(
                    T[wbd].ap().rearrange("p (c m) -> p c m", m=512),
                    dsrc.rearrange("(c p) m -> p c m", p=128)),
                    wait=[free[wbd]])
                for tt in range(8):
                    bn = "P2" if (tt + ch) % 2 == 0 else "P3"
                    bank = P2 if (tt + ch) % 2 == 0 else P3
                    for it in range(8):
                        pb.on("tensor", lambda e, bank=bank, it=it, tt=tt, wbd=wbd:
                              e.matmul(
                                  bank[:],
                                  T["HT"].ap()[:, it * 1024 + tt * 128: it * 1024 + tt * 128 + 128],
                                  T[wbd].ap()[:, it * 512:(it + 1) * 512],
                                  start=(it == 0), stop=(it == 7)),
                              wait=([dd, ht_done, free[bn], dxm] if it == 0 else ()),
                              selfserial=False, inc=("sp" if it == 7 else None))
                    spd = pb.mark("tensor")
                    pb.on("vector", lambda e, tt=tt, ch=ch, bank=bank: e.tensor_add(
                        xm_slice(tt, ch), xm_slice(tt, ch), bank[:]),
                        wait=[spd])
                    free[bn] = pb.mark("vector")
                free[wbd] = pb.mark("tensor")
            ent_down_sp = pb.mark("tensor")
        vfin = pb.mark("vector")
        for tt in range(8):
            pb.dma("sync", lambda e, tt=tt, hf=hf: e.dma_start(
                y[hf * 1024 + tt * 128: hf * 1024 + (tt + 1) * 128, :],
                T["XME"].ap()[:, tt * 2048:(tt + 1) * 2048]), wait=[vfin])

    pb.on("sync", lambda e: None,
          wait=[("qs", pb.now("qs")), ("qg", pb.now("qg"))], selfserial=False)

    sems = {}
    for name in set(list(pb.cnt.keys()) + ["sv", "ss", "sp", "sg", "sy", "qs", "qg"]):
        sems[name] = ctx.enter_context(nc.semaphore(name))

    with nc.Block() as blk:
        def make_runner(ename):
            def runner(e):
                for fn, waits, inc, n in pb.ops[ename]:
                    for s, v in waits:
                        e.wait_ge(sems[s], v)
                    ins = fn(e)
                    if inc is not None:
                        if ins is None:
                            raise RuntimeError(f"{ename}: inc with no instruction")
                        ins.then_inc(sems[inc], n)
            return runner

        blk.tensor(make_runner("tensor"))
        blk.vector(make_runner("vector"))
        blk.scalar(make_runner("scalar"))
        blk.gpsimd(make_runner("gpsimd"))
        blk.sync(make_runner("sync"))

    ctx.close()
    return nc



def _host_masks(p):
    cmp_bias = np.zeros((NJ, 128, NB), np.float32)
    cmp_add2 = np.zeros((NJ, 128, NB), np.float32)
    anym = np.zeros((NJ, 128, 1), np.float32)
    for j in range(NJ):
        a = 2 * j + p
        i = 128 * a + np.arange(128)
        b = np.arange(NB)
        mask = i[:, None] >= (64 * (b[None, :] + 1) - 1)
        cmp_bias[j] = np.where(mask, 0.0, NEG)
        anym[j, :, 0] = mask.any(axis=1).astype(np.float32)
        cur = i // 64
        cmp_add2[j, np.arange(128), cur] = 2.0
    tail01 = np.zeros((128, 256), np.float32)
    r = np.arange(128)
    c = np.arange(128)
    tri = (c[None, :] <= r[:, None]).astype(np.float32)
    if p == 0:
        tail01[:, 0:128] = tri
    else:
        tail01[:, 0:128] = 1.0
        tail01[:, 128:256] = tri
    swab = np.full((2, 128, 384), NEG, np.float32)
    cc = np.arange(384)
    delta = 128 * (p + 1) + r[:, None] - cc[None, :]
    band = (delta >= 0) & (delta <= WIN)
    swab[1] = np.where(band, 0.0, NEG)
    band0 = band & (cc[None, :] >= 128)
    swab[0] = np.where(band0, 0.0, NEG)
    if p == 1:
        swab[0] = swab[1]
    return cmp_bias, cmp_add2, anym, tail01, swab


def _rope_tables(pos):
    half = HD // 2
    inv = 1.0 / (10000.0 ** (np.arange(half, dtype=np.float64) / half))
    th = pos[:, None].astype(np.float64) * inv[None, :]
    cos = np.cos(th).astype(np.float32).T
    sin = np.sin(th).astype(np.float32).T
    return np.ascontiguousarray(np.concatenate([cos, sin], axis=0))


_NC = None


def kernel(L0_attn_x, L0_attn_Wq, L0_attn_Wk, L0_attn_Wv, L0_attn_Wg, L0_attn_Wo,
           L0_ffn_w_router, L0_ffn_wg, L0_ffn_wu, L0_ffn_wd,
           L0_ffn_sw_gate, L0_ffn_sw_up, L0_ffn_sw_down, L0_ffn_gate_w):
    global _NC
    x = np.ascontiguousarray(np.asarray(L0_attn_x, dtype=np.float32))
    wq = np.ascontiguousarray(np.asarray(L0_attn_Wq, np.float32))
    wk = np.ascontiguousarray(np.asarray(L0_attn_Wk, np.float32))
    wv = np.ascontiguousarray(np.asarray(L0_attn_Wv, np.float32))
    wg48 = np.ascontiguousarray(np.asarray(L0_attn_Wg, np.float32))
    wo = np.ascontiguousarray(np.asarray(L0_attn_Wo, np.float32))
    wrT = np.ascontiguousarray(np.asarray(L0_ffn_w_router, np.float32).T)
    wgT = np.ascontiguousarray(np.transpose(np.asarray(L0_ffn_wg, np.float32), (0, 2, 1)))
    wuT = np.ascontiguousarray(np.transpose(np.asarray(L0_ffn_wu, np.float32), (0, 2, 1)))
    wdT = np.ascontiguousarray(np.transpose(np.asarray(L0_ffn_wd, np.float32), (0, 2, 1)))
    sgT = np.ascontiguousarray(np.asarray(L0_ffn_sw_gate, np.float32).T)
    suT = np.ascontiguousarray(np.asarray(L0_ffn_sw_up, np.float32).T)
    sdT = np.ascontiguousarray(np.asarray(L0_ffn_sw_down, np.float32).T)
    gwT = np.ascontiguousarray(np.asarray(L0_ffn_gate_w, np.float32).T)

    cs_seq = _rope_tables(np.arange(N))
    in_maps = []
    for cix in range(8):
        s, p = cix // 2, cix % 2
        xs = np.ascontiguousarray(x[s])
        xo = np.ascontiguousarray(xs.reshape(32, 128, D)[p::2].reshape(TOWN, D))
        pos_own = (np.arange(TOWN) // 128 * 2 + p) * 128 + np.arange(TOWN) % 128
        cmp_bias, cmp_add2, anym, tail01, swab = _host_masks(p)
        in_maps.append({
            "x_own": xo, "x_seq": xs,
            "wq": wq, "wk": wk, "wv": wv, "wg48": wg48, "wo": wo,
            "wrT": wrT, "gwT": gwT, "wgT": wgT, "wuT": wuT, "wdT": wdT,
            "sgT": sgT, "suT": suT, "sdT": sdT,
            "cs_own": _rope_tables(pos_own), "cs_seq": cs_seq,
            "cmp_bias": cmp_bias, "cmp_add2": cmp_add2, "anym": anym,
            "tail01": tail01, "swab": swab,
        })

    if _NC is None:
        _NC = build_nc()
    res = run_bass_kernel_spmd(_NC, in_maps, core_ids=list(range(8)))
    out = np.empty((4, N, D), np.float32)
    for cix in range(8):
        s, p = cix // 2, cix % 2
        out[s].reshape(32, 128, D)[p::2] = res.results[cix]["y"].reshape(NJ, 128, D)
    return out


# version 10
# speedup vs baseline: 1.1628x; 1.1628x over previous
import numpy as np
import ml_dtypes
import concourse.bass as bass
import concourse.mybir as mybir
from concourse.bass_utils import run_bass_kernel_spmd

F32 = mybir.dt.float32
BF16 = mybir.dt.bfloat16
AF = mybir.ActivationFunctionType
ALU = mybir.AluOpType
AX = mybir.AxisListType

D = 2048
N = 4096
H = 16
HD = 128
NJ = 16
TOWN = NJ * 128
NB = N // 64
WIN = 64
SCALE = float(HD) ** -0.5
NKT = 33
VW = 132
NE = 8
EI = 1024
SI = 4096
NEG = -1.0e30

ENGS = ("tensor", "vector", "scalar", "gpsimd", "sync")
DEBUG_STAGE = False


class PB:
    def __init__(self):
        self.ops = {e: [] for e in ENGS}
        self.cnt = {}
        self.pending = {e: [] for e in ENGS}
        self.eng_sem = {"vector": "sv", "scalar": "ss", "gpsimd": "sg",
                        "tensor": "sp", "sync": "sy"}

    def now(self, sem):
        return self.cnt.get(sem, 0)

    def mark(self, eng):
        return (self.eng_sem[eng], self.now(self.eng_sem[eng]))

    def on(self, eng, fn, wait=(), inc=None, n=1, selfserial=None, swait=False):
        if selfserial is None:
            selfserial = eng in ("vector", "scalar")
        waits = [w for w in wait if w is not None and w[1] > 0]
        if selfserial:
            s = self.eng_sem[eng]
            c = self.now(s)
            if c > 0:
                waits.append((s, c))
            if inc is None:
                inc = s
        if self.pending[eng]:
            waits = self.pending[eng] + waits
            self.pending[eng] = []
        wd = {}
        for s, v in waits:
            wd[s] = max(wd.get(s, 0), v)
        self.ops[eng].append((fn, tuple(wd.items()), inc, n))
        if inc is not None:
            self.cnt[inc] = self.cnt.get(inc, 0) + n
            return (inc, self.cnt[inc])
        return None

    def dma(self, eng, fn, wait=()):
        q = "qs" if eng == "sync" else "qg"
        return self.on(eng, fn, wait=wait, inc=q, n=16, selfserial=False)

    def force_inc_last(self, eng):
        if not self.ops[eng]:
            return None
        fn, w, inc, n = self.ops[eng][-1]
        if inc is None:
            s = self.eng_sem[eng]
            self.cnt[s] = self.cnt.get(s, 0) + 1
            self.ops[eng][-1] = (fn, w, s, 1)
        if self.ops[eng][-1][2] == self.eng_sem[eng]:
            return self.mark(eng)
        return None

    def barrier(self):
        marks = []
        for e in ("tensor", "vector", "scalar"):
            m = self.force_inc_last(e)
            if m and m[1] > 0:
                marks.append(m)
        for q in ("qs", "qg"):
            if self.now(q) > 0:
                marks.append((q, self.now(q)))
        for e in ENGS:
            self.pending[e].extend(list(marks))


def ceil_div(a, b):
    return (a + b - 1) // b


def build_nc():
    nc = bass.Bass("TRN2", target_bir_lowering=False)

    inp = {}

    def di(name, shape, dtype=F32):
        inp[name] = nc.dram_tensor(name, list(shape), dtype, kind="ExternalInput").ap()
        return inp[name]

    di("x_own", (TOWN, D))
    di("x_seq", (N, D))
    di("wq", (D, D)); di("wk", (D, D)); di("wv", (D, D))
    di("wg48", (D, 48)); di("wo", (D, D))
    di("wrT", (D, NE)); di("gwT", (D, 1))
    di("wgT", (NE, D, EI)); di("wuT", (NE, D, EI)); di("wdT", (NE, EI, D))
    di("sgT", (D, SI)); di("suT", (D, SI)); di("sdT", (SI, D))
    di("cs_own", (128, TOWN)); di("cs_seq", (128, N))
    di("cmp_bias", (NJ, 128, NB)); di("cmp_add2", (NJ, 128, NB))
    di("anym", (NJ, 128, 1))
    di("tail01", (128, 256)); di("swab", (2, 128, 384))

    y = nc.dram_tensor("y", [TOWN, D], F32, kind="ExternalOutput").ap()

    stg = "ExternalOutput" if DEBUG_STAGE else "Internal"
    kt_d = nc.dram_tensor("kt_d", [H, 128, NKT * 128], BF16, kind=stg).ap()
    v_d = nc.dram_tensor("v_d", [H, NKT * 128, HD], BF16, kind=stg).ap()
    ot_d = nc.dram_tensor("ot_d", [NJ, H, HD, 128], BF16, kind=stg).ap()
    xmid_d = nc.dram_tensor("xmid_d", [TOWN, D], F32, kind=stg).ap()
    ln2_d = nc.dram_tensor("ln2_d", [16, 128, TOWN], BF16, kind=stg).ap()

    ident_h = nc.inline_tensor(np.eye(128, dtype=ml_dtypes.bfloat16), "identc")
    mm_np = np.zeros((128, 32 * 64), dtype=np.float32)
    for th in range(32):
        for t in range(128):
            mm_np[t, 64 * th + 2 * th + t // 64] = 1.0 / 64.0
    mm_h = nc.inline_tensor(mm_np.astype(ml_dtypes.bfloat16), "meanmatc")
    oh_np = np.zeros((16, 16 * 128), dtype=np.float32)
    for e in range(16):
        oh_np[e, 128 * e:128 * (e + 1)] = 1.0
    oh_h = nc.inline_tensor(oh_np.astype(ml_dtypes.bfloat16), "onehotc")

    BASE = ((nc.sbuf_base + 63) // 64) * 64
    TOP = nc.sbuf_top

    T = {}

    def at(name, shape, dtype, off):
        assert BASE + off + shape[1] * mybir.dt.size(dtype) <= TOP, name
        h = nc.alloc_sbuf_tensor_at(name, list(shape), dtype, offset=BASE + off)
        T[name] = h
        return h

    at("QT", [128, 32768], BF16, 0)
    at("XME", [128, 16384], F32, 0)
    at("LT", [128, 16384], BF16, 65536)
    at("S0", [128, 4096], F32, 98304)
    at("HT", [128, 8192], BF16, 98304)
    at("WB0", [128, 4096], BF16, 114688)
    at("WB1", [128, 4096], BF16, 122880)
    at("OC0", [128, 2048], F32, 114688)
    at("OC1", [128, 2048], F32, 122880)
    at("CMPB", [128, 1024], BF16, 131072)
    at("CMPA", [128, 1024], BF16, 133120)
    at("SWAB", [128, 768], BF16, 135168)
    at("TAIL", [128, 256], BF16, 136704)
    at("MMC", [128, 2048], BF16, 137216)
    at("V0", [128, NKT * VW], BF16, 141312)
    at("V1", [128, NKT * VW], BF16, 150080)
    at("XT0", [128, 2048], F32, 141312)
    at("XT1", [128, 2048], F32, 149568)
    at("KT0", [128, 4224], BF16, 158912)
    at("KT1", [128, 4224], BF16, 167424)
    at("XC", [128, 2048], F32, 175872)
    at("RP1", [64, 512], F32, 175872)
    at("RP2", [64, 512], F32, 177920)
    at("ID", [128, 128], BF16, 184064)
    at("ZERO", [128, 128], BF16, 184320)
    at("SM", [128, 64], F32, 184576)
    at("G", [128, 768], F32, 184832)
    at("WG", [128, 768], BF16, 187904)
    at("WR", [128, 128], BF16, 189440)
    at("GW", [128, 16], BF16, 189696)
    at("LNB", [128, 2048], BF16, 189760)
    at("CS", [128, 1024], F32, 193856)
    at("KST", [128, 512], BF16, 197952)
    at("VST", [128, 2048], BF16, 198976)
    _u = [189760]

    def atu(name, shape, dtype):
        off = _u[0]
        h = at(name, shape, dtype, off)
        sz = shape[1] * mybir.dt.size(dtype)
        _u[0] = ((off + sz + 63) // 64) * 64
        return h

    atu("PRAW", [128, 1024], BF16)
    atu("PMSK", [128, 1024], BF16)
    atu("PT", [128, 1024], BF16)
    atu("SSW", [128, 768], F32)
    atu("PSW", [128, 384], BF16)
    atu("PSWT", [128, 384], BF16)
    atu("KCT", [128, 128], BF16)
    atu("VCM", [64, 264], BF16)
    atu("PCT", [64, 256], BF16)
    atu("T8", [128, 8], F32)
    atu("MCOL", [128, 32], F32)
    atu("OTS", [128, 128], BF16)
    atu("OCB", [128, 128], BF16)
    atu("PCBall", [128, 2048], BF16)
    atu("SELall", [128, 2048], BF16)
    at("SCPall", [128, 1024], F32, 65536)
    at("CEN", [128, 1024], F32, 69632)
    at("PCall", [128, 1024], F32, 65536)
    at("IMPall", [128, 1024], F32, 69632)
    assert _u[0] + BASE <= TOP, _u[0]
    at("XCH", [128, 512], F32, 193856)
    at("OH", [16, 2048], BF16, 189760)
    at("E8", [128, 32], F32, 193856)
    at("CMB", [128, 128], F32, 193984)
    at("CMBB", [128, 16], BF16, 194496)
    at("CMBT", [16, 1024], BF16, 194560)
    at("GTS", [128, 512], BF16, 196608)
    at("HTA", [128, 512], BF16, 197632)
    at("REP", [128, 1024], F32, 198656)
    at("GRP", [128, 1024], F32, 202752)

    import contextlib
    ctx = contextlib.ExitStack()

    def psum(name, shape, dtype):
        return ctx.enter_context(nc.psum_tensor(name, list(shape), dtype))

    P0 = psum("P0", [128, 512], F32)
    P1 = psum("P1", [128, 512], F32)
    P2 = psum("P2", [128, 512], F32)
    P3 = psum("P3", [128, 512], F32)
    P4 = psum("P4", [128, 512], F32)
    PB0 = psum("PB0", [128, 1024], BF16)
    PB1 = psum("PB1", [128, 1024], BF16)
    PB2 = psum("PB2", [128, 1024], BF16)
    pbank = [PB0, PB1]
    pbanks_f = {"P0": P0, "P1": P1}

    pb = PB()
    free = {}
    for k in list(T.keys()) + ["P0", "P1", "P2", "P3", "P4", "PB0", "PB1", "PB2",
                               "PCB0", "PCB1", "PCT0", "PCT1"]:
        free.setdefault(k, None)

    pb.dma("sync", lambda e: e.dma_start(T["ID"].ap(), ident_h.ap()))
    pb.dma("sync", lambda e: e.dma_start(T["MMC"].ap(), mm_h.ap()))
    pb.dma("gpsimd", lambda e: e.dma_start(
        T["CMPB"].ap().rearrange("p (j b) -> p j b", b=64),
        inp["cmp_bias"].rearrange("j p b -> p j b")))
    pb.dma("gpsimd", lambda e: e.dma_start(
        T["CMPA"].ap().rearrange("p (j b) -> p j b", b=64),
        inp["cmp_add2"].rearrange("j p b -> p j b")))
    pb.dma("gpsimd", lambda e: e.dma_start(T["TAIL"].ap(), inp["tail01"]))
    pb.dma("gpsimd", lambda e: e.dma_start(
        T["SWAB"].ap().rearrange("p (s c) -> p s c", c=384),
        inp["swab"].rearrange("s p c -> p s c")))
    pb.dma("gpsimd", lambda e: e.dma_start(
        T["WG"].ap().rearrange("p (c g) -> p c g", g=48),
        inp["wg48"].rearrange("(c p) g -> p c g", p=128)))
    pb.dma("gpsimd", lambda e: e.dma_start(
        T["WR"].ap().rearrange("p (c g) -> p c g", g=8),
        inp["wrT"].rearrange("(c p) g -> p c g", p=128)))
    pb.dma("gpsimd", lambda e: e.dma_start(
        T["GW"].ap().rearrange("p (c g) -> p c g", g=1),
        inp["gwT"].rearrange("(c p) g -> p c g", p=128),
        allow_slow_non_contiguous=True))
    vz = pb.on("vector", lambda e: e.memset(T["ZERO"].ap(), 0.0))
    for h in range(H):
        pb.dma("sync", lambda e, h=h: e.dma_start(kt_d[h, :, 0:128], T["ZERO"].ap()),
               wait=[vz])
        pb.dma("sync", lambda e, h=h: e.dma_start(v_d[h, 0:128, :], T["ZERO"].ap()),
               wait=[vz])
    pb.dma("sync", lambda e: e.dma_start(
        T["SM"].ap()[:, 40:56].rearrange("p (j o) -> p j o", o=1),
        inp["anym"].rearrange("j p o -> p j o"),
        allow_slow_non_contiguous=True))

    def emit_ln(src_name, sq_name, lt_off, wait_src):
        src = T[src_name].ap()
        pb.on("vector", lambda e, src=src: e.reduce_sum(
            T["SM"].ap()[:, 0:1], src, axis=AX.X),
            wait=list(wait_src) + [free["XC"]])
        pb.on("vector", lambda e: e.tensor_scalar_mul(
            T["SM"].ap()[:, 1:2], T["SM"].ap()[:, 0:1], 1.0 / D))
        pb.on("vector", lambda e, src=src: e.tensor_scalar_sub(
            T["XC"].ap(), src, T["SM"].ap()[:, 1:2]),
            wait=[free.get("XCsq")], swait=True)
        vxc = pb.mark("vector")
        ssq = pb.on("scalar", lambda e, sq_name=sq_name: e.activation(
            T[sq_name].ap()[:, 0:D], T["XC"].ap(), AF.Square,
            accum_out=T["SM"].ap()[:, 3:4]),
            wait=[vxc, free[sq_name]])
        free["XCsq"] = ssq
        pb.on("vector", lambda e: e.tensor_scalar(
            T["SM"].ap()[:, 4:5], T["SM"].ap()[:, 3:4], 1.0 / D, 1e-5,
            ALU.mult, ALU.add), wait=[ssq])
        vv = pb.mark("vector")
        pb.on("scalar", lambda e: e.activation(
            T["SM"].ap()[:, 5:6], T["SM"].ap()[:, 4:5], AF.Ln), wait=[vv])
        sr = pb.on("scalar", lambda e: e.activation(
            T["SM"].ap()[:, 6:7], T["SM"].ap()[:, 5:6], AF.Exp, scale=-0.5),
            swait=True)
        pb.on("vector", lambda e: e.tensor_scalar_mul(
            T["LNB"].ap(), T["XC"].ap(), T["SM"].ap()[:, 6:7]),
            wait=[sr, free["LNB"]])
        vln = pb.mark("vector")
        free["XC"] = vln
        tp = None
        for gi in range(2):
            bank = pbank[gi]
            bname = "PB%d" % gi
            for k in range(8):
                c = gi * 8 + k
                pb.on("tensor", lambda e, bank=bank, k=k, c=c: e.transpose(
                    bank[:, k * 128:(k + 1) * 128],
                    T["LNB"].ap()[:, c * 128:(c + 1) * 128], T["ID"].ap()),
                    wait=([vln, free[bname]] if k == 0 else ()),
                    selfserial=False, inc=("sp" if k == 7 else None))
            tp = pb.mark("tensor")
            pb.on("vector", lambda e, bank=bank, gi=gi, lt_off=lt_off: e.tensor_copy(
                T["LT"].ap().rearrange("p (c w) -> p c w", w=1024)[
                    :, gi * 8:(gi + 1) * 8, lt_off:lt_off + 128],
                bank[:].rearrange("p (k w) -> p k w", w=128)),
                wait=[tp])
            free[bname] = pb.mark("vector")
        free["LNB"] = tp
        return pb.mark("vector")

    def emit_rope_evac(bank, cs_off, dst_top, dst_bot, wait_first=()):
        pb.on("vector", lambda e, bank=bank, cs_off=cs_off: e.tensor_mul(
            T["RP1"].ap(), bank[0:64, :], T["CS"].ap()[0:64, cs_off:cs_off + 512]),
            wait=list(wait_first))
        pb.on("vector", lambda e, bank=bank, cs_off=cs_off: e.tensor_mul(
            T["RP2"].ap(), bank[64:128, :], T["CS"].ap()[64:128, cs_off:cs_off + 512]))
        pb.on("vector", lambda e, dst_top=dst_top: e.tensor_sub(
            dst_top, T["RP1"].ap(), T["RP2"].ap()))
        pb.on("vector", lambda e, bank=bank, cs_off=cs_off: e.tensor_mul(
            T["RP1"].ap(), bank[0:64, :], T["CS"].ap()[64:128, cs_off:cs_off + 512]))
        pb.on("vector", lambda e, bank=bank, cs_off=cs_off: e.tensor_mul(
            T["RP2"].ap(), bank[64:128, :], T["CS"].ap()[0:64, cs_off:cs_off + 512]))
        pb.on("vector", lambda e, dst_bot=dst_bot: e.tensor_add(
            dst_bot, T["RP1"].ap(), T["RP2"].ap()))
        return pb.mark("vector")

    for half in range(2):
        pb.pending["vector"].append(("sp", pb.now("sp")))
        vln_half = None
        for tt8 in range(8):
            tt = half * 8 + tt8
            xb = "XT0" if tt8 % 2 == 0 else "XT1"
            sq = "XT1" if tt8 % 2 == 0 else "XT0"
            dq = pb.dma("sync", lambda e, tt=tt, xb=xb: e.dma_start(
                T[xb].ap(), inp["x_own"][tt * 128:(tt + 1) * 128, :]),
                wait=[free[xb], free[sq]])
            vln_half = emit_ln(xb, sq, tt8 * 128, [dq])
            free[xb] = vln_half
            free[sq] = pb.mark("scalar")
        dcs = pb.dma("sync", lambda e, half=half: e.dma_start(
            T["CS"].ap(), inp["cs_own"][:, half * 1024:(half + 1) * 1024]),
            wait=[free["CS"]])
        for sl in range(8):
            wb = "WB0" if sl % 2 == 0 else "WB1"
            dw = pb.dma("gpsimd", lambda e, sl=sl, wb=wb: e.dma_start(
                T[wb].ap().rearrange("p (c m) -> p c m", m=256),
                inp["wq"][:, sl * 256:(sl + 1) * 256].rearrange(
                    "(c p) m -> p c m", p=128)),
                wait=[free[wb]])
            for hh in range(2):
                h = sl * 2 + hh
                for tb in range(2):
                    bn = "P0" if (hh * 2 + tb) % 2 == 0 else "P1"
                    bank = pbanks_f[bn]
                    for c in range(16):
                        pb.on("tensor", lambda e, bank=bank, c=c, hh=hh, tb=tb, wb=wb:
                              e.matmul(
                                  bank[:],
                                  T[wb].ap()[:, c * 256 + hh * 128: c * 256 + (hh + 1) * 128],
                                  T["LT"].ap()[:, c * 1024 + tb * 512: c * 1024 + (tb + 1) * 512],
                                  start=(c == 0), stop=(c == 15)),
                              wait=([dw, vln_half, free[bn], dcs] if c == 0 else ()),
                              selfserial=False, inc=("sp" if c == 15 else None))
                    spq = pb.mark("tensor")
                    qc = h * TOWN + half * 1024 + tb * 512
                    vr = emit_rope_evac(
                        bank, tb * 512,
                        T["QT"].ap()[0:64, qc:qc + 512],
                        T["QT"].ap()[64:128, qc:qc + 512],
                        wait_first=[spq])
                    free[bn] = vr
            free[wb] = pb.mark("tensor")
        free["CS"] = pb.mark("vector")
        for tt8 in range(8):
            tt = half * 8 + tt8
            for c in range(16):
                pb.on("tensor", lambda e, c=c, tt8=tt8: e.matmul(
                    P4[:, 0:48],
                    T["LT"].ap()[:, c * 1024 + tt8 * 128: c * 1024 + tt8 * 128 + 128],
                    T["WG"].ap()[:, c * 48:(c + 1) * 48],
                    start=(c == 0), stop=(c == 15)),
                    wait=([free["P4"], vln_half] if c == 0 else ()),
                    selfserial=False, inc=("sp" if c == 15 else None))
            spg = pb.mark("tensor")
            free["P4"] = pb.on("scalar", lambda e, tt=tt: e.activation(
                T["G"].ap()[:, tt * 48:(tt + 1) * 48], P4[:, 0:48], AF.Sigmoid),
                wait=[spg])

    for qd in range(4):
        pb.pending["vector"].append(("sp", pb.now("sp")))
        vln_q = None
        for tt8 in range(8):
            tt = qd * 8 + tt8
            xb = "XT0" if tt8 % 2 == 0 else "XT1"
            sq = "XT1" if tt8 % 2 == 0 else "XT0"
            dq = pb.dma("sync", lambda e, tt=tt, xb=xb: e.dma_start(
                T[xb].ap(), inp["x_seq"][tt * 128:(tt + 1) * 128, :]),
                wait=[free[xb], free[sq]])
            vln_q = emit_ln(xb, sq, tt8 * 128, [dq])
            free[xb] = vln_q
            free[sq] = pb.mark("scalar")
        dcs = pb.dma("sync", lambda e, qd=qd: e.dma_start(
            T["CS"].ap(), inp["cs_seq"][:, qd * 1024:(qd + 1) * 1024]),
            wait=[free["CS"]])
        for sl in range(8):
            wb = "WB0" if sl % 2 == 0 else "WB1"
            dw = pb.dma("gpsimd", lambda e, sl=sl, wb=wb: e.dma_start(
                T[wb].ap().rearrange("p (c m) -> p c m", m=256),
                inp["wk"][:, sl * 256:(sl + 1) * 256].rearrange(
                    "(c p) m -> p c m", p=128)),
                wait=[free[wb]])
            for hh in range(2):
                h = sl * 2 + hh
                for tb in range(2):
                    bn = "P0" if (hh * 2 + tb) % 2 == 0 else "P1"
                    bank = pbanks_f[bn]
                    for c in range(16):
                        pb.on("tensor", lambda e, bank=bank, c=c, hh=hh, tb=tb, wb=wb:
                              e.matmul(
                                  bank[:],
                                  T[wb].ap()[:, c * 256 + hh * 128: c * 256 + (hh + 1) * 128],
                                  T["LT"].ap()[:, c * 1024 + tb * 512: c * 1024 + (tb + 1) * 512],
                                  start=(c == 0), stop=(c == 15)),
                              wait=([dw, vln_q, free[bn], dcs] if c == 0 else ()),
                              selfserial=False, inc=("sp" if c == 15 else None))
                    spq = pb.mark("tensor")
                    vr = emit_rope_evac(
                        bank, tb * 512,
                        T["KST"].ap()[0:64, :], T["KST"].ap()[64:128, :],
                        wait_first=[spq, free["KST"]])
                    free[bn] = vr
                    col = 128 + qd * 1024 + tb * 512
                    free["KST"] = pb.dma("sync", lambda e, h=h, col=col: e.dma_start(
                        kt_d[h, :, col:col + 512], T["KST"].ap()), wait=[vr])
            free[wb] = pb.mark("tensor")
        free["CS"] = pb.mark("vector")
        for ch in range(8):
            wb = "WB0" if ch % 2 == 0 else "WB1"
            dw = pb.dma("gpsimd", lambda e, ch=ch, wb=wb: e.dma_start(
                T[wb].ap().rearrange("p (c m) -> p c m", m=256),
                inp["wv"][:, ch * 256:(ch + 1) * 256].rearrange(
                    "(c p) m -> p c m", p=128)),
                wait=[free[wb]])
            for tt8 in range(8):
                bn = "P0" if tt8 % 2 == 0 else "P1"
                bank = pbanks_f[bn]
                for c in range(16):
                    pb.on("tensor", lambda e, bank=bank, c=c, tt8=tt8, wb=wb: e.matmul(
                        bank[:, 0:256],
                        T["LT"].ap()[:, c * 1024 + tt8 * 128: c * 1024 + tt8 * 128 + 128],
                        T[wb].ap()[:, c * 256:(c + 1) * 256],
                        start=(c == 0), stop=(c == 15)),
                        wait=([dw, free[bn]] if c == 0 else ()),
                        selfserial=False, inc=("sp" if c == 15 else None))
                spv = pb.mark("tensor")
                vp = tt8 % 2
                sv_ = pb.on("scalar", lambda e, bank=bank, vp=vp: e.activation(
                    T["VST"].ap()[:, vp * 256:(vp + 1) * 256], bank[:, 0:256], AF.Copy),
                    wait=[spv, free["VST%d" % vp] if ("VST%d" % vp) in free else None])
                free[bn] = sv_
                row = 128 + qd * 1024 + tt8 * 128
                free["VST%d" % vp] = pb.dma(
                    "sync", lambda e, row=row, ch=ch, vp=vp: e.dma_start(
                        v_d[2 * ch:2 * ch + 2, row:row + 128, :].rearrange(
                            "h r c -> r h c"),
                        T["VST"].ap()[:, vp * 256:(vp + 1) * 256].rearrange(
                            "r (h c) -> r h c", c=128)),
                    wait=[sv_])
            free[wb] = pb.mark("tensor")
    pb.barrier()

    for vb in ("V0", "V1"):
        pb.on("vector", lambda e, vb=vb: e.memset(
            T[vb].ap().rearrange("p (t w) -> p t w", w=VW)[:, :, 128:132], 0.0))
        pb.on("vector", lambda e, vb=vb: e.memset(
            T[vb].ap().rearrange("p (t w) -> p t w", w=VW)[:, :, 128:129], 1.0))
    vones = pb.mark("vector")

    kv_load_q = [None] * (H + 2)
    cmp_ready = [None] * (H + 1)
    sel_ready = [None] * (H + 1)
    head_free = [None] * (H + 2)
    head_ocb_free = {}
    head_selbuf_free = [None] * (H + 2)
    vcmp_pvdone = [None, None]
    head_pcb = [None] * (H + 1)

    def kbuf(hp):
        return T["KT0"].ap() if hp == 0 else T["KT1"].ap()

    def vbuf(hp):
        return T["V0"].ap() if hp == 0 else T["V1"].ap()

    def qtile(h, j):
        return T["QT"].ap()[:, h * TOWN + j * 128: h * TOWN + (j + 1) * 128]

    def emit_kv_load(h):
        hp = h % 2
        kb, vb = ("KT0", "V0") if hp == 0 else ("KT1", "V1")
        w = [head_free[h], vones]
        pb.dma("sync", lambda e, h=h, kb=kb: e.dma_start(T[kb].ap(), kt_d[h]), wait=w)
        kv_load_q[h] = pb.dma("sync", lambda e, h=h, vb=vb: e.dma_start(
            T[vb].ap().rearrange("p (t w) -> p t w", w=VW)[:, :, 0:128],
            v_d[h].rearrange("(t p) c -> p t c", p=128)), wait=w)

    def emit_head_prep(h):
        hp = h % 2
        kb = kbuf(hp)
        vb = vbuf(hp)
        src = kb[:, 128:128 + 4096]
        width = 4096
        spall = pb.mark("tensor")
        for r in range(6):
            dst = T["XC"].ap()
            pb.on("vector", lambda e, src=src, dst=dst, width=width: e.tensor_add(
                dst[:, 0:width // 2].rearrange("p (x o) -> p x o", o=1),
                src.rearrange("p (x two) -> p x two", two=2)[:, :, 0:1],
                src.rearrange("p (x two) -> p x two", two=2)[:, :, 1:2]),
                wait=([kv_load_q[h], free["XC"], spall] if r == 0 else ()))
            src = T["XC"].ap()[:, 0:width // 2]
            width //= 2
        pb.on("vector", lambda e, src=src, hp=hp: e.tensor_scalar_mul(
            T["KCT"].ap()[:, hp * 64:(hp + 1) * 64], src, 1.0 / 64.0))
        free["XC"] = pb.mark("vector")
        kct_done = pb.mark("vector")
        for th in range(32):
            pb.on("tensor", lambda e, th=th, vb=vb: e.matmul(
                P4[0:64, 0:VW], T["MMC"].ap()[:, th * 64:(th + 1) * 64],
                vb[:, (1 + th) * VW:(1 + th) * VW + VW],
                start=(th == 0), stop=(th == 31)),
                wait=([kv_load_q[h], free["P4"], vcmp_pvdone[hp]] if th == 0 else ()),
                selfserial=False, inc=("sp" if th == 31 else None))
        spv = pb.mark("tensor")
        pb.on("vector", lambda e, hp=hp: e.tensor_copy(
            T["VCM"].ap()[0:64, hp * VW:(hp + 1) * VW], P4[0:64, 0:VW]), wait=[spv])
        free["P4"] = pb.mark("vector")
        cmp_ready[h] = pb.mark("vector")
        for grp, bank, bn in ((0, P3, "P3"), (1, P4, "P4")):
            for jj in range(8):
                j = grp * 8 + jj
                pb.on("tensor", lambda e, h=h, j=j, jj=jj, hp=hp, bank=bank: e.matmul(
                    bank[:, jj * 64:(jj + 1) * 64], qtile(h, j),
                    T["KCT"].ap()[:, hp * 64:(hp + 1) * 64],
                    start=(jj == 0), stop=(jj == 7)),
                    wait=([free[bn], kct_done] if jj == 0 else ()),
                    selfserial=False, inc=("sp" if jj == 7 else None))
            spb = pb.mark("tensor")
            pb.on("vector", lambda e, grp=grp, bank=bank: e.tensor_add(
                T["SCPall"].ap()[:, grp * 512:(grp + 1) * 512],
                bank[:, 0:512], T["CMPB"].ap()[:, grp * 512:(grp + 1) * 512]),
                wait=[spb, head_selbuf_free[h]])
            free[bn] = pb.mark("vector")
        pb.on("vector", lambda e: e.reduce_max(
            T["MCOL"].ap()[:, 0:16].rearrange("p (j o) -> p j o", o=1),
            T["SCPall"].ap().rearrange("p (j b) -> p j b", b=64),
            axis=AX.X, negate=True), swait=True)
        pb.on("vector", lambda e: e.tensor_scalar_min(
            T["MCOL"].ap()[:, 0:16], T["MCOL"].ap()[:, 0:16], 1.0e5), swait=True)
        pb.on("vector", lambda e: e.tensor_tensor(
            T["CEN"].ap().rearrange("p (j b) -> p j b", b=64),
            T["SCPall"].ap().rearrange("p (j b) -> p j b", b=64),
            T["MCOL"].ap()[:, 0:16].rearrange("p (j o) -> p j o", o=1)
            .to_broadcast([128, 16, 64]), ALU.add), swait=True)
        vcen = pb.mark("vector")
        sexp = pb.on("scalar", lambda e: e.activation(
            T["PCall"].ap(), T["CEN"].ap(), AF.Exp, scale=SCALE), wait=[vcen])
        pb.on("vector", lambda e: e.reduce_sum(
            T["MCOL"].ap()[:, 16:32].rearrange("p (j o) -> p j o", o=1),
            T["PCall"].ap().rearrange("p (j b) -> p j b", b=64), axis=AX.X),
            wait=[sexp])
        pb.on("vector", lambda e: e.tensor_scalar_add(
            T["MCOL"].ap()[:, 16:32], T["MCOL"].ap()[:, 16:32], 1.0e-30), swait=True)
        pb.on("vector", lambda e: e.reciprocal(
            T["MCOL"].ap()[:, 16:32], T["MCOL"].ap()[:, 16:32]), swait=True)
        pb.on("vector", lambda e: e.tensor_tensor(
            T["PCall"].ap().rearrange("p (j b) -> p j b", b=64),
            T["PCall"].ap().rearrange("p (j b) -> p j b", b=64),
            T["MCOL"].ap()[:, 16:32].rearrange("p (j o) -> p j o", o=1)
            .to_broadcast([128, 16, 64]), ALU.mult), swait=True)
        pb.on("vector", lambda e: e.tensor_tensor(
            T["PCall"].ap().rearrange("p (j b) -> p j b", b=64),
            T["PCall"].ap().rearrange("p (j b) -> p j b", b=64),
            T["SM"].ap()[:, 40:56].rearrange("p (j o) -> p j o", o=1)
            .to_broadcast([128, 16, 64]), ALU.mult), swait=True)
        vpc = pb.mark("vector")
        head_pcb[h] = pb.on("scalar", lambda e, hp=hp: e.activation(
            T["PCBall"].ap()[:, hp * 1024:(hp + 1) * 1024], T["PCall"].ap(),
            AF.Copy), wait=[vpc])
        pb.on("vector", lambda e: e.tensor_add(
            T["IMPall"].ap(), T["PCall"].ap(), T["CMPA"].ap()), wait=[vpc],
            swait=True)
        for j in range(NJ):
            pb.on("vector", lambda e, j=j: e.max(
                out=T["T8"].ap(), in_=T["IMPall"].ap()[:, j * 64:(j + 1) * 64]),
                swait=True)
            pb.on("vector", lambda e, j=j, hp=hp: e.tensor_scalar(
                T["SELall"].ap()[:, hp * 1024 + j * 64: hp * 1024 + (j + 1) * 64],
                T["IMPall"].ap()[:, j * 64:(j + 1) * 64],
                T["T8"].ap()[:, 7:8], None, ALU.is_ge), swait=True)
        sel_ready[h] = pb.mark("vector")

    class JS:
        pass

    jobs = []
    for h in range(H):
        for j in range(NJ):
            js = JS()
            js.h, js.j, js.f = h, j, h * NJ + j
            js.KE = 2 * j + 2
            js.nch = ceil_div(js.KE, 4)
            jobs.append(js)
    NJOBS = len(jobs)

    def emit_A(js):
        h, j = js.h, js.j
        hp = h % 2
        js.spm = [None] * js.nch
        for c in range(js.nch):
            cw = 128 * min(4, js.KE - 4 * c)
            bn = "P0" if c % 2 == 0 else "P1"
            bank = pbanks_f[bn]
            pb.on("tensor", lambda e, h=h, j=j, c=c, cw=cw, bank=bank, hp=hp: e.matmul(
                bank[:, 0:cw], qtile(h, j),
                kbuf(hp)[:, 128 + c * 512: 128 + c * 512 + cw],
                start=True, stop=True),
                wait=[free[bn], kv_load_q[h]], selfserial=False, inc="sp")
            js.spm[c] = pb.mark("tensor")

    def emit_A_post(js):
        h, j, jp = js.h, js.j, js.f % 2
        for c in range(js.nch):
            cw = 128 * min(4, js.KE - 4 * c)
            bn = "P0" if c % 2 == 0 else "P1"
            bank = pbanks_f[bn]
            ssx = pb.on("scalar", lambda e, c=c, cw=cw, bank=bank: e.activation(
                T["S0"].ap()[:, c * 512: c * 512 + cw], bank[:, 0:cw],
                AF.Copy, scale=SCALE),
                wait=[js.spm[c]])
            free[bn] = ssx
            pb.on("vector", lambda e, c=c, cw=cw, jp=jp: e.reduce_max(
                T["SM"].ap()[:, 24 + jp * 8 + c: 25 + jp * 8 + c],
                T["S0"].ap()[:, c * 512: c * 512 + cw], axis=AX.X),
                wait=[ssx])
        pb.on("vector", lambda e, jp=jp, nch=js.nch: e.reduce_max(
            T["SM"].ap()[:, 12 + jp:13 + jp],
            T["SM"].ap()[:, 24 + jp * 8: 24 + jp * 8 + nch],
            axis=AX.X, negate=True), swait=True)
        js.mneg_done = pb.mark("vector")

    def emit_B1(js):
        h, j, jp = js.h, js.j, js.f % 2
        hp = h % 2
        vb = vbuf(hp)
        vmask_marks = [None] * js.nch
        for c in range(js.nch):
            nt = min(4, js.KE - 4 * c)
            cw = 128 * nt
            pp = c % 2
            sse = pb.on("scalar", lambda e, c=c, cw=cw, pp=pp, jp=jp: e.activation(
                T["PRAW"].ap()[:, pp * 512: pp * 512 + cw],
                T["S0"].ap()[:, c * 512:c * 512 + cw],
                AF.Exp, bias=T["SM"].ap()[:, 12 + jp:13 + jp]),
                wait=[js.mneg_done, free["PRAW%d" % pp]])
            nb_c = cw // 64
            pb.on("vector", lambda e, cw=cw, pp=pp, c=c, h=h, j=j, nb_c=nb_c:
                  e.tensor_tensor(
                      T["PMSK"].ap()[:, pp * 512:pp * 512 + cw].rearrange(
                          "p (a b) -> p a b", b=64),
                      T["PRAW"].ap()[:, pp * 512:pp * 512 + cw].rearrange(
                          "p (a b) -> p a b", b=64),
                      T["SELall"].ap()[:, hp * 1024 + j * 64 + 8 * c:
                                       hp * 1024 + j * 64 + 8 * c + nb_c]
                      .rearrange("p (a o) -> p a o", o=1).to_broadcast([128, nb_c, 64]),
                      ALU.mult),
                  wait=[sse, sel_ready[h], free["PMSK%d" % pp]])
            free["PRAW%d" % pp] = pb.mark("vector")
            if c == js.nch - 1:
                pb.on("vector", lambda e, pp=pp, cw=cw: e.tensor_mul(
                    T["PMSK"].ap()[:, pp * 512 + cw - 256: pp * 512 + cw],
                    T["PMSK"].ap()[:, pp * 512 + cw - 256: pp * 512 + cw],
                    T["TAIL"].ap()), swait=True)
            vmask_marks[c] = pb.mark("vector")
        first = True
        for c in range(js.nch):
            nt = min(4, js.KE - 4 * c)
            cw = 128 * nt
            pp = c % 2
            for t in range(nt):
                pb.on("tensor", lambda e, t=t, pp=pp: e.transpose(
                    pbank[pp][:, t * 128:(t + 1) * 128],
                    T["PMSK"].ap()[:, pp * 512 + t * 128: pp * 512 + (t + 1) * 128],
                    T["ID"].ap()),
                    wait=([vmask_marks[c], free["PB%d" % pp]] if t == 0 else ()),
                    selfserial=False, inc=("sp" if t == nt - 1 else None))
            tpm = pb.mark("tensor")
            free["PMSK%d" % pp] = tpm
            pb.on("scalar", lambda e, pp=pp, cw=cw: e.activation(
                T["PT"].ap()[:, pp * 512: pp * 512 + cw],
                pbank[pp][:, 0:cw], AF.Copy),
                wait=[tpm, ptslot_free[pp]])
            ptm = pb.mark("scalar")
            free["PB%d" % pp] = ptm
            for t in range(nt):
                kidx = 1 + 4 * c + t
                st = first and t == 0
                sp_ = (c == js.nch - 1 and t == nt - 1)
                pb.on("tensor", lambda e, t=t, pp=pp, kidx=kidx, vb=vb, st=st,
                      sp_=sp_: e.matmul(
                          P2[:, 0:129],
                          T["PT"].ap()[:, pp * 512 + t * 128: pp * 512 + (t + 1) * 128],
                          vb[:, kidx * VW: kidx * VW + 129],
                          start=st, stop=sp_),
                      wait=(([ptm] + ([free["P2"]] if st else []))
                            if t == 0 else ()),
                      selfserial=False, inc=("sp" if sp_ else None))
            first = False
            ptslot_free[pp] = pb.mark("tensor")
        js.slc_pv = pb.mark("tensor")

    def emit_swaS(js):
        h, j, jp = js.h, js.j, js.f % 2
        hp = h % 2
        pb.on("tensor", lambda e, h=h, j=j, hp=hp: e.matmul(
            P3[:, 0:384], qtile(h, j), kbuf(hp)[:, (2 * j) * 128: (2 * j) * 128 + 384],
            start=True, stop=True),
            wait=[free["P3"], kv_load_q[h]], selfserial=False, inc="sp")
        spw = pb.mark("tensor")
        jsel = 0 if j == 0 else 1
        pb.on("vector", lambda e, jp=jp, jsel=jsel: e.tensor_add(
            T["SSW"].ap()[:, jp * 384:(jp + 1) * 384], P3[:, 0:384],
            T["SWAB"].ap()[:, jsel * 384:(jsel + 1) * 384]),
            wait=[spw, free["SSW%d" % jp]])
        free["P3"] = pb.mark("vector")
        pb.on("vector", lambda e, jp=jp: e.reduce_max(
            T["SM"].ap()[:, 14 + jp:15 + jp], T["SSW"].ap()[:, jp * 384:(jp + 1) * 384],
            axis=AX.X, negate=True), swait=True)
        pb.on("vector", lambda e, jp=jp: e.tensor_scalar_mul(
            T["SM"].ap()[:, 16 + jp:17 + jp], T["SM"].ap()[:, 14 + jp:15 + jp], SCALE),
            swait=True)
        js.swa_m = pb.mark("vector")

    def emit_B2a(js):
        h, j = js.h, js.j
        ocb = "OC0" if h % 2 == 0 else "OC1"
        pb.on("vector", lambda e: e.reciprocal(T["SM"].ap()[:, 22:23], P2[:, 128:129]),
              wait=[js.slc_pv])
        pb.on("vector", lambda e, h=h, j=j: e.tensor_mul(
            T["SM"].ap()[:, 23:24], T["SM"].ap()[:, 22:23],
            T["G"].ap()[:, j * 48 + 16 + h: j * 48 + 16 + h + 1]), swait=True)
        pb.on("vector", lambda e, h=h, j=j, ocb=ocb: e.scalar_tensor_tensor(
            T[ocb].ap()[:, j * 128:(j + 1) * 128], P2[:, 0:128], T["SM"].ap()[:, 23:24],
            T[ocb].ap()[:, j * 128:(j + 1) * 128], ALU.mult, ALU.add),
            wait=[js.oc_cmp_done], swait=True)
        free["P2"] = pb.mark("vector")
        js.done = pb.mark("vector")

    def emit_B2(js):
        h, j, jp = js.h, js.j, js.f % 2
        hp = h % 2
        vb = vbuf(hp)
        ocb = "OC0" if hp == 0 else "OC1"
        pb.on("tensor", lambda e, j=j, hp=hp: e.transpose(
            PB2[0:64, 384:512],
            T["PCBall"].ap()[:, hp * 1024 + j * 64: hp * 1024 + (j + 1) * 64],
            T["ID"].ap()),
            wait=[head_pcb[h], free["PB2"]], selfserial=False, inc="sp")
        spct = pb.mark("tensor")
        pb.on("scalar", lambda e, jp=jp: e.activation(
            T["PCT"].ap()[0:64, jp * 128:(jp + 1) * 128], PB2[0:64, 384:512], AF.Copy),
            wait=[spct, free["PCT%d" % jp]])
        free["PB2"] = pb.mark("scalar")
        vpct = pb.mark("scalar")
        pb.on("tensor", lambda e, jp=jp, hp=hp: e.matmul(
            P4[:, 0:VW], T["PCT"].ap()[0:64, jp * 128:(jp + 1) * 128],
            T["VCM"].ap()[0:64, hp * VW:(hp + 1) * VW],
            start=True, stop=True),
            wait=[vpct, free["P4"]], selfserial=False, inc="sp")
        spcv = pb.mark("tensor")
        vcmp_pvdone[hp] = spcv
        free["PCT%d" % jp] = spcv
        pb.on("vector", lambda e, h=h, j=j, ocb=ocb: e.tensor_scalar_mul(
            T[ocb].ap()[:, j * 128:(j + 1) * 128], P4[:, 0:128],
            T["G"].ap()[:, j * 48 + h: j * 48 + h + 1]),
            wait=[spcv, head_ocb_free.get(h)])
        free["P4"] = pb.mark("vector")
        js.oc_cmp_done = pb.mark("vector")
        ssew = pb.on("scalar", lambda e, jp=jp: e.activation(
            T["PSW"].ap(), T["SSW"].ap()[:, jp * 384:(jp + 1) * 384], AF.Exp,
            bias=T["SM"].ap()[:, 16 + jp:17 + jp], scale=SCALE),
            wait=[js.swa_m, free["PSW"]])
        free["SSW%d" % jp] = ssew
        for t in range(3):
            pb.on("tensor", lambda e, t=t: e.transpose(
                PB2[:, t * 128:(t + 1) * 128], T["PSW"].ap()[:, t * 128:(t + 1) * 128],
                T["ID"].ap()),
                wait=([ssew, free["PB2"]] if t == 0 else ()),
                selfserial=False, inc=("sp" if t == 2 else None))
        spwt = pb.mark("tensor")
        free["PSW"] = spwt
        pb.on("scalar", lambda e: e.activation(T["PSWT"].ap(), PB2[:, 0:384], AF.Copy),
              wait=[spwt, free["PSWT"]])
        free["PB2"] = pb.mark("scalar")
        vpswt = pb.mark("scalar")
        for t in range(3):
            kidx = 2 * j + t
            pb.on("tensor", lambda e, t=t, kidx=kidx, vb=vb: e.matmul(
                P4[:, 0:129], T["PSWT"].ap()[:, t * 128:(t + 1) * 128],
                vb[:, kidx * VW: kidx * VW + 129],
                start=(t == 0), stop=(t == 2)),
                wait=([vpswt, free["P4"]] if t == 0 else ()),
                selfserial=False, inc=("sp" if t == 2 else None))
        spwv = pb.mark("tensor")
        free["PSWT"] = spwv
        pb.on("vector", lambda e: e.reciprocal(T["SM"].ap()[:, 20:21], P4[:, 128:129]),
              wait=[spwv])
        pb.on("vector", lambda e, h=h, j=j: e.tensor_mul(
            T["SM"].ap()[:, 21:22], T["SM"].ap()[:, 20:21],
            T["G"].ap()[:, j * 48 + 32 + h: j * 48 + 32 + h + 1]), swait=True)
        pb.on("vector", lambda e, h=h, j=j, ocb=ocb: e.scalar_tensor_tensor(
            T[ocb].ap()[:, j * 128:(j + 1) * 128], P4[:, 0:128], T["SM"].ap()[:, 21:22],
            T[ocb].ap()[:, j * 128:(j + 1) * 128], ALU.mult, ALU.add), swait=True)
        free["P4"] = pb.mark("vector")

    def emit_head_tail(h, ocdone):
        ocb = "OC0" if h % 2 == 0 else "OC1"
        prev_sp = None
        for j in range(NJ):
            scb = pb.on("scalar", lambda e, j=j, ocb=ocb: e.activation(
                T["OCB"].ap(), T[ocb].ap()[:, j * 128:(j + 1) * 128], AF.Copy),
                wait=[ocdone, prev_sp])
            pb.on("tensor", lambda e: e.transpose(
                PB2[:, 512:640], T["OCB"].ap(), T["ID"].ap()),
                wait=[scb, free["PB2"]], selfserial=False, inc="sp")
            prev_sp = pb.mark("tensor")
            pb.on("vector", lambda e: e.tensor_copy(T["OTS"].ap(), PB2[:, 512:640]),
                  wait=[prev_sp, free["OTS"]])
            free["PB2"] = pb.mark("vector")
            vot = pb.mark("vector")
            free["OTS"] = pb.dma("sync", lambda e, j=j, h=h: e.dma_start(
                ot_d[j, h], T["OTS"].ap()), wait=[vot])
        head_free[h + 2] = pb.mark("tensor")
        head_ocb_free[h + 2] = pb.mark("scalar")
        head_selbuf_free[h + 2] = pb.mark("tensor")

    for k in ["PRAW0", "PRAW1", "PMSK0", "PMSK1", "SSW0", "SSW1",
              "PCT0", "PCT1"]:
        free.setdefault(k, None)
    ptslot_free = [None, None]

    emit_kv_load(0)
    emit_head_prep(0)
    emit_kv_load(1)

    for f in range(NJOBS + 2):
        if f >= 2:
            emit_B2(jobs[f - 2])
            emit_B2a(jobs[f - 2])
            if jobs[f - 2].j == NJ - 1:
                hdone = jobs[f - 2].h
                emit_head_tail(hdone, jobs[f - 2].done)
                if hdone + 2 < H:
                    emit_kv_load(hdone + 2)
        if f < NJOBS:
            emit_A(jobs[f])
        if 1 <= f <= NJOBS:
            emit_B1(jobs[f - 1])
            emit_swaS(jobs[f - 1])
        if f < NJOBS:
            js = jobs[f]
            emit_A_post(js)
            if js.j == 12 and js.h + 1 < H:
                emit_head_prep(js.h + 1)
    pb.barrier()


    dot = pb.dma("sync", lambda e: e.dma_start(
        T["QT"].ap().rearrange("p (tt h t) -> p tt h t", h=16, t=128),
        ot_d.rearrange("tt h d t -> d tt h t")))
    for ch in range(8):
        wb = "WB0" if ch % 2 == 0 else "WB1"
        dw = pb.dma("gpsimd", lambda e, ch=ch, wb=wb: e.dma_start(
            T[wb].ap().rearrange("p (c m) -> p c m", m=256),
            inp["wo"][:, ch * 256:(ch + 1) * 256].rearrange(
                "(c p) m -> p c m", p=128)),
            wait=[free[wb]])
        for tt in range(NJ):
            xp = tt % 2
            dx = pb.dma("sync", lambda e, tt=tt, ch=ch, xp=xp: e.dma_start(
                T["XCH"].ap()[:, xp * 256:(xp + 1) * 256],
                inp["x_own"][tt * 128:(tt + 1) * 128, ch * 256:(ch + 1) * 256]),
                wait=[free["XCH%d" % xp] if ("XCH%d" % xp) in free else None])
            bn = "P0" if tt % 2 == 0 else "P1"
            bank = pbanks_f[bn]
            for hh in range(16):
                pb.on("tensor", lambda e, bank=bank, hh=hh, tt=tt, wb=wb: e.matmul(
                    bank[:, 0:256],
                    T["QT"].ap()[:, tt * 2048 + hh * 128: tt * 2048 + (hh + 1) * 128],
                    T[wb].ap()[:, hh * 256:(hh + 1) * 256],
                    start=(hh == 0), stop=(hh == 15)),
                    wait=([dw, dot, free[bn]] if hh == 0 else ()),
                    selfserial=False, inc=("sp" if hh == 15 else None))
            spa = pb.mark("tensor")
            pb.on("vector", lambda e, bank=bank, xp=xp: e.tensor_add(
                T["XCH"].ap()[:, xp * 256:(xp + 1) * 256],
                T["XCH"].ap()[:, xp * 256:(xp + 1) * 256], bank[:, 0:256]),
                wait=[spa, dx])
            free[bn] = pb.mark("vector")
            vadd = pb.mark("vector")
            free["XCH%d" % xp] = pb.dma("sync", lambda e, tt=tt, ch=ch, xp=xp:
                                        e.dma_start(
                xmid_d[tt * 128:(tt + 1) * 128, ch * 256:(ch + 1) * 256],
                T["XCH"].ap()[:, xp * 256:(xp + 1) * 256]), wait=[vadd])
        free[wb] = pb.mark("tensor")
    free["XT0"] = None
    free["XT1"] = None
    free["LNB"] = None
    free["XC"] = None
    qs_all = ("qs", pb.now("qs"))
    for tt in range(NJ):
        xb = "XT0" if tt % 2 == 0 else "XT1"
        sq = "S0"
        dq = pb.dma("sync", lambda e, tt=tt, xb=xb: e.dma_start(
            T[xb].ap(), xmid_d[tt * 128:(tt + 1) * 128, :]),
            wait=[free[xb], qs_all if tt == 0 else None])
        vln = emit_ln(xb, sq, (tt % 8) * 128, [dq])
        free[xb] = vln
        free["S0"] = pb.mark("scalar")
        if tt % 8 == 7:
            half = tt // 8
            for c in range(16):
                pb.dma("sync", lambda e, c=c, half=half: e.dma_start(
                    ln2_d[c, :, half * 1024:(half + 1) * 1024],
                    T["LT"].ap()[:, c * 1024:(c + 1) * 1024]), wait=[vln])
            pb.pending["vector"].append(("qs", pb.now("qs")))
    pb.barrier()

    doh = pb.dma("sync", lambda e: e.dma_start(T["OH"].ap(), oh_h.ap()))
    ent_down_sp = None
    for hf in range(2):
        dln = pb.dma("sync", lambda e, hf=hf: e.dma_start(
            T["LT"].ap().rearrange("p (c w) -> p c w", w=1024),
            ln2_d[:, :, hf * 1024:(hf + 1) * 1024].rearrange("c p t -> p c t")),
            wait=[("sp", pb.now("sp")) if hf == 1 else None])
        dxm = pb.dma("sync", lambda e, hf=hf: e.dma_start(
            T["XME"].ap().rearrange("p (t w) -> p t w", w=2048),
            xmid_d[hf * 1024:(hf + 1) * 1024, :].rearrange("(t p) w -> p t w", p=128)),
            wait=[("sv", pb.now("sv")) if hf == 1 else None])

        def xm_slice(tt, ch):
            return T["XME"].ap()[:, tt * 2048 + ch * 512: tt * 2048 + (ch + 1) * 512]

        for tt in range(8):
            for c in range(16):
                pb.on("tensor", lambda e, tt=tt, c=c: e.matmul(
                    P4[:, 0:8],
                    T["LT"].ap()[:, c * 1024 + tt * 128: c * 1024 + tt * 128 + 128],
                    T["WR"].ap()[:, c * 8:(c + 1) * 8], start=(c == 0), stop=(c == 15)),
                    wait=([dln, free["P4"]] if c == 0 else ()),
                    selfserial=False, inc=("sp" if c == 15 else None))
            spr = pb.mark("tensor")
            pb.on("vector", lambda e: e.reduce_max(
                T["SM"].ap()[:, 30:31], P4[:, 0:8], axis=AX.X, negate=True),
                wait=[spr])
            vmr = pb.mark("vector")
            sse = pb.on("scalar", lambda e: e.activation(
                T["E8"].ap()[:, 0:8], P4[:, 0:8], AF.Exp, bias=T["SM"].ap()[:, 30:31]),
                wait=[vmr])
            free["P4"] = sse
            pb.on("vector", lambda e: e.max(out=T["T8"].ap(), in_=T["E8"].ap()[:, 0:8]),
                  wait=[sse])
            pb.on("vector", lambda e: e.tensor_scalar(
                T["E8"].ap()[:, 8:16], T["E8"].ap()[:, 0:8], T["T8"].ap()[:, 1:2],
                None, ALU.is_ge), swait=True)
            pb.on("vector", lambda e: e.tensor_mul(
                T["E8"].ap()[:, 16:24], T["E8"].ap()[:, 0:8], T["E8"].ap()[:, 8:16]))
            pb.on("vector", lambda e: e.reduce_sum(
                T["SM"].ap()[:, 32:33], T["E8"].ap()[:, 16:24], axis=AX.X))
            pb.on("vector", lambda e: e.reciprocal(
                T["SM"].ap()[:, 33:34], T["SM"].ap()[:, 32:33]))
            pb.on("vector", lambda e, tt=tt: e.tensor_scalar_mul(
                T["CMB"].ap()[:, tt * 16: tt * 16 + 8], T["E8"].ap()[:, 16:24],
                T["SM"].ap()[:, 33:34]), swait=True)
            pb.on("vector", lambda e, tt=tt: e.memset(
                T["CMB"].ap()[:, tt * 16 + 9:(tt + 1) * 16], 0.0))
        for tt in range(8):
            for c in range(16):
                pb.on("tensor", lambda e, tt=tt, c=c: e.matmul(
                    P4[:, 0:1],
                    T["LT"].ap()[:, c * 1024 + tt * 128: c * 1024 + tt * 128 + 128],
                    T["GW"].ap()[:, c:c + 1], start=(c == 0), stop=(c == 15)),
                    wait=([free["P4"]] if c == 0 else ()),
                    selfserial=False, inc=("sp" if c == 15 else None))
            spg = pb.mark("tensor")
            free["P4"] = pb.on("scalar", lambda e, tt=tt: e.activation(
                T["CMB"].ap()[:, tt * 16 + 8: tt * 16 + 9], P4[:, 0:1], AF.Sigmoid),
                wait=[spg])
        for tt in range(8):
            vcm = pb.mark("vector")
            scb = pb.on("scalar", lambda e, tt=tt: e.activation(
                T["CMBB"].ap(), T["CMB"].ap()[:, tt * 16:(tt + 1) * 16], AF.Copy),
                wait=[vcm])
            pb.on("tensor", lambda e: e.transpose(
                PB0[0:16, 0:128], T["CMBB"].ap(), T["ID"].ap()),
                wait=[scb, free["PB0"], doh], selfserial=False, inc="sp")
            spt = pb.mark("tensor")
            pb.on("vector", lambda e, tt=tt: e.tensor_copy(
                T["CMBT"].ap()[0:16, tt * 128:(tt + 1) * 128], PB0[0:16, 0:128]),
                wait=[spt])
            free["PB0"] = pb.mark("vector")
        vcmbt = pb.mark("vector")
        for q2 in range(2):
            pb.on("tensor", lambda e, q2=q2: e.matmul(
                P4[:], T["OH"].ap()[:, 8 * 128:9 * 128],
                T["CMBT"].ap()[0:16, q2 * 512:(q2 + 1) * 512],
                start=True, stop=True),
                wait=[vcmbt, free["P4"], doh], selfserial=False, inc="sp")
            spr = pb.mark("tensor")
            free["P4"] = pb.on("scalar", lambda e, q2=q2: e.activation(
                T["GRP"].ap()[:, q2 * 512:(q2 + 1) * 512], P4[:], AF.Copy),
                wait=[spr])
        grp_done = pb.mark("scalar")

        for ent in range(12):
            is_shared = ent < 4
            eidx = ent - 4
            if is_shared:
                rep_t, rep_done = "GRP", grp_done
            else:
                for q2 in range(2):
                    pb.on("tensor", lambda e, eidx=eidx, q2=q2: e.matmul(
                        P4[:], T["OH"].ap()[:, eidx * 128:(eidx + 1) * 128],
                        T["CMBT"].ap()[0:16, q2 * 512:(q2 + 1) * 512],
                        start=True, stop=True),
                        wait=[vcmbt, free["REP"], free["P4"]],
                        selfserial=False, inc="sp")
                    spr = pb.mark("tensor")
                    free["P4"] = pb.on("scalar", lambda e, q2=q2: e.activation(
                        T["REP"].ap()[:, q2 * 512:(q2 + 1) * 512], P4[:], AF.Copy),
                        wait=[spr])
                rep_t, rep_done = "REP", pb.mark("scalar")
            for sl in range(4):
                wbg = "WB0" if sl % 2 == 0 else "WB1"
                wbu = "KT0" if sl % 2 == 0 else "KT1"
                if is_shared:
                    gsrc = inp["sgT"][:, ent * 1024 + sl * 256: ent * 1024 + (sl + 1) * 256]
                    usrc = inp["suT"][:, ent * 1024 + sl * 256: ent * 1024 + (sl + 1) * 256]
                else:
                    gsrc = inp["wgT"][eidx, :, sl * 256:(sl + 1) * 256]
                    usrc = inp["wuT"][eidx, :, sl * 256:(sl + 1) * 256]
                dg = pb.dma("gpsimd", lambda e, gsrc=gsrc, wbg=wbg: e.dma_start(
                    T[wbg].ap().rearrange("p (c m) -> p c m", m=256),
                    gsrc.rearrange("(c p) m -> p c m", p=128)),
                    wait=[free[wbg]])
                du = pb.dma("gpsimd", lambda e, usrc=usrc, wbu=wbu: e.dma_start(
                    T[wbu].ap()[:, 0:4096].rearrange("p (c m) -> p c m", m=256),
                    usrc.rearrange("(c p) m -> p c m", p=128)),
                    wait=[free[wbu]])
                for it2 in range(2):
                    it = sl * 2 + it2
                    for tch in range(2):
                        for c in range(16):
                            pb.on("tensor", lambda e, c=c, it2=it2, tch=tch, wbg=wbg:
                                  e.matmul(
                                      P0[:],
                                      T[wbg].ap()[:, c * 256 + it2 * 128: c * 256 + (it2 + 1) * 128],
                                      T["LT"].ap()[:, c * 1024 + tch * 512: c * 1024 + (tch + 1) * 512],
                                      start=(c == 0), stop=(c == 15)),
                                  wait=([dg, free["P0"], dln] if c == 0 else ()),
                                  selfserial=False, inc=("sp" if c == 15 else None))
                        spg_ = pb.mark("tensor")
                        sgs = pb.on("scalar", lambda e: e.activation(
                            T["GTS"].ap(), P0[:], AF.Silu), wait=[spg_, free["GTS"]])
                        free["P0"] = sgs
                        for c in range(16):
                            pb.on("tensor", lambda e, c=c, it2=it2, tch=tch, wbu=wbu:
                                  e.matmul(
                                      P1[:],
                                      T[wbu].ap()[:, c * 256 + it2 * 128: c * 256 + (it2 + 1) * 128],
                                      T["LT"].ap()[:, c * 1024 + tch * 512: c * 1024 + (tch + 1) * 512],
                                      start=(c == 0), stop=(c == 15)),
                                  wait=([du, free["P1"]] if c == 0 else ()),
                                  selfserial=False, inc=("sp" if c == 15 else None))
                        spu = pb.mark("tensor")
                        pb.on("vector", lambda e: e.tensor_mul(
                            T["HTA"].ap(), P1[:], T["GTS"].ap()),
                            wait=[spu, sgs, free["HTA"]])
                        free["P1"] = pb.mark("vector")
                        free["GTS"] = pb.mark("vector")
                        pb.on("vector", lambda e, it=it, tch=tch, rep_t=rep_t:
                              e.tensor_mul(
                                  T["HT"].ap()[:, it * 1024 + tch * 512: it * 1024 + (tch + 1) * 512],
                                  T["HTA"].ap(), T[rep_t].ap()[:, tch * 512:(tch + 1) * 512]),
                              wait=[rep_done, ent_down_sp])
                        free["HTA"] = pb.mark("vector")
                free[wbg] = pb.mark("tensor")
                free[wbu] = pb.mark("tensor")
            ht_done = pb.mark("vector")
            free["REP"] = ht_done
            for ch in range(4):
                wbd = "WB0" if ch % 2 == 0 else "WB1"
                if is_shared:
                    dsrc = inp["sdT"][ent * 1024:(ent + 1) * 1024, ch * 512:(ch + 1) * 512]
                else:
                    dsrc = inp["wdT"][eidx, :, ch * 512:(ch + 1) * 512]
                dd = pb.dma("gpsimd", lambda e, dsrc=dsrc, wbd=wbd: e.dma_start(
                    T[wbd].ap().rearrange("p (c m) -> p c m", m=512),
                    dsrc.rearrange("(c p) m -> p c m", p=128)),
                    wait=[free[wbd]])
                for tt in range(8):
                    bn = "P2" if (tt + ch) % 2 == 0 else "P3"
                    bank = P2 if (tt + ch) % 2 == 0 else P3
                    for it in range(8):
                        pb.on("tensor", lambda e, bank=bank, it=it, tt=tt, wbd=wbd:
                              e.matmul(
                                  bank[:],
                                  T["HT"].ap()[:, it * 1024 + tt * 128: it * 1024 + tt * 128 + 128],
                                  T[wbd].ap()[:, it * 512:(it + 1) * 512],
                                  start=(it == 0), stop=(it == 7)),
                              wait=([dd, ht_done, free[bn], dxm] if it == 0 else ()),
                              selfserial=False, inc=("sp" if it == 7 else None))
                    spd = pb.mark("tensor")
                    pb.on("vector", lambda e, tt=tt, ch=ch, bank=bank: e.tensor_add(
                        xm_slice(tt, ch), xm_slice(tt, ch), bank[:]),
                        wait=[spd])
                    free[bn] = pb.mark("vector")
                free[wbd] = pb.mark("tensor")
            ent_down_sp = pb.mark("tensor")
        vfin = pb.mark("vector")
        for tt in range(8):
            pb.dma("sync", lambda e, tt=tt, hf=hf: e.dma_start(
                y[hf * 1024 + tt * 128: hf * 1024 + (tt + 1) * 128, :],
                T["XME"].ap()[:, tt * 2048:(tt + 1) * 2048]), wait=[vfin])

    pb.on("sync", lambda e: None,
          wait=[("qs", pb.now("qs")), ("qg", pb.now("qg"))], selfserial=False)

    sems = {}
    for name in set(list(pb.cnt.keys()) + ["sv", "ss", "sp", "sg", "sy", "qs", "qg"]):
        sems[name] = ctx.enter_context(nc.semaphore(name))

    with nc.Block() as blk:
        def make_runner(ename):
            def runner(e):
                for fn, waits, inc, n in pb.ops[ename]:
                    for s, v in waits:
                        e.wait_ge(sems[s], v)
                    ins = fn(e)
                    if inc is not None:
                        if ins is None:
                            raise RuntimeError(f"{ename}: inc with no instruction")
                        ins.then_inc(sems[inc], n)
            return runner

        blk.tensor(make_runner("tensor"))
        blk.vector(make_runner("vector"))
        blk.scalar(make_runner("scalar"))
        blk.gpsimd(make_runner("gpsimd"))
        blk.sync(make_runner("sync"))

    ctx.close()
    return nc



def _host_masks(p):
    cmp_bias = np.zeros((NJ, 128, NB), np.float32)
    cmp_add2 = np.zeros((NJ, 128, NB), np.float32)
    anym = np.zeros((NJ, 128, 1), np.float32)
    for j in range(NJ):
        a = 2 * j + p
        i = 128 * a + np.arange(128)
        b = np.arange(NB)
        mask = i[:, None] >= (64 * (b[None, :] + 1) - 1)
        cmp_bias[j] = np.where(mask, 0.0, NEG)
        anym[j, :, 0] = mask.any(axis=1).astype(np.float32)
        cur = i // 64
        cmp_add2[j, np.arange(128), cur] = 2.0
    tail01 = np.zeros((128, 256), np.float32)
    r = np.arange(128)
    c = np.arange(128)
    tri = (c[None, :] <= r[:, None]).astype(np.float32)
    if p == 0:
        tail01[:, 0:128] = tri
    else:
        tail01[:, 0:128] = 1.0
        tail01[:, 128:256] = tri
    swab = np.full((2, 128, 384), NEG, np.float32)
    cc = np.arange(384)
    delta = 128 * (p + 1) + r[:, None] - cc[None, :]
    band = (delta >= 0) & (delta <= WIN)
    swab[1] = np.where(band, 0.0, NEG)
    band0 = band & (cc[None, :] >= 128)
    swab[0] = np.where(band0, 0.0, NEG)
    if p == 1:
        swab[0] = swab[1]
    return cmp_bias, cmp_add2, anym, tail01, swab


def _rope_tables(pos):
    half = HD // 2
    inv = 1.0 / (10000.0 ** (np.arange(half, dtype=np.float64) / half))
    th = pos[:, None].astype(np.float64) * inv[None, :]
    cos = np.cos(th).astype(np.float32).T
    sin = np.sin(th).astype(np.float32).T
    return np.ascontiguousarray(np.concatenate([cos, sin], axis=0))


_NC = None


def kernel(L0_attn_x, L0_attn_Wq, L0_attn_Wk, L0_attn_Wv, L0_attn_Wg, L0_attn_Wo,
           L0_ffn_w_router, L0_ffn_wg, L0_ffn_wu, L0_ffn_wd,
           L0_ffn_sw_gate, L0_ffn_sw_up, L0_ffn_sw_down, L0_ffn_gate_w):
    global _NC
    x = np.ascontiguousarray(np.asarray(L0_attn_x, dtype=np.float32))
    wq = np.ascontiguousarray(np.asarray(L0_attn_Wq, np.float32))
    wk = np.ascontiguousarray(np.asarray(L0_attn_Wk, np.float32))
    wv = np.ascontiguousarray(np.asarray(L0_attn_Wv, np.float32))
    wg48 = np.ascontiguousarray(np.asarray(L0_attn_Wg, np.float32))
    wo = np.ascontiguousarray(np.asarray(L0_attn_Wo, np.float32))
    wrT = np.ascontiguousarray(np.asarray(L0_ffn_w_router, np.float32).T)
    wgT = np.ascontiguousarray(np.transpose(np.asarray(L0_ffn_wg, np.float32), (0, 2, 1)))
    wuT = np.ascontiguousarray(np.transpose(np.asarray(L0_ffn_wu, np.float32), (0, 2, 1)))
    wdT = np.ascontiguousarray(np.transpose(np.asarray(L0_ffn_wd, np.float32), (0, 2, 1)))
    sgT = np.ascontiguousarray(np.asarray(L0_ffn_sw_gate, np.float32).T)
    suT = np.ascontiguousarray(np.asarray(L0_ffn_sw_up, np.float32).T)
    sdT = np.ascontiguousarray(np.asarray(L0_ffn_sw_down, np.float32).T)
    gwT = np.ascontiguousarray(np.asarray(L0_ffn_gate_w, np.float32).T)

    cs_seq = _rope_tables(np.arange(N))
    in_maps = []
    for cix in range(8):
        s, p = cix // 2, cix % 2
        xs = np.ascontiguousarray(x[s])
        xo = np.ascontiguousarray(xs.reshape(32, 128, D)[p::2].reshape(TOWN, D))
        pos_own = (np.arange(TOWN) // 128 * 2 + p) * 128 + np.arange(TOWN) % 128
        cmp_bias, cmp_add2, anym, tail01, swab = _host_masks(p)
        in_maps.append({
            "x_own": xo, "x_seq": xs,
            "wq": wq, "wk": wk, "wv": wv, "wg48": wg48, "wo": wo,
            "wrT": wrT, "gwT": gwT, "wgT": wgT, "wuT": wuT, "wdT": wdT,
            "sgT": sgT, "suT": suT, "sdT": sdT,
            "cs_own": _rope_tables(pos_own), "cs_seq": cs_seq,
            "cmp_bias": cmp_bias, "cmp_add2": cmp_add2, "anym": anym,
            "tail01": tail01, "swab": swab,
        })

    if _NC is None:
        _NC = build_nc()
    res = run_bass_kernel_spmd(_NC, in_maps, core_ids=list(range(8)))
    out = np.empty((4, N, D), np.float32)
    for cix in range(8):
        s, p = cix // 2, cix % 2
        out[s].reshape(32, 128, D)[p::2] = res.results[cix]["y"].reshape(NJ, 128, D)
    return out


# version 14 (final)
# speedup vs baseline: 1.2057x; 1.0369x over previous
import numpy as np
import ml_dtypes
import concourse.bass as bass
import concourse.mybir as mybir
from concourse.bass_utils import run_bass_kernel_spmd

F32 = mybir.dt.float32
BF16 = mybir.dt.bfloat16
AF = mybir.ActivationFunctionType
ALU = mybir.AluOpType
AX = mybir.AxisListType

D = 2048
N = 4096
H = 16
HD = 128
NJ = 16
TOWN = NJ * 128
NB = N // 64
WIN = 64
SCALE = float(HD) ** -0.5
NKT = 33
NE = 8
EI = 1024
SI = 4096
NEG = -1.0e30

ENGS = ("tensor", "vector", "scalar", "gpsimd", "sync")
DEBUG_STAGE = False


class PB:
    def __init__(self):
        self.ops = {e: [] for e in ENGS}
        self.cnt = {}
        self.pending = {e: [] for e in ENGS}
        self.eng_sem = {"vector": "sv", "scalar": "ss", "gpsimd": "sg",
                        "tensor": "sp", "sync": "sy"}
        self.dma_sem = {"sync": "qs", "gpsimd": "qg", "scalar": "qa"}

    def now(self, sem):
        return self.cnt.get(sem, 0)

    def mark(self, eng):
        return (self.eng_sem[eng], self.now(self.eng_sem[eng]))

    def on(self, eng, fn, wait=(), inc=None, n=1, selfserial=None, swait=False):
        if selfserial is None:
            selfserial = eng in ("vector", "scalar")
        waits = [w for w in wait if w is not None and w[1] > 0]
        if selfserial:
            s = self.eng_sem[eng]
            c = self.now(s)
            if c > 0:
                waits.append((s, c))
            if inc is None:
                inc = s
        if self.pending[eng]:
            waits = self.pending[eng] + waits
            self.pending[eng] = []
        wd = {}
        for s, v in waits:
            wd[s] = max(wd.get(s, 0), v)
        self.ops[eng].append((fn, tuple(wd.items()), inc, n))
        if inc is not None:
            self.cnt[inc] = self.cnt.get(inc, 0) + n
            return (inc, self.cnt[inc])
        return None

    def dma(self, eng, fn, wait=()):
        q = self.dma_sem[eng]
        return self.on(eng, fn, wait=wait, inc=q, n=16, selfserial=False)

    def force_inc_last(self, eng):
        if not self.ops[eng]:
            return None
        fn, w, inc, n = self.ops[eng][-1]
        if inc is None:
            s = self.eng_sem[eng]
            self.cnt[s] = self.cnt.get(s, 0) + 1
            self.ops[eng][-1] = (fn, w, s, 1)
        if self.ops[eng][-1][2] == self.eng_sem[eng]:
            return self.mark(eng)
        return None

    def barrier(self):
        marks = []
        for e in ("tensor", "vector", "scalar", "gpsimd"):
            m = self.force_inc_last(e)
            if m and m[1] > 0:
                marks.append(m)
        for q in ("qs", "qg", "qa"):
            if self.now(q) > 0:
                marks.append((q, self.now(q)))
        for e in ENGS:
            self.pending[e].extend(list(marks))


def ceil_div(a, b):
    return (a + b - 1) // b


def build_nc():
    nc = bass.Bass("TRN2", target_bir_lowering=False)

    inp = {}

    def di(name, shape, dtype=F32):
        inp[name] = nc.dram_tensor(name, list(shape), dtype, kind="ExternalInput").ap()
        return inp[name]

    di("x_own", (TOWN, D))
    di("x_seq", (N, D))
    di("wq", (D, D)); di("wk", (D, D)); di("wv", (D, D))
    di("wg48", (D, 48)); di("wo", (D, D))
    di("wrT", (D, NE)); di("gwT", (D, 1))
    di("wgT", (NE, D, EI)); di("wuT", (NE, D, EI)); di("wdT", (NE, EI, D))
    di("sgT", (D, SI)); di("suT", (D, SI)); di("sdT", (SI, D))
    di("cs_own", (128, TOWN)); di("cs_seq", (128, N))
    di("cmp_bias", (NJ, 128, NB)); di("cmp_add2", (NJ, 128, NB))
    di("anym", (NJ, 128, 1))
    di("tail01", (128, 256)); di("swab", (2, 128, 384))

    y = nc.dram_tensor("y", [TOWN, D], F32, kind="ExternalOutput").ap()

    stg = "ExternalOutput" if DEBUG_STAGE else "Internal"
    kt_d = nc.dram_tensor("kt_d", [H, 128, NKT * 128], BF16, kind=stg).ap()
    v_d = nc.dram_tensor("v_d", [H, NKT * 128, HD], BF16, kind=stg).ap()
    xmid_d = nc.dram_tensor("xmid_d", [TOWN, D], F32, kind=stg).ap()
    lt_snap = nc.dram_tensor("lt_snap", [2, 128, 16384], BF16, kind=stg).ap()
    xme_snap = nc.dram_tensor("xme_snap", [2, 128, 16384], F32, kind=stg).ap()
    cmb_snap = nc.dram_tensor("cmb_snap", [128, 128], F32, kind=stg).ap()
    cmbt_snap = nc.dram_tensor("cmbt_snap", [16, 1024], BF16, kind=stg).ap()
    grp_snap = nc.dram_tensor("grp_snap", [128, 1024], F32, kind=stg).ap()
    ht_snap = nc.dram_tensor("ht_snap", [128, 8192], BF16, kind=stg).ap()
    xme1_snap = nc.dram_tensor("xme1_snap", [128, 16384], F32, kind=stg).ap()
    xme2_snap = nc.dram_tensor("xme2_snap", [128, 16384], F32, kind=stg).ap()
    ln2_d = nc.dram_tensor("ln2_d", [16, 128, TOWN // 2], BF16, kind=stg).ap()

    ident_h = nc.inline_tensor(np.eye(128, dtype=ml_dtypes.bfloat16), "identc")
    mm_np = np.zeros((128, 32 * 64), dtype=np.float32)
    for th in range(32):
        for t in range(128):
            mm_np[t, 64 * th + 2 * th + t // 64] = 1.0 / 64.0
    mm_h = nc.inline_tensor(mm_np.astype(ml_dtypes.bfloat16), "meanmatc")
    oh_np = np.zeros((16, 16 * 128), dtype=np.float32)
    for e in range(16):
        oh_np[e, 128 * e:128 * (e + 1)] = 1.0
    oh_h = nc.inline_tensor(oh_np.astype(ml_dtypes.bfloat16), "onehotc")

    BASE = ((nc.sbuf_base + 63) // 64) * 64
    TOP = nc.sbuf_top

    T = {}

    def at(name, shape, dtype, off):
        assert BASE + off + shape[1] * mybir.dt.size(dtype) <= TOP, name
        h = nc.alloc_sbuf_tensor_at(name, list(shape), dtype, offset=BASE + off)
        T[name] = h
        return h

    at("QT", [128, 32768], BF16, 0)
    at("XME", [128, 16384], F32, 0)
    at("LT", [128, 16384], BF16, 65536)
    at("SCPall", [128, 1024], F32, 65536)
    at("CEN", [128, 1024], F32, 69632)
    at("PCall", [128, 1024], F32, 65536)
    at("IMPall", [128, 1024], F32, 69632)
    at("S0A", [128, 4480], F32, 73728)
    at("S0B", [128, 4480], F32, 91648)
    at("PM0", [128, 4480], BF16, 109568)
    at("PM1", [128, 4480], BF16, 118528)
    at("SQ", [128, 2048], BF16, 98304)
    at("XT2", [128, 2048], F32, 102400)
    at("HT", [128, 8192], BF16, 98304)
    at("WB0", [128, 4096], BF16, 114688)
    at("WB1", [128, 4096], BF16, 122880)
    at("CMPB", [128, 1024], BF16, 131072)
    at("CMPA", [128, 1024], BF16, 133120)
    at("MMC", [128, 2048], BF16, 135168)
    at("V0", [128, NKT * 128], BF16, 141312)
    at("V1", [128, NKT * 128], BF16, 149760)
    at("XT0", [128, 2048], F32, 141312)
    at("XT1", [128, 2048], F32, 149568)
    at("KT0", [128, 4224], BF16, 158912)
    at("KT1", [128, 4224], BF16, 167424)
    at("RP1", [64, 512], F32, 175872)
    at("RP2", [64, 512], F32, 177920)
    at("SWAB", [128, 768], F32, 175872)
    at("TAIL", [128, 256], BF16, 178944)
    at("PCG", [128, 2048], BF16, 179968)
    at("ID", [128, 128], BF16, 184064)
    at("ZERO", [128, 128], BF16, 184320)
    at("SM", [128, 64], F32, 184576)
    at("G", [128, 768], F32, 184832)
    at("WG", [128, 768], BF16, 187904)
    at("WR", [128, 128], BF16, 189440)
    at("GW", [128, 16], BF16, 189696)
    at("T8", [128, 8], F32, 189728)
    at("LNB", [128, 2048], BF16, 189760)
    at("CS", [128, 1024], F32, 193856)
    at("KST", [128, 512], BF16, 197952)
    at("VST", [128, 2048], BF16, 198976)
    _u = [189760]

    def atu(name, shape, dtype):
        off = _u[0]
        h = at(name, shape, dtype, off)
        sz = shape[1] * mybir.dt.size(dtype)
        _u[0] = ((off + sz + 63) // 64) * 64
        return h

    atu("PT", [128, 1024], BF16)
    atu("MB0", [128, 1024], F32)
    atu("MB1", [128, 1024], F32)
    atu("VCM", [64, 256], BF16)
    atu("PCT", [64, 256], BF16)
    atu("KSUM", [128, 64], F32)
    atu("MCH", [128, 32], F32)
    atu("LS", [128, 32], F32)
    atu("SC", [128, 8], F32)
    atu("MT", [128, 4], F32)
    assert _u[0] + BASE <= TOP, _u[0]
    at("XCH", [128, 512], F32, 193856)
    at("OH", [16, 2048], BF16, 189760)
    at("E8", [128, 32], F32, 193856)
    at("CMB", [128, 128], F32, 193984)
    at("CMBB", [128, 16], BF16, 194496)
    at("CMBT", [16, 1024], BF16, 194560)
    at("GTS", [128, 512], BF16, 196608)
    at("HTA", [128, 512], BF16, 197632)
    at("REP", [128, 1024], F32, 198656)
    at("GRP", [128, 1024], F32, 202752)

    import contextlib
    ctx = contextlib.ExitStack()

    def psum(name, shape, dtype):
        return ctx.enter_context(nc.psum_tensor(name, list(shape), dtype))

    P0 = psum("P0", [128, 512], F32)
    P1 = psum("P1", [128, 512], F32)
    P2 = psum("P2", [128, 512], F32)
    P3 = psum("P3", [128, 512], F32)
    P4 = psum("P4", [128, 512], F32)
    OA = psum("OA", [128, 512], F32)
    PB0 = psum("PB0", [128, 1024], BF16)
    PB1 = psum("PB1", [128, 1024], BF16)
    pbank = [PB0, PB1]
    ROT = [P0, P1, P2, P3]
    ROTN = ["P0", "P1", "P2", "P3"]
    pbanks_f = {"P0": P0, "P1": P1}

    pb = PB()
    free = {}
    for k in list(T.keys()) + ["P0", "P1", "P2", "P3", "P4", "PB0", "PB1"]:
        free.setdefault(k, None)

    pb.dma("sync", lambda e: e.dma_start(T["ID"].ap(), ident_h.ap()))
    pb.dma("sync", lambda e: e.dma_start(T["MMC"].ap(), mm_h.ap()))
    pb.dma("gpsimd", lambda e: e.dma_start(
        T["CMPB"].ap().rearrange("p (j b) -> p j b", b=64),
        inp["cmp_bias"].rearrange("j p b -> p j b")))
    pb.dma("gpsimd", lambda e: e.dma_start(
        T["CMPA"].ap().rearrange("p (j b) -> p j b", b=64),
        inp["cmp_add2"].rearrange("j p b -> p j b")))
    pb.dma("gpsimd", lambda e: e.dma_start(
        T["WG"].ap().rearrange("p (c g) -> p c g", g=48),
        inp["wg48"].rearrange("(c p) g -> p c g", p=128)))
    pb.dma("gpsimd", lambda e: e.dma_start(
        T["WR"].ap().rearrange("p (c g) -> p c g", g=8),
        inp["wrT"].rearrange("(c p) g -> p c g", p=128)))
    pb.dma("gpsimd", lambda e: e.dma_start(
        T["GW"].ap().rearrange("p (c g) -> p c g", g=1),
        inp["gwT"].rearrange("(c p) g -> p c g", p=128),
        allow_slow_non_contiguous=True))
    vz = pb.on("vector", lambda e: e.memset(T["ZERO"].ap(), 0.0))
    for h in range(H):
        pb.dma("sync", lambda e, h=h: e.dma_start(kt_d[h, :, 0:128], T["ZERO"].ap()),
               wait=[vz])
        pb.dma("sync", lambda e, h=h: e.dma_start(v_d[h, 0:128, :], T["ZERO"].ap()),
               wait=[vz])
    pb.dma("sync", lambda e: e.dma_start(
        T["SM"].ap()[:, 40:56].rearrange("p (j o) -> p j o", o=1),
        inp["anym"].rearrange("j p o -> p j o"),
        allow_slow_non_contiguous=True))

    def emit_ln(src_name, lt_off, wait_src):
        src = T[src_name].ap()
        pb.on("vector", lambda e, src=src: e.reduce_sum(
            T["SM"].ap()[:, 0:1], src, axis=AX.X), wait=list(wait_src))
        ssq = pb.on("scalar", lambda e, src=src: e.activation(
            T["SQ"].ap(), src, AF.Square, accum_out=T["SM"].ap()[:, 3:4]),
            wait=list(wait_src) + [free.get("SMsq")])
        pb.on("vector", lambda e: e.tensor_scalar_mul(
            T["SM"].ap()[:, 1:2], T["SM"].ap()[:, 0:1], 1.0 / D))
        pb.on("vector", lambda e: e.tensor_tensor(
            T["SM"].ap()[:, 2:3], T["SM"].ap()[:, 1:2], T["SM"].ap()[:, 1:2],
            ALU.mult), swait=True)
        pb.on("vector", lambda e: e.tensor_scalar(
            T["SM"].ap()[:, 4:5], T["SM"].ap()[:, 3:4], 1.0 / D, 1e-5,
            ALU.mult, ALU.add), wait=[ssq])
        free["SMsq"] = pb.mark("vector")
        pb.on("vector", lambda e: e.tensor_sub(
            T["SM"].ap()[:, 5:6], T["SM"].ap()[:, 4:5], T["SM"].ap()[:, 2:3]),
            wait=[free.get("SMvar")], swait=True)
        vv = pb.mark("vector")
        pb.on("scalar", lambda e: e.activation(
            T["SM"].ap()[:, 6:7], T["SM"].ap()[:, 5:6], AF.Ln), wait=[vv])
        free["SMvar"] = pb.mark("scalar")
        sr = pb.on("scalar", lambda e: e.activation(
            T["SM"].ap()[:, 7:8], T["SM"].ap()[:, 6:7], AF.Exp, scale=-0.5),
            wait=[free.get("SMrs")], swait=True)
        pb.on("vector", lambda e, src=src: e.tensor_scalar(
            T["LNB"].ap(), src, T["SM"].ap()[:, 1:2], T["SM"].ap()[:, 7:8],
            ALU.subtract, ALU.mult), wait=[sr, free["LNB"]])
        vln = pb.mark("vector")
        free["SMrs"] = vln
        tp = None
        for gi in range(2):
            bank = pbank[gi]
            bname = "PB%d" % gi
            for k in range(8):
                c = gi * 8 + k
                pb.on("tensor", lambda e, bank=bank, k=k, c=c: e.transpose(
                    bank[:, k * 128:(k + 1) * 128],
                    T["LNB"].ap()[:, c * 128:(c + 1) * 128], T["ID"].ap()),
                    wait=([vln, free[bname]] if k == 0 else ()),
                    selfserial=False, inc=("sp" if k == 7 else None))
            tp = pb.mark("tensor")
            pb.on("vector", lambda e, bank=bank, gi=gi, lt_off=lt_off: e.tensor_copy(
                T["LT"].ap().rearrange("p (c w) -> p c w", w=1024)[
                    :, gi * 8:(gi + 1) * 8, lt_off:lt_off + 128],
                bank[:].rearrange("p (k w) -> p k w", w=128)),
                wait=[tp])
            free[bname] = pb.mark("vector")
        free["LNB"] = tp
        return pb.mark("vector")

    def emit_rope_evac(bank, cs_off, dst_top, dst_bot, wait_first=()):
        pb.on("vector", lambda e, bank=bank, cs_off=cs_off: e.tensor_mul(
            T["RP1"].ap(), bank[0:64, :], T["CS"].ap()[0:64, cs_off:cs_off + 512]),
            wait=list(wait_first))
        pb.on("vector", lambda e, bank=bank, cs_off=cs_off: e.tensor_mul(
            T["RP2"].ap(), bank[64:128, :], T["CS"].ap()[64:128, cs_off:cs_off + 512]))
        pb.on("vector", lambda e, dst_top=dst_top: e.tensor_sub(
            dst_top, T["RP1"].ap(), T["RP2"].ap()))
        pb.on("vector", lambda e, bank=bank, cs_off=cs_off: e.tensor_mul(
            T["RP1"].ap(), bank[0:64, :], T["CS"].ap()[64:128, cs_off:cs_off + 512]))
        pb.on("vector", lambda e, bank=bank, cs_off=cs_off: e.tensor_mul(
            T["RP2"].ap(), bank[64:128, :], T["CS"].ap()[0:64, cs_off:cs_off + 512]))
        pb.on("vector", lambda e, dst_bot=dst_bot: e.tensor_add(
            dst_bot, T["RP1"].ap(), T["RP2"].ap()))
        return pb.mark("vector")

    xrot = ["XT0", "XT1", "XT2"]

    for half in range(2):
        pb.pending["vector"].append(("sp", pb.now("sp")))
        vln_half = None
        for tt8 in range(8):
            tt = half * 8 + tt8
            xb = xrot[tt % 3]
            dq = pb.dma("sync", lambda e, tt=tt, xb=xb: e.dma_start(
                T[xb].ap(), inp["x_own"][tt * 128:(tt + 1) * 128, :]),
                wait=[free[xb]])
            vln_half = emit_ln(xb, tt8 * 128, [dq])
            free[xb] = vln_half
        dcs = pb.dma("sync", lambda e, half=half: e.dma_start(
            T["CS"].ap(), inp["cs_own"][:, half * 1024:(half + 1) * 1024]),
            wait=[free["CS"]])
        for sl in range(8):
            wb = "WB0" if sl % 2 == 0 else "WB1"
            dw = pb.dma("gpsimd", lambda e, sl=sl, wb=wb: e.dma_start(
                T[wb].ap().rearrange("p (c m) -> p c m", m=256),
                inp["wq"][:, sl * 256:(sl + 1) * 256].rearrange(
                    "(c p) m -> p c m", p=128)),
                wait=[free[wb]])
            for hh in range(2):
                h = sl * 2 + hh
                for tb in range(2):
                    bn = "P0" if (hh * 2 + tb) % 2 == 0 else "P1"
                    bank = pbanks_f[bn]
                    for c in range(16):
                        pb.on("tensor", lambda e, bank=bank, c=c, hh=hh, tb=tb, wb=wb:
                              e.matmul(
                                  bank[:],
                                  T[wb].ap()[:, c * 256 + hh * 128: c * 256 + (hh + 1) * 128],
                                  T["LT"].ap()[:, c * 1024 + tb * 512: c * 1024 + (tb + 1) * 512],
                                  start=(c == 0), stop=(c == 15)),
                              wait=([dw, vln_half, free[bn], dcs] if c == 0 else ()),
                              selfserial=False, inc=("sp" if c == 15 else None))
                    spq = pb.mark("tensor")
                    qc = h * TOWN + half * 1024 + tb * 512
                    vr = emit_rope_evac(
                        bank, tb * 512,
                        T["QT"].ap()[0:64, qc:qc + 512],
                        T["QT"].ap()[64:128, qc:qc + 512],
                        wait_first=[spq])
                    free[bn] = vr
            free[wb] = pb.mark("tensor")
        free["CS"] = pb.mark("vector")
        for tt8 in range(8):
            tt = half * 8 + tt8
            for c in range(16):
                pb.on("tensor", lambda e, c=c, tt8=tt8: e.matmul(
                    P4[:, 0:48],
                    T["LT"].ap()[:, c * 1024 + tt8 * 128: c * 1024 + tt8 * 128 + 128],
                    T["WG"].ap()[:, c * 48:(c + 1) * 48],
                    start=(c == 0), stop=(c == 15)),
                    wait=([free["P4"], vln_half] if c == 0 else ()),
                    selfserial=False, inc=("sp" if c == 15 else None))
            spg = pb.mark("tensor")
            free["P4"] = pb.on("scalar", lambda e, tt=tt: e.activation(
                T["G"].ap()[:, tt * 48:(tt + 1) * 48], P4[:, 0:48], AF.Sigmoid),
                wait=[spg])

    for qd in range(4):
        pb.pending["vector"].append(("sp", pb.now("sp")))
        vln_q = None
        for tt8 in range(8):
            tt = qd * 8 + tt8
            xb = xrot[tt % 3]
            dq = pb.dma("sync", lambda e, tt=tt, xb=xb: e.dma_start(
                T[xb].ap(), inp["x_seq"][tt * 128:(tt + 1) * 128, :]),
                wait=[free[xb]])
            vln_q = emit_ln(xb, tt8 * 128, [dq])
            free[xb] = vln_q
        dcs = pb.dma("sync", lambda e, qd=qd: e.dma_start(
            T["CS"].ap(), inp["cs_seq"][:, qd * 1024:(qd + 1) * 1024]),
            wait=[free["CS"]])
        for sl in range(8):
            wb = "WB0" if sl % 2 == 0 else "WB1"
            dw = pb.dma("gpsimd", lambda e, sl=sl, wb=wb: e.dma_start(
                T[wb].ap().rearrange("p (c m) -> p c m", m=256),
                inp["wk"][:, sl * 256:(sl + 1) * 256].rearrange(
                    "(c p) m -> p c m", p=128)),
                wait=[free[wb]])
            for hh in range(2):
                h = sl * 2 + hh
                for tb in range(2):
                    bn = "P0" if (hh * 2 + tb) % 2 == 0 else "P1"
                    bank = pbanks_f[bn]
                    for c in range(16):
                        pb.on("tensor", lambda e, bank=bank, c=c, hh=hh, tb=tb, wb=wb:
                              e.matmul(
                                  bank[:],
                                  T[wb].ap()[:, c * 256 + hh * 128: c * 256 + (hh + 1) * 128],
                                  T["LT"].ap()[:, c * 1024 + tb * 512: c * 1024 + (tb + 1) * 512],
                                  start=(c == 0), stop=(c == 15)),
                              wait=([dw, vln_q, free[bn], dcs] if c == 0 else ()),
                              selfserial=False, inc=("sp" if c == 15 else None))
                    spq = pb.mark("tensor")
                    vr = emit_rope_evac(
                        bank, tb * 512,
                        T["KST"].ap()[0:64, :], T["KST"].ap()[64:128, :],
                        wait_first=[spq, free["KST"]])
                    free[bn] = vr
                    col = 128 + qd * 1024 + tb * 512
                    free["KST"] = pb.dma("scalar", lambda e, h=h, col=col: e.dma_start(
                        kt_d[h, :, col:col + 512], T["KST"].ap()), wait=[vr])
            free[wb] = pb.mark("tensor")
        free["CS"] = pb.mark("vector")
        for ch in range(8):
            wb = "WB0" if ch % 2 == 0 else "WB1"
            dw = pb.dma("gpsimd", lambda e, ch=ch, wb=wb: e.dma_start(
                T[wb].ap().rearrange("p (c m) -> p c m", m=256),
                inp["wv"][:, ch * 256:(ch + 1) * 256].rearrange(
                    "(c p) m -> p c m", p=128)),
                wait=[free[wb]])
            for tt8 in range(8):
                bn = "P0" if tt8 % 2 == 0 else "P1"
                bank = pbanks_f[bn]
                for c in range(16):
                    pb.on("tensor", lambda e, bank=bank, c=c, tt8=tt8, wb=wb: e.matmul(
                        bank[:, 0:256],
                        T["LT"].ap()[:, c * 1024 + tt8 * 128: c * 1024 + tt8 * 128 + 128],
                        T[wb].ap()[:, c * 256:(c + 1) * 256],
                        start=(c == 0), stop=(c == 15)),
                        wait=([dw, free[bn]] if c == 0 else ()),
                        selfserial=False, inc=("sp" if c == 15 else None))
                spv = pb.mark("tensor")
                vp = tt8 % 2
                sv_ = pb.on("scalar", lambda e, bank=bank, vp=vp: e.activation(
                    T["VST"].ap()[:, vp * 256:(vp + 1) * 256], bank[:, 0:256], AF.Copy),
                    wait=[spv, free["VST%d" % vp] if ("VST%d" % vp) in free else None])
                free[bn] = sv_
                row = 128 + qd * 1024 + tt8 * 128
                free["VST%d" % vp] = pb.dma(
                    "scalar", lambda e, row=row, ch=ch, vp=vp: e.dma_start(
                        v_d[2 * ch:2 * ch + 2, row:row + 128, :].rearrange(
                            "h r c -> r h c"),
                        T["VST"].ap()[:, vp * 256:(vp + 1) * 256].rearrange(
                            "r (h c) -> r h c", c=128)),
                    wait=[sv_])
            free[wb] = pb.mark("tensor")
    pb.barrier()

    dsw = pb.dma("sync", lambda e: e.dma_start(
        T["SWAB"].ap().rearrange("p (s c) -> p s c", c=384),
        inp["swab"].rearrange("s p c -> p s c")))
    dtl = pb.dma("gpsimd", lambda e: e.dma_start(T["TAIL"].ap(), inp["tail01"]))

    kv_load_q = [None] * (H + 2)
    kbuf_free = [None] * (H + 2)
    vbuf_free = [None] * (H + 2)
    kct_free = [None] * (H + 2)
    vcm_free = [None] * (H + 2)
    pcg_free = [None] * (H + 2)
    kct_done = [None] * (H + 1)
    vcm_ready = [None] * (H + 1)
    pcall_free = [None] * (H + 2)
    s0_free = [None, None]
    pm_free = [None, None]
    ls_free = [None, None]
    mt_free = [None, None]
    pct_free = [None, None]
    pt_free = [None, None]
    oa_free = [None] * 4
    gcnt = [0]
    tpp = [0]

    def kbuf(hp):
        return T["KT0"].ap() if hp == 0 else T["KT1"].ap()

    def vbuf(hp):
        return T["V0"].ap() if hp == 0 else T["V1"].ap()

    def qtile(h, j):
        return T["QT"].ap()[:, h * TOWN + j * 128: h * TOWN + (j + 1) * 128]

    def mbuf(hp):
        return T["MB0"].ap() if hp == 0 else T["MB1"].ap()

    def pmbuf(jp):
        return T["PM0"].ap() if jp == 0 else T["PM1"].ap()

    def s0buf(jp):
        return T["S0A"].ap() if jp == 0 else T["S0B"].ap()

    def emit_kv_load(h):
        hp = h % 2
        kb, vb = ("KT0", "V0") if hp == 0 else ("KT1", "V1")
        w = [kbuf_free[h], vbuf_free[h]]
        pb.dma("sync", lambda e, h=h, kb=kb: e.dma_start(T[kb].ap(), kt_d[h]), wait=w)
        kv_load_q[h] = pb.dma("sync", lambda e, h=h, vb=vb: e.dma_start(
            T[vb].ap().rearrange("p (t w) -> p t w", w=128),
            v_d[h].rearrange("(t p) c -> p t c", p=128)), wait=w)

    def emit_prep_a(h):
        hp = h % 2
        kb = kbuf(hp)
        vb = vbuf(hp)
        pb.on("vector", lambda e, kb=kb: e.reduce_sum(
            T["KSUM"].ap().rearrange("p (a o) -> p a o", o=1),
            kb[:, 128:128 + 4096].rearrange("p (a b) -> p a b", b=64),
            axis=AX.X),
            wait=[kv_load_q[h], kct_free[h]])
        pb.on("vector", lambda e, hp=hp: e.tensor_scalar_mul(
            T["KCT"].ap()[:, hp * 64:(hp + 1) * 64], T["KSUM"].ap(), 1.0 / 64.0),
            swait=True)
        kct_done[h] = pb.mark("vector")
        for th in range(32):
            pb.on("tensor", lambda e, th=th, vb=vb: e.matmul(
                P4[0:64, 0:128], T["MMC"].ap()[:, th * 64:(th + 1) * 64],
                vb[:, (1 + th) * 128:(2 + th) * 128],
                start=(th == 0), stop=(th == 31)),
                wait=([kv_load_q[h], free["P4"]] if th == 0 else ()),
                selfserial=False, inc=("sp" if th == 31 else None))
        spv = pb.mark("tensor")
        pb.on("vector", lambda e, hp=hp: e.tensor_copy(
            T["VCM"].ap()[0:64, hp * 128:(hp + 1) * 128], P4[0:64, 0:128]),
            wait=[spv, vcm_free[h]])
        free["P4"] = pb.mark("vector")
        vcm_ready[h] = pb.mark("vector")

    def emit_prep_b(h):
        hp = h % 2
        for grp in range(2):
            for jj in range(8):
                j = grp * 8 + jj
                pb.on("tensor", lambda e, h=h, j=j, jj=jj, hp=hp: e.matmul(
                    P4[:, jj * 64:(jj + 1) * 64], qtile(h, j),
                    T["KCT"].ap()[:, hp * 64:(hp + 1) * 64],
                    start=(jj == 0), stop=(jj == 7)),
                    wait=([free["P4"], kct_done[h]] if jj == 0 else ()),
                    selfserial=False, inc=("sp" if jj == 7 else None))
            spb = pb.mark("tensor")
            pb.on("vector", lambda e, grp=grp: e.tensor_add(
                T["SCPall"].ap()[:, grp * 512:(grp + 1) * 512],
                P4[:, 0:512], T["CMPB"].ap()[:, grp * 512:(grp + 1) * 512]),
                wait=[spb])
            free["P4"] = pb.mark("vector")
        if grp == 1:
            kct_free[h + 2] = pb.mark("tensor")
        pb.on("vector", lambda e: e.reduce_max(
            T["MCOL"].ap()[:, 0:16].rearrange("p (j o) -> p j o", o=1),
            T["SCPall"].ap().rearrange("p (j b) -> p j b", b=64),
            axis=AX.X, negate=True), swait=True)
        pb.on("vector", lambda e: e.tensor_scalar_min(
            T["MCOL"].ap()[:, 0:16], T["MCOL"].ap()[:, 0:16], 1.0e5), swait=True)
        pb.on("vector", lambda e: e.tensor_tensor(
            T["CEN"].ap().rearrange("p (j b) -> p j b", b=64),
            T["SCPall"].ap().rearrange("p (j b) -> p j b", b=64),
            T["MCOL"].ap()[:, 0:16].rearrange("p (j o) -> p j o", o=1)
            .to_broadcast([128, 16, 64]), ALU.add), swait=True)
        vcen = pb.mark("vector")
        sexp = pb.on("scalar", lambda e: e.activation(
            T["PCall"].ap(), T["CEN"].ap(), AF.Exp, scale=SCALE),
            wait=[vcen, pcall_free[h]])
        pb.on("vector", lambda e: e.reduce_sum(
            T["MCOL"].ap()[:, 16:32].rearrange("p (j o) -> p j o", o=1),
            T["PCall"].ap().rearrange("p (j b) -> p j b", b=64), axis=AX.X),
            wait=[sexp])
        pb.on("vector", lambda e: e.tensor_scalar_add(
            T["MCOL"].ap()[:, 16:32], T["MCOL"].ap()[:, 16:32], 1.0e-30), swait=True)
        pb.on("vector", lambda e: e.reciprocal(
            T["MCOL"].ap()[:, 16:32], T["MCOL"].ap()[:, 16:32]), swait=True)
        pb.on("vector", lambda e: e.tensor_tensor(
            T["PCall"].ap().rearrange("p (j b) -> p j b", b=64),
            T["PCall"].ap().rearrange("p (j b) -> p j b", b=64),
            T["MCOL"].ap()[:, 16:32].rearrange("p (j o) -> p j o", o=1)
            .to_broadcast([128, 16, 64]), ALU.mult), swait=True)
        pb.on("vector", lambda e: e.tensor_tensor(
            T["PCall"].ap().rearrange("p (j b) -> p j b", b=64),
            T["PCall"].ap().rearrange("p (j b) -> p j b", b=64),
            T["SM"].ap()[:, 40:56].rearrange("p (j o) -> p j o", o=1)
            .to_broadcast([128, 16, 64]), ALU.mult), swait=True)
        pb.on("vector", lambda e: e.tensor_add(
            T["IMPall"].ap(), T["PCall"].ap(), T["CMPA"].ap()), swait=True)

    def emit_prep_c(h):
        hp = h % 2
        gv = T["G"].ap().rearrange("p (j g) -> p j g", g=48)[:, :, h:h + 1]
        pb.on("vector", lambda e, hp=hp, gv=gv: e.tensor_tensor(
            T["PCG"].ap().rearrange("p (q j b) -> p q j b", q=2, b=64)[:, hp],
            T["PCall"].ap().rearrange("p (j b) -> p j b", b=64),
            gv.to_broadcast([128, 16, 64]), ALU.mult),
            wait=[pcg_free[h]])
        pcall_free[h + 1] = pb.mark("vector")
        mb = mbuf(hp)
        for j in range(NJ):
            pb.on("vector", lambda e, j=j: e.max(
                out=T["T8"].ap(), in_=T["IMPall"].ap()[:, j * 64:(j + 1) * 64]),
                swait=True)
            pb.on("vector", lambda e, j=j, mb=mb: e.tensor_scalar(
                mb[:, j * 64:(j + 1) * 64],
                T["IMPall"].ap()[:, j * 64:(j + 1) * 64],
                T["T8"].ap()[:, 7:8], NEG, ALU.is_lt, ALU.mult), swait=True)

    atu("KCT", [128, 128], BF16)
    atu("MCOL", [128, 32], F32)

    class JS:
        pass

    jobs = []
    for h in range(H):
        for j in range(NJ):
            js = JS()
            js.h, js.j, js.f = h, j, h * NJ + j
            js.KE = 2 * j + 2
            js.nch = ceil_div(js.KE, 4)
            jobs.append(js)
    NJOBS = len(jobs)

    def emit_scores_evac(js):
        h, j, jp = js.h, js.j, js.f % 2
        hp = h % 2
        kb = kbuf(hp)
        S0 = s0buf(jp)
        mb = mbuf(hp)
        for c in range(js.nch):
            nt = min(4, js.KE - 4 * c)
            cw = 128 * nt
            nb_c = cw // 64
            bi = gcnt[0] % 4
            gcnt[0] += 1
            bank, bn = ROT[bi], ROTN[bi]
            pb.on("tensor", lambda e, h=h, j=j, c=c, cw=cw, bank=bank, kb=kb:
                  e.matmul(
                      bank[:, 0:cw], qtile(h, j),
                      kb[:, 128 + c * 512: 128 + c * 512 + cw],
                      start=True, stop=True),
                  wait=[free[bn], kv_load_q[h]], selfserial=False, inc="sp")
            spm = pb.mark("tensor")
            w = [spm]
            if c == 0:
                w.append(s0_free[jp])
                if js.f == 0:
                    w.append(dtl)
            pb.on("vector", lambda e, c=c, cw=cw, nb_c=nb_c, bank=bank, S0=S0,
                  mb=mb, j=j: e.scalar_tensor_tensor(
                      S0[:, c * 512: c * 512 + cw].rearrange(
                          "p (a b) -> p a b", b=64),
                      bank[:, 0:cw].rearrange("p (a b) -> p a b", b=64),
                      SCALE,
                      mb[:, j * 64 + 8 * c: j * 64 + 8 * c + nb_c].rearrange(
                          "p (a o) -> p a o", o=1).to_broadcast([128, nb_c, 64]),
                      ALU.mult, ALU.add),
                  wait=w)
            free[bn] = pb.mark("vector")
            pb.on("vector", lambda e, c=c, cw=cw, S0=S0, jp=jp: e.reduce_max(
                T["MCH"].ap()[:, jp * 16 + c: jp * 16 + c + 1],
                S0[:, c * 512: c * 512 + cw], axis=AX.X), swait=True)
        kew = js.KE * 128
        pb.on("vector", lambda e, kew=kew, S0=S0: e.tensor_add(
            S0[:, kew - 256: kew], S0[:, kew - 256: kew], T["TAIL"].ap()),
            swait=True)
        bi = gcnt[0] % 4
        gcnt[0] += 1
        bank, bn = ROT[bi], ROTN[bi]
        pb.on("tensor", lambda e, h=h, j=j, bank=bank, kb=kb: e.matmul(
            bank[:, 0:384], qtile(h, j), kb[:, (2 * j) * 128: (2 * j) * 128 + 384],
            start=True, stop=True),
            wait=[free[bn], kv_load_q[h]], selfserial=False, inc="sp")
        spw = pb.mark("tensor")
        if j == NJ - 1:
            kbuf_free[h + 2] = spw
        jsel = 0 if j == 0 else 1
        w = [spw]
        if js.f == 0:
            w.append(dsw)
        pb.on("vector", lambda e, bank=bank, S0=S0, jsel=jsel:
              e.scalar_tensor_tensor(
                  S0[:, 4096:4480], bank[:, 0:384], SCALE,
                  T["SWAB"].ap()[:, jsel * 384:(jsel + 1) * 384],
                  ALU.mult, ALU.add),
              wait=w)
        free[bn] = pb.mark("vector")
        pb.on("vector", lambda e, S0=S0, jp=jp: e.reduce_max(
            T["MCH"].ap()[:, jp * 16 + 8: jp * 16 + 9],
            S0[:, 4096:4480], axis=AX.X), swait=True)
        pb.on("vector", lambda e, jp=jp, nch=js.nch: e.reduce_max(
            T["MT"].ap()[:, jp * 2: jp * 2 + 1],
            T["MCH"].ap()[:, jp * 16: jp * 16 + nch],
            axis=AX.X, negate=True), wait=[mt_free[jp]], swait=True)
        pb.on("vector", lambda e, jp=jp: e.reduce_max(
            T["MT"].ap()[:, jp * 2 + 1: jp * 2 + 2],
            T["MCH"].ap()[:, jp * 16 + 8: jp * 16 + 9],
            axis=AX.X, negate=True), swait=True)
        js.m_done = pb.mark("vector")

    def emit_exp(js):
        jp = js.f % 2
        S0 = s0buf(jp)
        PM = pmbuf(jp)
        for c in range(js.nch):
            cw = 128 * min(4, js.KE - 4 * c)
            w = [js.m_done] if c == 0 else []
            if c == 0:
                w += [pm_free[jp], ls_free[jp]]
            pb.on("scalar", lambda e, c=c, cw=cw, jp=jp, S0=S0, PM=PM: e.activation(
                PM[:, c * 512: c * 512 + cw], S0[:, c * 512: c * 512 + cw],
                AF.Exp, bias=T["MT"].ap()[:, jp * 2: jp * 2 + 1],
                accum_out=T["LS"].ap()[:, jp * 16 + c: jp * 16 + c + 1]),
                wait=w)
        pb.on("scalar", lambda e, jp=jp, S0=S0, PM=PM: e.activation(
            PM[:, 4096:4480], S0[:, 4096:4480],
            AF.Exp, bias=T["MT"].ap()[:, jp * 2 + 1: jp * 2 + 2],
            accum_out=T["LS"].ap()[:, jp * 16 + 8: jp * 16 + 9]),
            swait=True)
        js.exp_done = pb.mark("scalar")
        s0_free[jp] = js.exp_done
        mt_free[jp] = js.exp_done

    def emit_stage2(js):
        h, j, jp = js.h, js.j, js.f % 2
        hp = h % 2
        PM = pmbuf(jp)
        vb = vbuf(hp)
        slot = js.f % 4
        pb.on("vector", lambda e, jp=jp, nch=js.nch: e.reduce_sum(
            T["SC"].ap()[:, jp * 4: jp * 4 + 1],
            T["LS"].ap()[:, jp * 16: jp * 16 + nch], axis=AX.X),
            wait=[js.exp_done])
        pb.on("vector", lambda e, jp=jp: e.tensor_scalar_add(
            T["SC"].ap()[:, jp * 4: jp * 4 + 1],
            T["SC"].ap()[:, jp * 4: jp * 4 + 1], 1.0e-30), swait=True)
        pb.on("vector", lambda e, jp=jp: e.reciprocal(
            T["SC"].ap()[:, jp * 4: jp * 4 + 1],
            T["SC"].ap()[:, jp * 4: jp * 4 + 1]), swait=True)
        pb.on("vector", lambda e, jp=jp, h=h, j=j: e.tensor_mul(
            T["SC"].ap()[:, jp * 4 + 1: jp * 4 + 2],
            T["SC"].ap()[:, jp * 4: jp * 4 + 1],
            T["G"].ap()[:, j * 48 + 16 + h: j * 48 + 17 + h]), swait=True)
        pb.on("vector", lambda e, jp=jp: e.tensor_scalar_add(
            T["SC"].ap()[:, jp * 4 + 2: jp * 4 + 3],
            T["LS"].ap()[:, jp * 16 + 8: jp * 16 + 9], 1.0e-30), swait=True)
        ls_free[jp] = pb.mark("vector")
        pb.on("vector", lambda e, jp=jp: e.reciprocal(
            T["SC"].ap()[:, jp * 4 + 2: jp * 4 + 3],
            T["SC"].ap()[:, jp * 4 + 2: jp * 4 + 3]), swait=True)
        pb.on("vector", lambda e, jp=jp, h=h, j=j: e.tensor_mul(
            T["SC"].ap()[:, jp * 4 + 3: jp * 4 + 4],
            T["SC"].ap()[:, jp * 4 + 2: jp * 4 + 3],
            T["G"].ap()[:, j * 48 + 32 + h: j * 48 + 33 + h]), swait=True)
        kew = js.KE * 128
        pb.on("vector", lambda e, jp=jp, kew=kew, PM=PM: e.tensor_scalar_mul(
            PM[:, 0:kew], PM[:, 0:kew],
            T["SC"].ap()[:, jp * 4 + 1: jp * 4 + 2]), swait=True)
        pb.on("vector", lambda e, jp=jp, PM=PM: e.tensor_scalar_mul(
            PM[:, 4096:4480], PM[:, 4096:4480],
            T["SC"].ap()[:, jp * 4 + 3: jp * 4 + 4]), swait=True)
        vpm = pb.mark("vector")
        pp = tpp[0]
        tpp[0] ^= 1
        pb.on("tensor", lambda e, pp=pp, hp=hp, j=j: e.transpose(
            pbank[pp][0:64, 0:128],
            T["PCG"].ap()[:, hp * 1024 + j * 64: hp * 1024 + (j + 1) * 64],
            T["ID"].ap()),
            wait=[pcg_ready_m[h], free["PB%d" % pp]], selfserial=False, inc="sp")
        tcm = pb.mark("tensor")
        if j == NJ - 1:
            pcg_free[h + 2] = tcm
        pb.on("scalar", lambda e, pp=pp, jp=jp: e.activation(
            T["PCT"].ap()[0:64, jp * 128:(jp + 1) * 128], pbank[pp][0:64, 0:128],
            AF.Copy), wait=[tcm, pct_free[jp]])
        free["PB%d" % pp] = pb.mark("scalar")
        spct = pb.mark("scalar")
        pb.on("tensor", lambda e, hp=hp, jp=jp, slot=slot: e.matmul(
            OA[:, slot * 128:(slot + 1) * 128],
            T["VCM"].ap()[0:64, hp * 128:(hp + 1) * 128],
            T["PCT"].ap()[0:64, jp * 128:(jp + 1) * 128],
            start=True, stop=False),
            wait=[spct, oa_free[slot], vcm_ready[h]], selfserial=False, inc="sp")
        pct_free[jp] = pb.mark("tensor")
        if j == NJ - 1:
            vcm_free[h + 2] = pct_free[jp]
        tiles = [(i * 128, 1 + i) for i in range(js.KE)]
        tiles += [(4096 + t * 128, 2 * j + t) for t in range(3)]
        ntl = len(tiles)
        last_pv = None
        for w0 in range(0, ntl, 4):
            grp = tiles[w0:w0 + 4]
            gw = 128 * len(grp)
            pp = tpp[0]
            tpp[0] ^= 1
            for i, (off, kidx) in enumerate(grp):
                pb.on("tensor", lambda e, pp=pp, i=i, off=off, PM=PM: e.transpose(
                    pbank[pp][:, i * 128:(i + 1) * 128],
                    PM[:, off: off + 128], T["ID"].ap()),
                    wait=([vpm, free["PB%d" % pp]] if i == 0 else ()),
                    selfserial=False, inc=("sp" if i == len(grp) - 1 else None))
            tpm = pb.mark("tensor")
            pb.on("scalar", lambda e, pp=pp, gw=gw: e.activation(
                T["PT"].ap()[:, pp * 512: pp * 512 + gw], pbank[pp][:, 0:gw],
                AF.Copy), wait=[tpm, pt_free[pp]])
            free["PB%d" % pp] = pb.mark("scalar")
            ptm = pb.mark("scalar")
            for i, (off, kidx) in enumerate(grp):
                is_last = (w0 + 4 >= ntl) and (i == len(grp) - 1)
                pb.on("tensor", lambda e, pp=pp, i=i, kidx=kidx, vb=vb,
                      slot=slot, is_last=is_last: e.matmul(
                          OA[:, slot * 128:(slot + 1) * 128],
                          vb[:, kidx * 128:(kidx + 1) * 128],
                          T["PT"].ap()[:, pp * 512 + i * 128: pp * 512 + (i + 1) * 128],
                          start=False, stop=is_last),
                      wait=([ptm] if i == 0 else ()),
                      selfserial=False,
                      inc=("sp" if i == len(grp) - 1 else None))
            pt_free[pp] = pb.mark("tensor")
        pm_free[jp] = pb.mark("tensor")
        js.pv_done = pb.mark("tensor")
        if j == NJ - 1:
            vbuf_free[h + 2] = js.pv_done
        pb.on("vector", lambda e, h=h, j=j, slot=slot: e.tensor_copy(
            T["QT"].ap()[:, h * TOWN + j * 128: h * TOWN + (j + 1) * 128],
            OA[:, slot * 128:(slot + 1) * 128]), wait=[js.pv_done])
        oa_free[slot] = pb.mark("vector")
        if j == NJ - 1 and h + 2 < H:
            emit_kv_load(h + 2)

    pbanks_all = {"P0": P0, "P1": P1, "P2": P2, "P3": P3}
    pcg_ready_m = [None] * (H + 1)

    emit_kv_load(0)
    emit_prep_a(0)
    emit_prep_b(0)
    emit_prep_c(0)
    pcg_ready_m[0] = pb.mark("vector")
    emit_kv_load(1)

    for f in range(NJOBS + 2):
        if f >= 2:
            emit_stage2(jobs[f - 2])
        if 1 <= f <= NJOBS:
            emit_exp(jobs[f - 1])
        if f < NJOBS:
            emit_scores_evac(jobs[f])
            js = jobs[f]
            if js.h + 1 < H:
                if js.j == 9:
                    emit_prep_a(js.h + 1)
                elif js.j == 11:
                    emit_prep_b(js.h + 1)
                elif js.j == 13:
                    emit_prep_c(js.h + 1)
                    pcg_ready_m[js.h + 1] = pb.mark("vector")
    pb.barrier()

    for ch in range(8):
        wb = "WB0" if ch % 2 == 0 else "WB1"
        dw = pb.dma("gpsimd", lambda e, ch=ch, wb=wb: e.dma_start(
            T[wb].ap().rearrange("p (c m) -> p c m", m=256),
            inp["wo"][:, ch * 256:(ch + 1) * 256].rearrange(
                "(c p) m -> p c m", p=128)),
            wait=[free[wb]])
        for tt in range(NJ):
            xp = tt % 2
            dx = pb.dma("sync", lambda e, tt=tt, ch=ch, xp=xp: e.dma_start(
                T["XCH"].ap()[:, xp * 256:(xp + 1) * 256],
                inp["x_own"][tt * 128:(tt + 1) * 128, ch * 256:(ch + 1) * 256]),
                wait=[free["XCH%d" % xp] if ("XCH%d" % xp) in free else None])
            bn = "P0" if tt % 2 == 0 else "P1"
            bank = pbanks_f[bn]
            for hh in range(16):
                pb.on("tensor", lambda e, bank=bank, hh=hh, tt=tt, wb=wb: e.matmul(
                    bank[:, 0:256],
                    T["QT"].ap()[:, hh * TOWN + tt * 128: hh * TOWN + (tt + 1) * 128],
                    T[wb].ap()[:, hh * 256:(hh + 1) * 256],
                    start=(hh == 0), stop=(hh == 15)),
                    wait=([dw, free[bn]] if hh == 0 else ()),
                    selfserial=False, inc=("sp" if hh == 15 else None))
            spa = pb.mark("tensor")
            pb.on("vector", lambda e, bank=bank, xp=xp: e.tensor_add(
                T["XCH"].ap()[:, xp * 256:(xp + 1) * 256],
                T["XCH"].ap()[:, xp * 256:(xp + 1) * 256], bank[:, 0:256]),
                wait=[spa, dx])
            free[bn] = pb.mark("vector")
            vadd = pb.mark("vector")
            free["XCH%d" % xp] = pb.dma("scalar", lambda e, tt=tt, ch=ch, xp=xp:
                                        e.dma_start(
                xmid_d[tt * 128:(tt + 1) * 128, ch * 256:(ch + 1) * 256],
                T["XCH"].ap()[:, xp * 256:(xp + 1) * 256]), wait=[vadd])
        free[wb] = pb.mark("tensor")
    free["XT0"] = None
    free["XT1"] = None
    free["XT2"] = None
    free["LNB"] = None
    free["SMsq"] = None
    free["SMvar"] = None
    free["SMrs"] = None
    qa_all = ("qa", pb.now("qa"))
    vln_d = None
    for tt in range(NJ):
        xb = xrot[tt % 3]
        dq = pb.dma("sync", lambda e, tt=tt, xb=xb: e.dma_start(
            T[xb].ap(), xmid_d[tt * 128:(tt + 1) * 128, :]),
            wait=[free[xb], qa_all if tt == 0 else None])
        vln_d = emit_ln(xb, (tt % 8) * 128, [dq])
        free[xb] = vln_d
        if tt == 7:
            for c in range(16):
                pb.dma("sync", lambda e, c=c: e.dma_start(
                    ln2_d[c], T["LT"].ap()[:, c * 1024:(c + 1) * 1024]),
                    wait=[vln_d])
            pb.pending["vector"].append(("qs", pb.now("qs")))

    doh = pb.dma("sync", lambda e: e.dma_start(T["OH"].ap(), oh_h.ap()),
                 wait=[("sp", pb.now("sp")), ("sv", pb.now("sv"))])
    ent_down_sp = None
    for it_e, hf in enumerate((1, 0)):
        if it_e == 0:
            dln = vln_d
        else:
            dln = pb.dma("sync", lambda e: e.dma_start(
                T["LT"].ap().rearrange("p (c w) -> p c w", w=1024),
                ln2_d.rearrange("c p t -> p c t")),
                wait=[("sp", pb.now("sp"))])
        dxm = pb.dma("sync", lambda e, hf=hf: e.dma_start(
            T["XME"].ap().rearrange("p (t w) -> p t w", w=2048),
            xmid_d[hf * 1024:(hf + 1) * 1024, :].rearrange("(t p) w -> p t w", p=128)),
            wait=[("sv", pb.now("sv")) if it_e == 1 else None])

        if DEBUG_STAGE:
            pb.dma("sync", lambda e, it_e=it_e: e.dma_start(
                lt_snap[it_e], T["LT"].ap()), wait=[dln])
            snap_m = pb.dma("sync", lambda e, it_e=it_e: e.dma_start(
                xme_snap[it_e], T["XME"].ap()), wait=[dxm])
            pb.pending["tensor"].append(snap_m)

        def xm_slice(tt, ch):
            return T["XME"].ap()[:, tt * 2048 + ch * 512: tt * 2048 + (ch + 1) * 512]

        for tt in range(8):
            for c in range(16):
                pb.on("tensor", lambda e, tt=tt, c=c: e.matmul(
                    P4[:, 0:8],
                    T["LT"].ap()[:, c * 1024 + tt * 128: c * 1024 + tt * 128 + 128],
                    T["WR"].ap()[:, c * 8:(c + 1) * 8], start=(c == 0), stop=(c == 15)),
                    wait=([dln, free["P4"]] if c == 0 else ()),
                    selfserial=False, inc=("sp" if c == 15 else None))
            spr = pb.mark("tensor")
            pb.on("vector", lambda e: e.reduce_max(
                T["SM"].ap()[:, 30:31], P4[:, 0:8], axis=AX.X, negate=True),
                wait=[spr])
            vmr = pb.mark("vector")
            sse = pb.on("scalar", lambda e: e.activation(
                T["E8"].ap()[:, 0:8], P4[:, 0:8], AF.Exp, bias=T["SM"].ap()[:, 30:31]),
                wait=[vmr])
            free["P4"] = sse
            pb.on("vector", lambda e: e.max(out=T["T8"].ap(), in_=T["E8"].ap()[:, 0:8]),
                  wait=[sse])
            pb.on("vector", lambda e: e.tensor_scalar(
                T["E8"].ap()[:, 8:16], T["E8"].ap()[:, 0:8], T["T8"].ap()[:, 1:2],
                None, ALU.is_ge), swait=True)
            pb.on("vector", lambda e: e.tensor_mul(
                T["E8"].ap()[:, 16:24], T["E8"].ap()[:, 0:8], T["E8"].ap()[:, 8:16]))
            pb.on("vector", lambda e: e.reduce_sum(
                T["SM"].ap()[:, 32:33], T["E8"].ap()[:, 16:24], axis=AX.X))
            pb.on("vector", lambda e: e.reciprocal(
                T["SM"].ap()[:, 33:34], T["SM"].ap()[:, 32:33]))
            pb.on("vector", lambda e, tt=tt: e.tensor_scalar_mul(
                T["CMB"].ap()[:, tt * 16: tt * 16 + 8], T["E8"].ap()[:, 16:24],
                T["SM"].ap()[:, 33:34]), swait=True)
            pb.on("vector", lambda e, tt=tt: e.memset(
                T["CMB"].ap()[:, tt * 16 + 9:(tt + 1) * 16], 0.0))
        for tt in range(8):
            for c in range(16):
                pb.on("tensor", lambda e, tt=tt, c=c: e.matmul(
                    P4[:, 0:1],
                    T["LT"].ap()[:, c * 1024 + tt * 128: c * 1024 + tt * 128 + 128],
                    T["GW"].ap()[:, c:c + 1], start=(c == 0), stop=(c == 15)),
                    wait=([free["P4"]] if c == 0 else ()),
                    selfserial=False, inc=("sp" if c == 15 else None))
            spg = pb.mark("tensor")
            free["P4"] = pb.on("scalar", lambda e, tt=tt: e.activation(
                T["CMB"].ap()[:, tt * 16 + 8: tt * 16 + 9], P4[:, 0:1], AF.Sigmoid),
                wait=[spg])
        for tt in range(8):
            vcm = pb.mark("vector")
            scb = pb.on("scalar", lambda e, tt=tt: e.activation(
                T["CMBB"].ap(), T["CMB"].ap()[:, tt * 16:(tt + 1) * 16], AF.Copy),
                wait=[vcm])
            pb.on("tensor", lambda e: e.transpose(
                PB0[0:16, 0:128], T["CMBB"].ap(), T["ID"].ap()),
                wait=[scb, free["PB0"], doh], selfserial=False, inc="sp")
            spt = pb.mark("tensor")
            pb.on("vector", lambda e, tt=tt: e.tensor_copy(
                T["CMBT"].ap()[0:16, tt * 128:(tt + 1) * 128], PB0[0:16, 0:128]),
                wait=[spt])
            free["PB0"] = pb.mark("vector")
        vcmbt = pb.mark("vector")
        for q2 in range(2):
            pb.on("tensor", lambda e, q2=q2: e.matmul(
                P4[:], T["OH"].ap()[:, 8 * 128:9 * 128],
                T["CMBT"].ap()[0:16, q2 * 512:(q2 + 1) * 512],
                start=True, stop=True),
                wait=[vcmbt, free["P4"], doh], selfserial=False, inc="sp")
            spr = pb.mark("tensor")
            free["P4"] = pb.on("scalar", lambda e, q2=q2: e.activation(
                T["GRP"].ap()[:, q2 * 512:(q2 + 1) * 512], P4[:], AF.Copy),
                wait=[spr])
        grp_done = pb.mark("scalar")
        if DEBUG_STAGE and it_e == 0:
            pb.dma("sync", lambda e: e.dma_start(cmb_snap, T["CMB"].ap()),
                   wait=[vcmbt])
            pb.dma("sync", lambda e: e.dma_start(cmbt_snap, T["CMBT"].ap()),
                   wait=[vcmbt])
            sg_m = pb.dma("sync", lambda e: e.dma_start(grp_snap, T["GRP"].ap()),
                          wait=[grp_done])
            pb.pending["tensor"].append(sg_m)

        for ent in range(12):
            is_shared = ent < 4
            eidx = ent - 4
            if is_shared:
                rep_t, rep_done = "GRP", grp_done
            else:
                for q2 in range(2):
                    pb.on("tensor", lambda e, eidx=eidx, q2=q2: e.matmul(
                        P4[:], T["OH"].ap()[:, eidx * 128:(eidx + 1) * 128],
                        T["CMBT"].ap()[0:16, q2 * 512:(q2 + 1) * 512],
                        start=True, stop=True),
                        wait=[vcmbt, free["REP"], free["P4"]],
                        selfserial=False, inc="sp")
                    spr = pb.mark("tensor")
                    free["P4"] = pb.on("scalar", lambda e, q2=q2: e.activation(
                        T["REP"].ap()[:, q2 * 512:(q2 + 1) * 512], P4[:], AF.Copy),
                        wait=[spr])
                rep_t, rep_done = "REP", pb.mark("scalar")
            for sl in range(4):
                wbg = "WB0" if sl % 2 == 0 else "WB1"
                wbu = "KT0" if sl % 2 == 0 else "KT1"
                if is_shared:
                    gsrc = inp["sgT"][:, ent * 1024 + sl * 256: ent * 1024 + (sl + 1) * 256]
                    usrc = inp["suT"][:, ent * 1024 + sl * 256: ent * 1024 + (sl + 1) * 256]
                else:
                    gsrc = inp["wgT"][eidx, :, sl * 256:(sl + 1) * 256]
                    usrc = inp["wuT"][eidx, :, sl * 256:(sl + 1) * 256]
                dg = pb.dma("gpsimd", lambda e, gsrc=gsrc, wbg=wbg: e.dma_start(
                    T[wbg].ap().rearrange("p (c m) -> p c m", m=256),
                    gsrc.rearrange("(c p) m -> p c m", p=128)),
                    wait=[free[wbg]])
                du = pb.dma("gpsimd", lambda e, usrc=usrc, wbu=wbu: e.dma_start(
                    T[wbu].ap()[:, 0:4096].rearrange("p (c m) -> p c m", m=256),
                    usrc.rearrange("(c p) m -> p c m", p=128)),
                    wait=[free[wbu]])
                for it2 in range(2):
                    it = sl * 2 + it2
                    for tch in range(2):
                        for c in range(16):
                            pb.on("tensor", lambda e, c=c, it2=it2, tch=tch, wbg=wbg:
                                  e.matmul(
                                      P0[:],
                                      T[wbg].ap()[:, c * 256 + it2 * 128: c * 256 + (it2 + 1) * 128],
                                      T["LT"].ap()[:, c * 1024 + tch * 512: c * 1024 + (tch + 1) * 512],
                                      start=(c == 0), stop=(c == 15)),
                                  wait=([dg, free["P0"], dln] if c == 0 else ()),
                                  selfserial=False, inc=("sp" if c == 15 else None))
                        spg_ = pb.mark("tensor")
                        sgs = pb.on("scalar", lambda e: e.activation(
                            T["GTS"].ap(), P0[:], AF.Silu), wait=[spg_, free["GTS"]])
                        free["P0"] = sgs
                        for c in range(16):
                            pb.on("tensor", lambda e, c=c, it2=it2, tch=tch, wbu=wbu:
                                  e.matmul(
                                      P1[:],
                                      T[wbu].ap()[:, c * 256 + it2 * 128: c * 256 + (it2 + 1) * 128],
                                      T["LT"].ap()[:, c * 1024 + tch * 512: c * 1024 + (tch + 1) * 512],
                                      start=(c == 0), stop=(c == 15)),
                                  wait=([du, free["P1"]] if c == 0 else ()),
                                  selfserial=False, inc=("sp" if c == 15 else None))
                        spu = pb.mark("tensor")
                        pb.on("vector", lambda e: e.tensor_mul(
                            T["HTA"].ap(), P1[:], T["GTS"].ap()),
                            wait=[spu, sgs, free["HTA"]])
                        free["P1"] = pb.mark("vector")
                        free["GTS"] = pb.mark("vector")
                        pb.on("vector", lambda e, it=it, tch=tch, rep_t=rep_t:
                              e.tensor_mul(
                                  T["HT"].ap()[:, it * 1024 + tch * 512: it * 1024 + (tch + 1) * 512],
                                  T["HTA"].ap(), T[rep_t].ap()[:, tch * 512:(tch + 1) * 512]),
                              wait=[rep_done, ent_down_sp])
                        free["HTA"] = pb.mark("vector")
                free[wbg] = pb.mark("tensor")
                free[wbu] = pb.mark("tensor")
            ht_done = pb.mark("vector")
            free["REP"] = ht_done
            if DEBUG_STAGE and it_e == 0 and ent == 0:
                sh_m = pb.dma("sync", lambda e: e.dma_start(ht_snap, T["HT"].ap()),
                              wait=[ht_done])
                pb.pending["tensor"].append(sh_m)
            for ch in range(4):
                wbd = "WB0" if ch % 2 == 0 else "WB1"
                if is_shared:
                    dsrc = inp["sdT"][ent * 1024:(ent + 1) * 1024, ch * 512:(ch + 1) * 512]
                else:
                    dsrc = inp["wdT"][eidx, :, ch * 512:(ch + 1) * 512]
                dd = pb.dma("gpsimd", lambda e, dsrc=dsrc, wbd=wbd: e.dma_start(
                    T[wbd].ap().rearrange("p (c m) -> p c m", m=512),
                    dsrc.rearrange("(c p) m -> p c m", p=128)),
                    wait=[free[wbd]])
                for tt in range(8):
                    bn = "P2" if (tt + ch) % 2 == 0 else "P3"
                    bank = P2 if (tt + ch) % 2 == 0 else P3
                    for it in range(8):
                        pb.on("tensor", lambda e, bank=bank, it=it, tt=tt, wbd=wbd:
                              e.matmul(
                                  bank[:],
                                  T["HT"].ap()[:, it * 1024 + tt * 128: it * 1024 + tt * 128 + 128],
                                  T[wbd].ap()[:, it * 512:(it + 1) * 512],
                                  start=(it == 0), stop=(it == 7)),
                              wait=([dd, ht_done, free[bn], dxm] if it == 0 else ()),
                              selfserial=False, inc=("sp" if it == 7 else None))
                    spd = pb.mark("tensor")
                    pb.on("vector", lambda e, tt=tt, ch=ch, bank=bank: e.tensor_add(
                        xm_slice(tt, ch), xm_slice(tt, ch), bank[:]),
                        wait=[spd])
                    free[bn] = pb.mark("vector")
                free[wbd] = pb.mark("tensor")
            ent_down_sp = pb.mark("tensor")
            if DEBUG_STAGE and it_e == 0 and ent in (0, 4):
                snp = xme1_snap if ent == 0 else xme2_snap
                sx_m = pb.dma("sync", lambda e, snp=snp: e.dma_start(
                    snp, T["XME"].ap()), wait=[("sv", pb.now("sv"))])
                pb.pending["tensor"].append(sx_m)
        vfin = pb.mark("vector")
        for tt in range(8):
            pb.dma("scalar", lambda e, tt=tt, hf=hf: e.dma_start(
                y[hf * 1024 + tt * 128: hf * 1024 + (tt + 1) * 128, :],
                T["XME"].ap()[:, tt * 2048:(tt + 1) * 2048]), wait=[vfin])

    pb.on("sync", lambda e: None,
          wait=[("qs", pb.now("qs")), ("qg", pb.now("qg")), ("qa", pb.now("qa"))],
          selfserial=False)

    sems = {}
    for name in set(list(pb.cnt.keys()) + ["sv", "ss", "sp", "sg", "sy",
                                           "qs", "qg", "qa"]):
        sems[name] = ctx.enter_context(nc.semaphore(name))

    with nc.Block() as blk:
        def make_runner(ename):
            def runner(e):
                for fn, waits, inc, n in pb.ops[ename]:
                    for s, v in waits:
                        e.wait_ge(sems[s], v)
                    ins = fn(e)
                    if inc is not None:
                        if ins is None:
                            raise RuntimeError(f"{ename}: inc with no instruction")
                        ins.then_inc(sems[inc], n)
            return runner

        blk.tensor(make_runner("tensor"))
        blk.vector(make_runner("vector"))
        blk.scalar(make_runner("scalar"))
        blk.gpsimd(make_runner("gpsimd"))
        blk.sync(make_runner("sync"))

    ctx.close()
    return nc



def _host_masks(p):
    cmp_bias = np.zeros((NJ, 128, NB), np.float32)
    cmp_add2 = np.zeros((NJ, 128, NB), np.float32)
    anym = np.zeros((NJ, 128, 1), np.float32)
    for j in range(NJ):
        a = 2 * j + p
        i = 128 * a + np.arange(128)
        b = np.arange(NB)
        mask = i[:, None] >= (64 * (b[None, :] + 1) - 1)
        cmp_bias[j] = np.where(mask, 0.0, NEG)
        anym[j, :, 0] = mask.any(axis=1).astype(np.float32)
        cur = i // 64
        cmp_add2[j, np.arange(128), cur] = 2.0
    tail01 = np.zeros((128, 256), np.float32)
    r = np.arange(128)
    c = np.arange(128)
    tri = (c[None, :] <= r[:, None]).astype(np.float32)
    if p == 0:
        tail01[:, 0:128] = tri
    else:
        tail01[:, 0:128] = 1.0
        tail01[:, 128:256] = tri
    taila = (tail01 - 1.0) * 1.0e30
    swab = np.full((2, 128, 384), NEG, np.float32)
    cc = np.arange(384)
    delta = 128 * (p + 1) + r[:, None] - cc[None, :]
    band = (delta >= 0) & (delta <= WIN)
    swab[1] = np.where(band, 0.0, NEG)
    band0 = band & (cc[None, :] >= 128)
    swab[0] = np.where(band0, 0.0, NEG)
    if p == 1:
        swab[0] = swab[1]
    return cmp_bias, cmp_add2, anym, taila, swab


def _rope_tables(pos):
    half = HD // 2
    inv = 1.0 / (10000.0 ** (np.arange(half, dtype=np.float64) / half))
    th = pos[:, None].astype(np.float64) * inv[None, :]
    cos = np.cos(th).astype(np.float32).T
    sin = np.sin(th).astype(np.float32).T
    return np.ascontiguousarray(np.concatenate([cos, sin], axis=0))


_NC = None


def kernel(L0_attn_x, L0_attn_Wq, L0_attn_Wk, L0_attn_Wv, L0_attn_Wg, L0_attn_Wo,
           L0_ffn_w_router, L0_ffn_wg, L0_ffn_wu, L0_ffn_wd,
           L0_ffn_sw_gate, L0_ffn_sw_up, L0_ffn_sw_down, L0_ffn_gate_w):
    global _NC
    x = np.ascontiguousarray(np.asarray(L0_attn_x, dtype=np.float32))
    wq = np.ascontiguousarray(np.asarray(L0_attn_Wq, np.float32))
    wk = np.ascontiguousarray(np.asarray(L0_attn_Wk, np.float32))
    wv = np.ascontiguousarray(np.asarray(L0_attn_Wv, np.float32))
    wg48 = np.ascontiguousarray(np.asarray(L0_attn_Wg, np.float32))
    wo = np.ascontiguousarray(np.asarray(L0_attn_Wo, np.float32))
    wrT = np.ascontiguousarray(np.asarray(L0_ffn_w_router, np.float32).T)
    wgT = np.ascontiguousarray(np.transpose(np.asarray(L0_ffn_wg, np.float32), (0, 2, 1)))
    wuT = np.ascontiguousarray(np.transpose(np.asarray(L0_ffn_wu, np.float32), (0, 2, 1)))
    wdT = np.ascontiguousarray(np.transpose(np.asarray(L0_ffn_wd, np.float32), (0, 2, 1)))
    sgT = np.ascontiguousarray(np.asarray(L0_ffn_sw_gate, np.float32).T)
    suT = np.ascontiguousarray(np.asarray(L0_ffn_sw_up, np.float32).T)
    sdT = np.ascontiguousarray(np.asarray(L0_ffn_sw_down, np.float32).T)
    gwT = np.ascontiguousarray(np.asarray(L0_ffn_gate_w, np.float32).T)

    cs_seq = _rope_tables(np.arange(N))
    in_maps = []
    for cix in range(8):
        s, p = cix // 2, cix % 2
        xs = np.ascontiguousarray(x[s])
        xo = np.ascontiguousarray(xs.reshape(32, 128, D)[p::2].reshape(TOWN, D))
        pos_own = (np.arange(TOWN) // 128 * 2 + p) * 128 + np.arange(TOWN) % 128
        cmp_bias, cmp_add2, anym, tail01, swab = _host_masks(p)
        in_maps.append({
            "x_own": xo, "x_seq": xs,
            "wq": wq, "wk": wk, "wv": wv, "wg48": wg48, "wo": wo,
            "wrT": wrT, "gwT": gwT, "wgT": wgT, "wuT": wuT, "wdT": wdT,
            "sgT": sgT, "suT": suT, "sdT": sdT,
            "cs_own": _rope_tables(pos_own), "cs_seq": cs_seq,
            "cmp_bias": cmp_bias, "cmp_add2": cmp_add2, "anym": anym,
            "tail01": tail01, "swab": swab,
        })

    if _NC is None:
        _NC = build_nc()
    res = run_bass_kernel_spmd(_NC, in_maps, core_ids=list(range(8)))
    out = np.empty((4, N, D), np.float32)
    for cix in range(8):
        s, p = cix // 2, cix % 2
        out[s].reshape(32, 128, D)[p::2] = res.results[cix]["y"].reshape(NJ, 128, D)
    return out
